# version 03
# speedup vs baseline: 1.0908x; 1.0081x over previous
import numpy as np
import ml_dtypes
from contextlib import ExitStack

import concourse.bass as bass
import concourse.mybir as mybir
from concourse.bass_utils import run_bass_kernel_spmd

F32 = mybir.dt.float32
BF16 = mybir.dt.bfloat16
I32 = mybir.dt.int32
AF = mybir.ActivationFunctionType
ALU = mybir.AluOpType
AXX = mybir.AxisListType.X

NCORES = 8
L = 4096
DM = 1024
DI = 2048
NST = 16
TA = 256
NTA = L // TA
TB = 128
NTB = L // TB
NH = 4
DK = 256
FFI = 3072

ENGS = ("t", "v", "a", "p", "s")


def sap(t, off_extra, dims):
    a = t[:]
    return bass.AP(a.tensor, a.offset + off_extra, [a.ap[0]] + dims)


def fmaj(dram_slice, nch, T):
    a = dram_slice
    return bass.AP(a.tensor, a.offset, [[T, 128], [128 * T, nch], [1, T]])


class Emit:
    def __init__(self, nc, es):
        self.nc = nc
        self.q = {e: [] for e in ENGS}
        self.cnt = {}
        self.hands = {}
        self.hwm = {e: {} for e in ENGS}
        self.es = es
        self.ops = []
        self.eng_pos = {}
        self.sem_incs = {}
        self.sem_owner = {}
        self.check = True

    def sem(self, name):
        if name not in self.hands:
            self.hands[name] = self.es.enter_context(self.nc.semaphore(name))
            self.cnt.setdefault(name, 0)
        return self.hands[name]

    def ev(self, name):
        self.sem(name)
        return (name, self.cnt[name])

    def _rec(self, eng, waits, incs, rd, wr, tag):
        i = self.eng_pos.get(eng, 0)
        self.eng_pos[eng] = i + 1
        oid = len(self.ops)
        self.ops.append(dict(id=oid, eng=eng, idx=i, waits=list(waits),
                             rd=list(rd or ()), wr=list(wr or ()), tag=tag))
        for (s, n) in incs:
            ow = self.sem_owner.setdefault(s, eng)
            assert ow == eng, f"sem {s} inc'd from {eng} and {ow}"
            self.sem_incs.setdefault(s, []).append((self.cnt[s], oid))
        return oid

    def op(self, eng, fn, wait=(), inc=None, dinc=None, rd=None, wr=None, tag=""):
        if inc is None:
            inc = {"v": ["vs"], "a": ["as"], "t": [], "p": [], "s": []}[eng]
        allw = [w for w in wait if w is not None and w[1] > 0]
        waits = []
        for (s, v) in allw:
            h = self.hwm[eng].setdefault(s, 0)
            if v > h:
                waits.append((self.sem(s), v))
                self.hwm[eng][s] = v
        incs = []
        evt = None
        recinc = []
        for s in inc:
            self.cnt[s] = self.cnt.get(s, 0) + 1
            incs.append((self.sem(s), 1))
            recinc.append((s, 1))
            evt = (s, self.cnt[s])
        if dinc is not None:
            self.cnt[dinc] = self.cnt.get(dinc, 0) + 16
            incs.append((self.sem(dinc), 16))
            recinc.append((dinc, 16))
            evt = (dinc, self.cnt[dinc])
        self._rec(eng, allw, recinc, rd, wr, tag)

        def run(e, waits=waits, incs=incs, fn=fn):
            for (h, v) in waits:
                e.wait_ge(h, v)
            r = fn(e)
            for (h, n) in incs:
                r = r.then_inc(h, n)
        self.q[eng].append(run)
        return evt

    def barrier(self):
        s = "bar"
        self.sem(s)
        tgt = self.cnt.get(s, 0) + len(ENGS)
        incops = []
        for e in ENGS:
            self.cnt[s] += 1
            h = self.hands[s]
            self.q[e].append(lambda en, h=h: en.nop().then_inc(h, 1))
            incops.append(self._rec(e, [], [], None, None, f"bar-inc-{e}"))
        for e in ENGS:
            h = self.hands[s]
            self.q[e].append(lambda en, h=h, tgt=tgt: en.wait_ge(h, tgt))
            self.hwm[e][s] = tgt
            self._rec(e, [("__bar__", incops)], [], None, None, f"bar-wait-{e}")

    def dma(self, eng, out, in_, wait=(), rd=None, wr=None, tag="", q=None):
        sem = q or {"s": "sd", "p": "pd", "a": "ad", "v": "vd"}[eng]
        allw = [w for w in wait if w is not None and w[1] > 0]
        waits = []
        for (s, v) in allw:
            h = self.hwm[eng].setdefault(s, 0)
            if v > h:
                waits.append((self.sem(s), v))
                self.hwm[eng][s] = v
        self.cnt[sem] = self.cnt.get(sem, 0) + 16
        evt = (sem, self.cnt[sem])
        iss = self._rec(eng, allw, [], None, None, tag + "-iss")
        self.ops[iss]["dma_issue_for"] = sem
        tr = self._rec(sem + "q", [("__op__", iss)], [(sem, 16)], rd, wr, tag)
        self.ops[tr]["issue_op"] = iss
        h = self.sem(sem)

        def run(e, waits=waits, h=h, out=out, in_=in_):
            for (hh, v) in waits:
                e.wait_ge(hh, v)
            e.dma_start(out=out, in_=in_).then_inc(h, 16)
        self.q[eng].append(run)
        return evt

    def verify(self):
        n = len(self.ops)
        engs = sorted({o["eng"] for o in self.ops})
        eidx = {e: k for k, e in enumerate(engs)}
        ne = len(engs)
        clock = [None] * n
        import bisect
        sem_tl = {s: ([c for c, _ in lst], [o for _, o in lst])
                  for s, lst in self.sem_incs.items()}
        per_eng_ops = {}
        for o in self.ops:
            per_eng_ops.setdefault(o["eng"], []).append(o)
        for o in self.ops:
            c = [-1] * ne
            if o["idx"] > 0:
                pc = clock[per_eng_ops[o["eng"]][o["idx"] - 1]["id"]]
                c = [max(a, b) for a, b in zip(c, pc)]
            for w in o["waits"]:
                if w[0] == "__bar__":
                    for oid in w[1]:
                        wc = clock[oid]
                        c = [max(a, b) for a, b in zip(c, wc)]
                elif w[0] == "__op__":
                    wc = clock[w[1]]
                    c = [max(a, b) for a, b in zip(c, wc)]
                else:
                    s, v = w
                    if s not in sem_tl:
                        raise AssertionError(f"wait on never-inc'd sem {s} at {o['tag']}")
                    cums, oids = sem_tl[s]
                    k = bisect.bisect_left(cums, v)
                    if k >= len(cums):
                        raise AssertionError(
                            f"wait ({s},{v}) never satisfied at {o['tag']}")
                    wc = clock[oids[k]]
                    c = [max(a, b) for a, b in zip(c, wc)]
            c[eidx[o["eng"]]] = max(c[eidx[o["eng"]]], o["idx"])
            clock[o["id"]] = c

        def hb(x, y):
            return clock[y["id"]][eidx[x["eng"]]] >= x["idx"]

        qerrs = []
        for o in self.ops:
            for w in o["waits"]:
                if isinstance(w[0], str) and not w[0].startswith("__") \
                        and w[0] in sem_tl and self.sem_owner.get(w[0], "").endswith("q") is False:
                    pass
        for o in self.ops:
            for w in o["waits"]:
                s_, v = (w if not isinstance(w[0], str) or not w[0].startswith("__")
                         else (None, None))
                if s_ is None or s_ not in sem_tl:
                    continue
                cums, oids = sem_tl[s_]
                k = bisect.bisect_left(cums, v)
                if k >= len(cums):
                    continue
                top = self.ops[oids[k]]
                if "issue_op" not in top:
                    continue
                if k + 1 < len(oids):
                    nxt = self.ops[oids[k + 1]]
                    iss = self.ops[nxt["issue_op"]]
                    if not hb(o, iss):
                        qerrs.append(
                            f"UNSOUND DMA WAIT ({s_},{v}) at {o['tag']}: next "
                            f"dma {nxt['tag']} issued concurrently")
        if qerrs:
            seen = []
            for e in qerrs:
                if e not in seen:
                    seen.append(e)
            raise AssertionError("QWAITS:\n" + "\n".join(seen[:60]) +
                                 f"\n({len(seen)} unique)")

        bufs = {}
        errs = []
        for o in self.ops:
            for k in o["rd"]:
                st = bufs.setdefault(k, {"w": None, "r": []})
                if st["w"] is not None and not hb(st["w"], o):
                    errs.append(f"RAW {k}: read {o['tag']} !after write {st['w']['tag']}")
                st["r"].append(o)
            for k in o["wr"]:
                st = bufs.setdefault(k, {"w": None, "r": []})
                if st["w"] is not None and not hb(st["w"], o):
                    errs.append(f"WAW {k}: write {o['tag']} !after write {st['w']['tag']}")
                for r in st["r"]:
                    if r is not o and not hb(r, o):
                        errs.append(f"WAR {k}: write {o['tag']} !after read {r['tag']}")
                st["w"] = o
                st["r"] = [o] if k in o["rd"] else []
        if errs:
            seen = set()
            out = []
            for e in errs:
                if e not in seen:
                    seen.add(e)
                    out.append(e)
            raise AssertionError("RACES:\n" + "\n".join(out[:80]) +
                                 f"\n({len(out)} unique)")


def build(nc, mode="full"):
    es = ExitStack()
    E = Emit(nc, es)
    dram = nc.dram_tensor

    x0 = dram("x0", [NTA, DM, TA], F32, kind="ExternalInput").ap()
    w_in = dram("w_in", [DM, 2 * DI], F32, kind="ExternalInput").ap()
    w_x = dram("w_x", [16, 128, 33], F32, kind="ExternalInput").ap()
    w_out = dram("w_out", [DI, DM], F32, kind="ExternalInput").ap()
    l0cd = dram("l0cd", [64, 128, 128], F32, kind="ExternalInput").ap()
    conv_b = dram("conv_b", [128, 16], F32, kind="ExternalInput").ap()
    dt_w = dram("dt_w", [128, 16], F32, kind="ExternalInput").ap()
    dt_b = dram("dt_b", [128, 16], F32, kind="ExternalInput").ap()
    a_log = dram("a_log", [128, 16 * NST], F32, kind="ExternalInput").ap()
    d_res = dram("d_res", [128, 16], F32, kind="ExternalInput").ap()
    w1n = dram("w1n", [2 * DI, DM], F32, kind="ExternalInput").ap()
    w1t = dram("w1t", [DM, 2 * DI], F32, kind="ExternalInput").ap()
    w2n = dram("w2n", [DM, 2 * DI], F32, kind="ExternalInput").ap()
    w2t = dram("w2t", [2 * DI, DM], F32, kind="ExternalInput").ap()
    b1 = dram("b1", [128, 32], F32, kind="ExternalInput").ap()
    b2 = dram("b2", [128, 8], F32, kind="ExternalInput").ap()
    ones_f = dram("ones_f", [128, 2], F32, kind="ExternalInput").ap()
    ones_r = dram("ones_r", [1, 128], F32, kind="ExternalInput").ap()
    qwT = dram("qwT", [DM, DM], F32, kind="ExternalInput").ap()
    kwT = dram("kwT", [DM, DM], F32, kind="ExternalInput").ap()
    vwT = dram("vwT", [DM, DM], F32, kind="ExternalInput").ap()
    bwT = dram("bwT", [DM, 4], F32, kind="ExternalInput").ap()
    gm1T = dram("gm1T", [DM, 2 * DM], F32, kind="ExternalInput").ap()
    gm2T = dram("gm2T", [2 * DM, 16], F32, kind="ExternalInput").ap()
    gmb1 = dram("gmb1", [128, 16], F32, kind="ExternalInput").ap()
    gmb2 = dram("gmb2", [16, 1], F32, kind="ExternalInput").ap()
    qkvcd = dram("qkvcd", [96, 128, 128], F32, kind="ExternalInput").ap()
    fircd = dram("fircd", [304, 128, 128], F32, kind="ExternalInput").ap()
    oprojT = dram("oprojT", [DM, DM], F32, kind="ExternalInput").ap()
    onw_rep = dram("onw_rep", [128, DM], F32, kind="ExternalInput").ap()
    alpha_rep = dram("alpha_rep", [128, 16], F32, kind="ExternalInput").ap()
    maskstd = dram("maskstd", [128, 512], F32, kind="ExternalInput").ap()
    maskor = dram("maskor", [128, 512], F32, kind="ExternalInput").ap()
    maskal = dram("maskal", [128, 512], F32, kind="ExternalInput").ap()
    idrep4 = dram("idrep4", [128, 512], BF16, kind="ExternalInput").ap()
    id_bf = dram("id_bf", [128, 128], BF16, kind="ExternalInput").ap()
    bsel = dram("bsel", [4, 512], F32, kind="ExternalInput").ap()
    id_f32 = dram("id_f32", [128, 128], F32, kind="ExternalInput").ap()
    wupN = dram("wupN", [FFI, DM], F32, kind="ExternalInput").ap()
    wgateN = dram("wgateN", [FFI, DM], F32, kind="ExternalInput").ap()
    wdownN = dram("wdownN", [DM, FFI], F32, kind="ExternalInput").ap()

    yout = dram("yout", [NTA, DM, TA], F32, kind="ExternalOutput").ap()
    dbg = dram("dbg", [12, 128, 4096], F32, kind="ExternalOutput").ap() \
        if mode == "dbg" else None
    dbgb = dram("dbgb", [2, 128, 4096], BF16, kind="ExternalOutput").ap() \
        if mode == "dbg" else None

    xnbf_s = dram("xnbf_s", [NTA, DM, TA], BF16).ap()
    xpart_s = dram("xpart_s", [NTA, DM, TA], F32).ap()
    bc_b = dram("bc_b", [NTA, 2, NST, TA], F32).ap()
    w1s_s = dram("w1s_s", [DM, 2 * DI], BF16).ap()
    w2s_s = dram("w2s_s", [2 * DI, DM], BF16).ap()
    x1_s = dram("x1_s", [NTA, DM, TA], F32).ap()
    dbg_xn = dram("dbg_xn", [128, 8 * TA], BF16).ap()
    dbg_qf = dram("dbg_qf", [128, 24 * TA], F32).ap()
    qhatF = dram("qhatF", [DM, L], BF16).ap()
    khatF = dram("khatF", [DM, L], BF16).ap()
    kbF = dram("kbF", [DM, L], BF16).ap()
    khT = dram("khT", [NTB, 128, DM], BF16).ap()
    kbT = dram("kbT", [NTB, 128, DM], BF16).ap()
    vbT = dram("vbT", [NTB, 128, DM], BF16).ap()
    vT = dram("vT", [NTB, 128, DM], F32).ap()
    vF = dram("vF", [DM, 31 + L], BF16).ap()
    deltaT = dram("deltaT", [NTB, 128, DM], F32).ap()
    gateT = dram("gateT", [NTB, 128, 16], F32).ap()
    attn1_s = dram("attn1_s", [NTA, DM, TA], F32).ap()
    wupT_dq = dram("wupT_dq", [DM, FFI], BF16).ap()
    wgateT_dq = dram("wgateT_dq", [DM, FFI], BF16).ap()
    wdownT_dq = dram("wdownT_dq", [FFI, DM], BF16).ap()
    up_s = dram("up_s", [NTA, FFI, TA], F32).ap()
    gate_s = dram("gate_s", [NTA, FFI, TA], F32).ap()
    gs_s = dram("gs_s", [NTA, FFI, TA], F32).ap()
    op_s = dram("op_s", [NTA, FFI, TA], F32).ap()
    cc_in = [dram(f"cc{i}_in", [256], F32).ap() for i in range(4)]
    cc_out = [dram(f"cc{i}_out", [256], F32).ap() for i in range(4)]

    def sb(name, shape, dtyp=F32, stk=None):
        return (stk or es).enter_context(nc.sbuf_tensor(name, shape, dtyp))

    P = [es.enter_context(nc.psum_tensor(f"pbk{i}", [128, 512], F32))
         for i in range(8)]

    cb_t = sb("cb_t", [128, 16])
    dtw_t = sb("dtw_t", [128, 16])
    dtb_t = sb("dtb_t", [128, 16])
    acol_t = sb("acol_t", [128, 16 * NST])
    dres_t = sb("dres_t", [128, 16])
    b1_t = sb("b1_t", [128, 32])
    b2_t = sb("b2_t", [128, 8])
    a1_t = sb("a1_t", [128, 32])
    a2_t = sb("a2_t", [128, 8])
    ones_t = sb("ones_t", [128, 2])
    onesr_t = sb("onesr_t", [1, 128])
    eps_t = sb("eps_t", [128, 1])
    id_bf_t = sb("id_bf_t", [128, 128], BF16)
    idf_t = sb("idf_t", [128, 128], F32)
    sc_t = sb("sc_t", [128, 16])
    c64k_t = sb("c64k_t", [128, 1])
    rowc_t = sb("rowc_t", [1, 1024])
    ev_eps = E.op("v", lambda v: v.memset(eps_t[:], 1e-5), wr=["eps"], tag="eps-init")
    E.op("v", lambda v: v.memset(c64k_t[:], 65536.0), wr=["c64k"], tag="c64k-init")

    for (d, sr, nm) in [(cb_t, conv_b, "cb"), (dtw_t, dt_w, "dtw"),
                        (dtb_t, dt_b, "dtb"), (dres_t, d_res, "dres"),
                        (b1_t, b1, "b1"), (b2_t, b2, "b2"),
                        (ones_t, ones_f, "ones"), (onesr_t, ones_r, "onesr")]:
        E.dma("s", d[:], sr, wr=[nm], tag=f"ld-{nm}", q="sdS")
    ev0 = E.dma("s", acol_t[:], a_log, wr=["acol"], tag="ld-acol", q="sdS")
    evS = ev0
    eva = E.op("a", lambda a: a.activation(out=acol_t[:], in_=acol_t[:], func=AF.Exp),
               wait=[ev0], rd=["acol"], wr=["acol"], tag="acol-exp")
    E.op("v", lambda v: v.tensor_scalar(out=acol_t[:], in0=acol_t[:],
                                        scalar1=-1.0, scalar2=None, op0=ALU.mult),
         wait=[eva], rd=["acol"], wr=["acol"], tag="acol-neg")

    p0 = ExitStack()
    p0buf = sb("p0buf", [128, 4096], stk=p0)
    p0bf = sb("p0bf", [128, 4096], BF16, stk=p0)
    p0col = sb("p0col", [128, 1], stk=p0)

    def prep_sign(nat, natT, ncn, natF, a_dst, adk, nct, tF, s_dst, sdk):
        ev3 = None
        for c in range(ncn):
            ev = E.dma("s", p0buf[:, :natF], nat[c * 128:(c + 1) * 128, :],
                       wait=[ev3], wr=["p0buf"], tag=f"p0-ld-{adk}{c}", q="sdP")
            ev2 = E.op("v", lambda v: v.tensor_reduce(
                out=p0col[:], in_=p0buf[:, :natF], axis=AXX, op=ALU.add,
                apply_absolute_value=True), wait=[ev],
                rd=["p0buf"], wr=["p0col"], tag=f"p0-red-{adk}{c}")
            ev3 = E.op("v", lambda v, c=c: v.tensor_scalar(
                out=a_dst[:, c:c + 1], in0=p0col[:], scalar1=1.0 / natF,
                scalar2=None, op0=ALU.mult), wait=[ev2],
                rd=["p0col"], wr=[adk], tag=f"p0-al-{adk}{c}")
        last = None
        for c in range(nct):
            ev = E.dma("s", p0buf[:, :tF], natT[c * 128:(c + 1) * 128, :],
                       wait=[ev3, last], wr=["p0buf"], tag=f"p0-ldT-{sdk}{c}", q="sdP")
            ev2 = E.op("a", lambda a: a.activation(
                out=p0bf[:, :tF], in_=p0buf[:, :tF], func=AF.Sign), wait=[ev],
                rd=["p0buf"], wr=["p0bf"], tag=f"p0-sgn-{sdk}{c}")
            last = E.dma("s", s_dst[c * 128:(c + 1) * 128, :], p0bf[:, :tF],
                         wait=[ev2], rd=["p0bf"], wr=[sdk], tag=f"p0-st-{sdk}{c}", q="sdP")
        return last

    ev_w1 = prep_sign(w1n, w1t, 32, DM, a1_t, "a1", 8, 2 * DI, w1s_s, "d:w1s")
    ev_w2 = prep_sign(w2n, w2t, 8, 2 * DI, a2_t, "a2", 32, DM, w2s_s, "d:w2s")
    ev_w1 = E.ev("sdP")
    ev_w2 = ev_w1
    p0.close()
    E.barrier()

    pa = ExitStack()
    w_in_t = sb("w_in_t", [128, 8 * 4096], BF16, stk=pa)
    w_x_t = sb("w_x_t", [128, 16 * 33], BF16, stk=pa)
    l0cd_t = sb("l0cd_t", [128, 64 * 128], BF16, stk=pa)
    wout_db = sb("wout_db", [128, 2 * DM], BF16, stk=pa)
    x_t = sb("x_t", [128, 8 * TA], stk=pa)
    xnbf_t = sb("xnbf_t", [128, 8 * TA], BF16, stk=pa)
    xm_st = sb("xm_st", [128, 2 * (TA + 3)], BF16, stk=pa)
    xtail_t = sb("xtail_t", [128, 16 * 3], BF16, stk=pa)
    silz_t = sb("silz_t", [128, 16 * TA], stk=pa)
    xc_t = sb("xc_t", [128, 16 * TA], stk=pa)
    xcbf_st = sb("xcbf_st", [128, 2 * TA], BF16, stk=pa)
    ssm_t = sb("ssm_t", [33, TA], stk=pa)
    delta_t = sb("delta_t", [128, 2 * TA], stk=pa)
    dx_t = sb("dx_t", [128, 2 * TA], stk=pa)
    sp_t = sb("sp_t", [128, 4 * TA], stk=pa)
    da_t = sb("da_t", [128, 16 * TA], stk=pa)
    dbx_t = sb("dbx_t", [128, 8 * TA], stk=pa)
    h_t = sb("h_t", [128, 8 * TA], stk=pa)
    brep_t = sb("brep_t", [128, NST * TA], BF16, stk=pa)
    crep_t = sb("crep_t", [128, NST * TA], BF16, stk=pa)
    hst_t = sb("hst_t", [128, 16 * NST], stk=pa)
    y_t = sb("y_t", [128, 2 * TA], stk=pa)
    y2bf_t = sb("y2bf_t", [128, 2 * TA], BF16, stk=pa)
    row_t = sb("row_t", [1, 4 * TA], stk=pa)

    E.op("v", lambda v: v.memset(hst_t[:], 0.0), wr=["hst"], tag="hst-init")
    E.op("v", lambda v: v.memset(xtail_t[:], 0.0), wr=["xtail"], tag="xtail-init")

    ev_wil = None
    for c in range(8):
        ev_wil = E.dma("p", w_in_t[:, c * 4096:(c + 1) * 4096],
                       w_in[c * 128:(c + 1) * 128, :], wr=["w_in"],
                       tag=f"ld-w_in{c}", q="pdW")
    ev_wx = E.dma("p", w_x_t[:],
                  bass.AP(w_x.tensor, 0, [[33, 128], [128 * 33, 16], [1, 33]]),
                  wr=["wxw"], tag="ld-wx", q="pdW")
    ev_cd = E.dma("p", l0cd_t[:],
                  bass.AP(l0cd.tensor, 0, [[128, 128], [128 * 128, 64], [1, 128]]),
                  wr=["l0cd"], tag="ld-l0cd", q="pdW")
    evW = ev_cd
    ev_wil = evW
    ev_wx = evW
    ev_cd = evW

    WOB = {0: 5, 1: 5, 2: 6, 3: 6, 4: 0, 5: 0, 6: 1, 7: 1}
    prev = {}
    T = TA
    for it in range(NTA):
        tg = f"t{it}"
        ev_x = E.dma("s", x_t[:], fmaj(x0[it], 8, TA),
                     wait=[prev.get("x_used")], wr=["x"], tag=f"{tg}-ldx", q="sdx")
        b0 = P[0]
        mm2ev = {}
        ev_sums = None
        for c in range(8):
            ev_sq = E.op("a", lambda a, c=c, sp_t=sp_t: a.activation(
                out=sp_t[:, (c % 2) * T:(c % 2) * T + T],
                in_=x_t[:, c * T:(c + 1) * T], func=AF.Square),
                wait=[ev_x, mm2ev.get(c % 2), prev.get("sp_free")],
                rd=["x"], wr=[f"sp{c % 2}"], tag=f"{tg}-sq{c}")
            E.op("t", lambda t, c=c: t.matmul(
                out=b0[0:1, 0:T], lhsT=ones_t[:, 0:1],
                rhs=x_t[:, c * T:(c + 1) * T], start=(c == 0), stop=False),
                wait=[ev_x, prev.get("b0_free"), evS] if c == 0 else [],
                rd=["x", "ones"], wr=["P0"], tag=f"{tg}-mmx{c}")
            ev_sums = E.op("t", lambda t, c=c, sp_t=sp_t: t.matmul(
                out=b0[0:1, T:2 * T], lhsT=ones_t[:, 0:1],
                rhs=sp_t[:, (c % 2) * T:(c % 2) * T + T],
                start=False, stop=(c == 7)),
                wait=[ev_sq], inc=["ts"], rd=[f"sp{c % 2}", "ones"], wr=["P0"],
                tag=f"{tg}-mmsq{c}")
            mm2ev[c % 2] = ev_sums
        e1 = E.op("v", lambda v, row_t=row_t: v.tensor_scalar(
            out=row_t[0:1, 0:2 * T], in0=b0[0:1, 0:2 * T], scalar1=1.0 / DM,
            scalar2=None, op0=ALU.mult), wait=[ev_sums],
            rd=["P0"], wr=["row"], tag=f"{tg}-ln1")
        prev["b0_rows_done"] = e1
        e2 = E.op("v", lambda v, row_t=row_t: v.tensor_mul(
            out=row_t[0:1, 2 * T:3 * T], in0=row_t[0:1, 0:T],
            in1=row_t[0:1, 0:T]), wait=[e1], rd=["row"], wr=["row"],
            tag=f"{tg}-ln2")
        e3 = E.op("v", lambda v, row_t=row_t: v.tensor_sub(
            out=row_t[0:1, 2 * T:3 * T], in0=row_t[0:1, T:2 * T],
            in1=row_t[0:1, 2 * T:3 * T]), wait=[e2], rd=["row"], wr=["row"],
            tag=f"{tg}-ln3")
        e4a = E.op("a", lambda a, row_t=row_t: a.activation(
            out=row_t[0:1, 3 * T:4 * T], in_=row_t[0:1, 2 * T:3 * T],
            func=AF.Sqrt, bias=eps_t[0:1, 0:1]), wait=[e3, ev_eps],
            rd=["row", "eps"], wr=["row"], tag=f"{tg}-ln4")
        e4 = E.op("v", lambda v, row_t=row_t: v.reciprocal(
            out=row_t[0:1, 3 * T:4 * T], in_=row_t[0:1, 3 * T:4 * T]),
            wait=[e4a], rd=["row"], wr=["row"], tag=f"{tg}-ln5")
        e5 = E.op("v", lambda v, row_t=row_t: v.tensor_mul(
            out=row_t[0:1, 2 * T:3 * T], in0=row_t[0:1, 0:T],
            in1=row_t[0:1, 3 * T:4 * T]), wait=[e4], rd=["row"], wr=["row"],
            tag=f"{tg}-ln6")
        e6 = E.op("v", lambda v, row_t=row_t: v.tensor_copy(
            out=row_t[0:1, 0:T], in_=row_t[0:1, 3 * T:4 * T]), wait=[e5],
            rd=["row"], wr=["row"], tag=f"{tg}-ln7")
        e7 = E.op("v", lambda v, row_t=row_t: v.tensor_scalar(
            out=row_t[0:1, T:2 * T], in0=row_t[0:1, 2 * T:3 * T], scalar1=-1.0,
            scalar2=None, op0=ALU.mult), wait=[e6], rd=["row"], wr=["row"],
            tag=f"{tg}-ln8")
        b1p = P[1]
        ebc = E.op("t", lambda t, row_t=row_t: t.matmul(
            out=b1p[:, 0:2 * T], lhsT=onesr_t[0:1, :], rhs=row_t[0:1, 0:2 * T],
            start=True, stop=True), wait=[e7, prev.get("b1_free"), evS], inc=["ts"],
            rd=["row", "onesr"], wr=["P1"], tag=f"{tg}-lnbc")
        ev_xn = None
        for c in range(8):
            em = E.op("v", lambda v, c=c, sp_t=sp_t: v.tensor_mul(
                out=sp_t[:, (c % 2) * T:(c % 2) * T + T],
                in0=x_t[:, c * T:(c + 1) * T], in1=b1p[:, 0:T]),
                wait=[ebc, mm2ev.get(c % 2), prev.get("xnbf_stash")],
                rd=["x", "P1"], wr=[f"sp{c % 2}"], tag=f"{tg}-xnm{c}")
            ev_xn = E.op("v", lambda v, c=c, sp_t=sp_t: v.tensor_tensor(
                out=xnbf_t[:, c * T:(c + 1) * T],
                in0=sp_t[:, (c % 2) * T:(c % 2) * T + T], in1=b1p[:, T:2 * T],
                op=ALU.add), wait=[em], rd=[f"sp{c % 2}", "P1"], wr=["xnbf"],
                tag=f"{tg}-xna{c}")
            mm2ev[c % 2] = ev_xn
        prev["b1_free"] = ev_xn
        prev["sp_free"] = ev_xn
        prev["xnbf_stash"] = E.dma("s", fmaj(xnbf_s[it], 8, TA), xnbf_t[:],
                                   wait=[ev_xn], rd=["xnbf"],
                                   wr=[f"d:xn:{it}"], tag=f"{tg}-stxn", q="sdn")

        ev_ssm_mm = None
        for oc in range(32):
            bank = P[oc % 2]
            bkk = f"P{oc % 2}"
            ev_mm = None
            for kc in range(8):
                w0 = []
                if kc == 0:
                    w0 = [ev_xn, prev.get(("xzf", oc % 2)), ev_wil]
                    if oc == 0:
                        w0.append(prev.get("b0_rows_done"))
                    if oc == 1:
                        w0.append(prev.get("b1_free"))
                ev_mm = E.op("t", lambda t, oc=oc, kc=kc, bank=bank: t.matmul(
                    out=bank[:, 0:T],
                    lhsT=w_in_t[:, kc * 4096 + oc * 128:kc * 4096 + oc * 128 + 128],
                    rhs=xnbf_t[:, kc * T:(kc + 1) * T],
                    start=(kc == 0), stop=(kc == 7)),
                    wait=w0, inc=["ts"] if kc == 7 else [],
                    rd=["w_in", "xnbf"], wr=[bkk], tag=f"{tg}-xz{oc}.{kc}")
            if oc < 16:
                ch = oc
                st0 = (ch % 2) * (TA + 3)
                stk_ = f"xmst{ch % 2}"
                ev_ev = E.op("a", lambda a, st0=st0, bank=bank: a.activation(
                    out=xm_st[:, st0 + 3:st0 + 3 + T], in_=bank[:, 0:T],
                    func=AF.Copy), wait=[ev_mm, prev.get(("xmst", ch % 2))],
                    rd=[bkk], wr=[stk_], tag=f"{tg}-xme{ch}")
                prev[("xzf", oc % 2)] = ev_ev
                ev_h = E.op("v", lambda v, st0=st0, ch=ch: v.tensor_copy(
                    out=xm_st[:, st0:st0 + 3], in_=xtail_t[:, ch * 3:ch * 3 + 3]),
                    wait=[ev_ev], rd=["xtail"], wr=[stk_], tag=f"{tg}-xmh{ch}")
                ev_tl = E.op("v", lambda v, st0=st0, ch=ch: v.tensor_copy(
                    out=xtail_t[:, ch * 3:ch * 3 + 3],
                    in_=xm_st[:, st0 + T:st0 + T + 3]), wait=[ev_h],
                    rd=[stk_], wr=["xtail"], tag=f"{tg}-xmt{ch}")
                cbk = P[2 + ch % 2]
                cbkk = f"P{2 + ch % 2}"
                ev_cv = None
                for k in range(4):
                    ev_cv = E.op("t", lambda t, ch=ch, k=k, st0=st0, cbk=cbk: t.matmul(
                        out=cbk[:, 0:T],
                        lhsT=l0cd_t[:, (ch * 4 + k) * 128:(ch * 4 + k) * 128 + 128],
                        rhs=xm_st[:, st0 + k:st0 + k + T],
                        start=(k == 0), stop=(k == 3)),
                        wait=[ev_tl, ev_cd, prev.get(("cvf", ch % 2))] if k == 0 else [],
                        inc=["ts"] if k == 3 else [],
                        rd=["l0cd", stk_], wr=[cbkk], tag=f"{tg}-cv{ch}.{k}")
                prev[("xmst", ch % 2)] = ev_cv
                ev_xc = E.op("a", lambda a, ch=ch, cbk=cbk: a.activation(
                    out=xc_t[:, ch * T:(ch + 1) * T], in_=cbk[:, 0:T],
                    func=AF.Silu, bias=cb_t[:, ch:ch + 1]),
                    wait=[ev_cv, prev.get(("xcu", ch)), evS],
                    rd=[cbkk, "cb"], wr=[f"xc{ch}"], tag=f"{tg}-xc{ch}")
                prev[("cvf", ch % 2)] = ev_xc
                ev_cb = E.op("v", lambda v, ch=ch: v.tensor_copy(
                    out=xcbf_st[:, (ch % 2) * T:(ch % 2) * T + T],
                    in_=xc_t[:, ch * T:(ch + 1) * T]),
                    wait=[ev_xc, prev.get(("xcbf", ch % 2))],
                    rd=[f"xc{ch}"], wr=[f"xcbf{ch % 2}"], tag=f"{tg}-xcb{ch}")
                ev_ssm_mm = E.op("t", lambda t, ch=ch: t.matmul(
                    out=P[4][0:33, 0:T],
                    lhsT=w_x_t[:, ch * 33:(ch + 1) * 33],
                    rhs=xcbf_st[:, (ch % 2) * T:(ch % 2) * T + T],
                    start=(ch == 0), stop=(ch == 15)),
                    wait=[ev_cb, ev_wx] + ([prev.get("b4_free")] if ch == 0 else []),
                    inc=["ts"], rd=["wxw", f"xcbf{ch % 2}"], wr=["P4"],
                    tag=f"{tg}-ssm{ch}")
                prev[("xcbf", ch % 2)] = ev_ssm_mm
            else:
                ch = oc - 16
                ev_z = E.op("a", lambda a, ch=ch, bank=bank: a.activation(
                    out=silz_t[:, ch * T:(ch + 1) * T], in_=bank[:, 0:T],
                    func=AF.Silu), wait=[ev_mm, prev.get(("szu", ch))],
                    rd=[bkk], wr=[f"silz{ch}"], tag=f"{tg}-sz{ch}")
                prev[("xzf", oc % 2)] = ev_z

        ev_se = E.op("v", lambda v: v.tensor_copy(out=ssm_t[:], in_=P[4][0:33, 0:T]),
                     wait=[ev_ssm_mm, prev.get("ssm_used"),
                           prev.get("ssm_used_pe")],
                     rd=["P4"], wr=["ssm"], tag=f"{tg}-sse")
        ev_bb = E.dma("s", bc_b[it, 0], ssm_t[1:17, :], wait=[ev_se],
                      rd=["ssm"], wr=[f"d:bcB:{it}"], tag=f"{tg}-bB", q="sdbB")
        ev_cc = E.dma("s", bc_b[it, 1], ssm_t[17:33, :], wait=[ev_se],
                      rd=["ssm"], wr=[f"d:bcC:{it}"], tag=f"{tg}-bC", q="sdbC")
        ev_rb = E.op("t", lambda t: t.matmul(
            out=P[4][:, 256:256 + T], lhsT=onesr_t[0:1, :], rhs=ssm_t[0:1, :],
            start=True, stop=True), wait=[ev_se, evS], inc=["ts"],
            rd=["ssm", "onesr"], wr=["P4"], tag=f"{tg}-rawbc")
        ev_br = E.dma("p", brep_t[:],
                      bass.AP(bc_b.tensor, bc_b[it, 0].offset, [[0, 128], [1, NST * TA]]),
                      wait=[ev_bb, prev.get("brep_used")],
                      rd=[f"d:bcB:{it}"], wr=["brep"], tag=f"{tg}-brep", q="sdr")
        ev_cr = E.dma("p", crep_t[:],
                      bass.AP(bc_b.tensor, bc_b[it, 1].offset, [[0, 128], [1, NST * TA]]),
                      wait=[ev_cc, prev.get("crep_used")],
                      rd=[f"d:bcC:{it}"], wr=["crep"], tag=f"{tg}-crep", q="sdc")

        prev[("sp0_scan", 0)] = ev_xn
        prev[("sp1_scan", 0)] = ev_xn
        last_ed = {}
        last_db = None
        last_q = None
        for ch in range(16):
            par = ch % 2
            s0 = par * 2 * T
            s1 = par * 2 * T + T
            dl = par * T
            k0 = f"sp0p{par}" if par else "sp0"
            k1 = f"sp1p{par}" if par else "sp1"
            kd = f"delta{par}"
            ea1 = E.op("a", lambda a, ch=ch, s0=s0, sp_t=sp_t: a.activation(
                out=sp_t[:, s0:s0 + T], in_=P[4][:, 256:256 + T],
                func=AF.Identity,
                scale=dtw_t[:, ch:ch + 1], bias=dtb_t[:, ch:ch + 1]),
                wait=[ev_rb, prev.get(("sp0_scan", par)), evS],
                rd=["P4", "dtw", "dtb"], wr=[k0], tag=f"{tg}-d1.{ch}")
            ea2 = E.op("a", lambda a, s0=s0, s1=s1, sp_t=sp_t: a.activation(
                out=sp_t[:, s1:s1 + T], in_=sp_t[:, s0:s0 + T], func=AF.Abs),
                wait=[ea1, prev.get(("sp1_scan", par))],
                rd=[k0], wr=[k1], tag=f"{tg}-d2.{ch}")
            ea3 = E.op("a", lambda a, s1=s1, sp_t=sp_t: a.activation(
                out=sp_t[:, s1:s1 + T], in_=sp_t[:, s1:s1 + T], func=AF.Exp,
                scale=-1.0),
                wait=[ea2], rd=[k1], wr=[k1], tag=f"{tg}-d3.{ch}")
            ea4 = E.op("a", lambda a, s1=s1, sp_t=sp_t: a.activation(
                out=sp_t[:, s1:s1 + T], in_=sp_t[:, s1:s1 + T], func=AF.Ln,
                bias=1.0),
                wait=[ea3], rd=[k1], wr=[k1], tag=f"{tg}-d4.{ch}")
            ea5 = E.op("a", lambda a, s0=s0, sp_t=sp_t: a.activation(
                out=sp_t[:, s0:s0 + T], in_=sp_t[:, s0:s0 + T], func=AF.Relu),
                wait=[ea4], rd=[k0], wr=[k0], tag=f"{tg}-d5.{ch}")
            ed = E.op("v", lambda v, s0=s0, s1=s1, dl=dl, sp_t=sp_t: v.tensor_add(
                out=delta_t[:, dl:dl + T], in0=sp_t[:, s0:s0 + T],
                in1=sp_t[:, s1:s1 + T]),
                wait=[ea5, prev.get(("delta_used", par)), evS],
                rd=[k0, k1], wr=[kd], tag=f"{tg}-dd.{ch}")
            prev[("sp0_scan", par)] = ed
            prev[("sp1_scan", par)] = ed
            last_ed[par] = ed
            edx = E.op("v", lambda v, ch=ch, dl=dl: v.tensor_mul(
                out=dx_t[:, dl:dl + T], in0=delta_t[:, dl:dl + T],
                in1=xc_t[:, ch * T:(ch + 1) * T]),
                wait=[ed], rd=[kd, f"xc{ch}"], wr=[f"dx{par}"],
                tag=f"{tg}-dx.{ch}")
            ev_y = None
            for nh in range(2):
                nb = nh * 8
                d8 = par * 8 * T
                ev_da = None
                for n in range(8):
                    ev_da = E.op("a", lambda a, ch=ch, n=n, nb=nb,
                                 d8=d8, dl=dl: a.activation(
                        out=da_t[:, d8 + n * T:d8 + (n + 1) * T],
                        in_=delta_t[:, dl:dl + T], func=AF.Exp,
                        scale=acol_t[:, ch * 16 + nb + n:ch * 16 + nb + n + 1]),
                        wait=[ed] + ([prev.get(("da_used", par))]
                                     if n == 0 else []),
                        rd=[kd, "acol"], wr=[f"da{par}"],
                        tag=f"{tg}-da.{ch}.{nb}.{n}")
                ev_db = E.op("v", lambda v, nb=nb, dl=dl: v.tensor_tensor(
                    out=sap(dbx_t, 0, [[T, 8], [1, T]]),
                    in0=sap(dx_t, dl, [[0, 8], [1, T]]),
                    in1=sap(brep_t, nb * T, [[T, 8], [1, T]]),
                    op=ALU.mult), wait=[edx, ev_br],
                    rd=[f"dx{par}", "brep"], wr=["dbx"], tag=f"{tg}-db.{ch}.{nb}")
                last_db = ev_db
                ev_sc = None
                for n in range(8):
                    ev_sc = E.op("v", lambda v, ch=ch, n=n, nb=nb,
                                 d8=d8: v.tensor_tensor_scan(
                        out=h_t[:, n * T:(n + 1) * T],
                        data0=da_t[:, d8 + n * T:d8 + (n + 1) * T],
                        data1=dbx_t[:, n * T:(n + 1) * T],
                        initial=hst_t[:, ch * 16 + nb + n:ch * 16 + nb + n + 1],
                        op0=ALU.mult, op1=ALU.add),
                        wait=[ev_da, ev_db] if n == 0 else [],
                        rd=[f"da{par}", "dbx", "hst"], wr=["h"],
                        tag=f"{tg}-sc.{ch}.{nb}.{n}")
                E.op("v", lambda v, ch=ch, nb=nb: v.tensor_copy(
                    out=hst_t[:, ch * 16 + nb:ch * 16 + nb + 8],
                    in_=sap(h_t, T - 1, [[T, 8]])), wait=[ev_sc],
                    rd=["h"], wr=["hst"], tag=f"{tg}-hc.{ch}.{nb}")
                ev_q = E.op("v", lambda v, nb=nb, d8=d8: v.tensor_mul(
                    out=da_t[:, d8:d8 + 8 * T], in0=h_t[:, 0:8 * T],
                    in1=crep_t[:, nb * T:(nb + 8) * T]),
                    wait=[ev_cr],
                    rd=["h", "crep"], wr=[f"da{par}"], tag=f"{tg}-q.{ch}.{nb}")
                last_q = ev_q
                ev_y = E.op("v", lambda v, nh=nh, d8=d8: v.tensor_reduce(
                    out=y_t[:, nh * T:(nh + 1) * T],
                    in_=sap(da_t, d8, [[1, T], [T, 8]]),
                    axis=AXX, op=ALU.add), wait=[ev_q],
                    rd=[f"da{par}"], wr=["y"], tag=f"{tg}-yr.{ch}.{nb}")
                prev[("da_used", par)] = ev_y
            prev[("delta_used", par)] = ev_da
            ey = E.op("v", lambda v: v.tensor_add(
                out=y_t[:, 0:T], in0=y_t[:, 0:T], in1=y_t[:, T:2 * T]),
                wait=[ev_y], rd=["y"], wr=["y"], tag=f"{tg}-ya.{ch}")
            ey2 = E.op("v", lambda v, ch=ch: v.scalar_tensor_tensor(
                out=y_t[:, T:2 * T], in0=xc_t[:, ch * T:(ch + 1) * T],
                scalar=dres_t[:, ch:ch + 1], in1=y_t[:, 0:T],
                op0=ALU.mult, op1=ALU.add), wait=[ey, evS],
                rd=[f"xc{ch}", "dres", "y"], wr=["y"], tag=f"{tg}-y2.{ch}")
            prev[("xcu", ch)] = ey2
            ey3 = E.op("v", lambda v, ch=ch, par=par: v.tensor_mul(
                out=y2bf_t[:, par * T:(par + 1) * T], in0=y_t[:, T:2 * T],
                in1=silz_t[:, ch * T:(ch + 1) * T]),
                wait=[ey2, prev.get(("y2_used", par))],
                rd=["y", f"silz{ch}"], wr=[f"y2{par}"], tag=f"{tg}-y3.{ch}")
            prev[("szu", ch)] = ey3
            ev_wl = E.dma("p", wout_db[:, (ch % 2) * DM:(ch % 2) * DM + DM],
                          w_out[ch * 128:(ch + 1) * 128, :],
                          wait=[prev.get(("wou", ch % 2))],
                          wr=[f"wod{ch % 2}"], tag=f"{tg}-wol.{ch}", q=f"pdw{ch % 2}")
            ev_mo = None
            for m in range(8):
                bk = P[WOB[m]]
                col = (m % 2) * 256
                w0 = []
                if m == 0:
                    w0 = [ev_wl, ey3]
                    if ch == 0:
                        w0 += [prev.get("wo_evac"), prev.get(("xzf", 0)),
                               prev.get(("xzf", 1))]
                ev_mo = E.op("t", lambda t, ch=ch, m=m, bk=bk, col=col: t.matmul(
                    out=bk[:, col:col + T],
                    lhsT=wout_db[:, (ch % 2) * DM + m * 128:(ch % 2) * DM + m * 128 + 128],
                    rhs=y2bf_t[:, (ch % 2) * T:(ch % 2) * T + T],
                    start=(ch == 0 and m % 2 == 0),
                    stop=(ch == 15 and m % 2 == 1)),
                    wait=w0, inc=["ts"] if m == 7 else [],
                    rd=[f"wod{ch % 2}", f"y2{ch % 2}"], wr=[f"P{WOB[m]}"],
                    tag=f"{tg}-wo.{ch}.{m}")
            prev[("wou", ch % 2)] = ev_mo
            prev[("y2_used", ch % 2)] = ev_mo
        if mode == "dbg" and it == 0:
            E.dma("s", dbgb[0][:, 0:2048], xnbf_t[:], wait=[ev_xn], rd=["xnbf"], tag="dbg0", q="sdd")
            E.dma("s", dbg[0], xc_t[:], wait=[E.ev("as")], rd=[], tag="dbg1", q="sdd")
            E.dma("s", dbg[1], silz_t[:], wait=[E.ev("as")], rd=[], tag="dbg2", q="sdd")
            E.dma("s", dbg[2][0:33, 0:TA], ssm_t[:], wait=[ev_se], rd=["ssm"], tag="dbg3", q="sdd")
            E.dma("s", dbg[3], brep_t[:], wait=[ev_br], rd=["brep"], tag="dbg4", q="sdd")
            E.dma("s", dbg[4], crep_t[:], wait=[ev_cr], rd=["crep"], tag="dbg5", q="sdd")
            E.dma("s", dbg[5][:, 0:TA], delta_t[:], wait=[E.ev("vs")], rd=[], tag="dbg6", q="sdd")
            E.dma("s", dbg[6][:, 0:2048], da_t[:], wait=[E.ev("vs")], rd=[], tag="dbg7", q="sdd")
            E.dma("s", dbg[7][:, 0:2048], dbx_t[:], wait=[E.ev("vs")], rd=[], tag="dbg8", q="sdd")
            E.dma("s", dbg[8][:, 0:2048], h_t[:], wait=[E.ev("vs")], rd=[], tag="dbg9", q="sdd")
            E.dma("s", dbg[9][:, 0:512], y_t[:], wait=[E.ev("vs")], rd=[], tag="dbg10", q="sdd")
            E.dma("s", dbg[11][:, 0:256], hst_t[:], wait=[E.ev("vs")], rd=[], tag="dbg11", q="sdd")
        ev_ev = None
        for m in range(8):
            bk = P[WOB[m]]
            col = (m % 2) * 256
            ev_ev = E.op("v", lambda v, m=m, bk=bk, col=col: v.tensor_tensor(
                out=x_t[:, m * T:(m + 1) * T], in0=bk[:, col:col + T],
                in1=x_t[:, m * T:(m + 1) * T], op=ALU.add), wait=[ev_mo],
                rd=[f"P{WOB[m]}", "x"], wr=["x"], tag=f"{tg}-oev{m}")
        prev["wo_evac"] = ev_ev
        prev["b0_free"] = ev_ev
        prev["b1_free"] = ev_ev
        prev[("xzf", 0)] = ev_ev
        prev[("xzf", 1)] = ev_ev
        xp_dest = yout if mode == "xpart" else xpart_s
        ev_out = E.dma("s", fmaj(xp_dest[it], 8, TA), x_t[:], wait=[ev_ev],
                       rd=["x"], wr=[f"d:xp:{it}"], tag=f"{tg}-stxp", q="sdo")
        prev["x_used"] = ev_out
        prev["ssm_used"] = ev_cc
        prev["ssm_used_pe"] = ev_rb
        prev["sp_free"] = last_ed[0]
        prev["brep_used"] = last_db
        prev["crep_used"] = last_q
        prev["b4_free"] = E.ev("as")

    ev_sdn_tot = E.ev("sdn")
    ev_sdo_tot = E.ev("sdo")
    pa.close()
    E.barrier()

    if mode == "xpart":
        qsems = [k for k in E.cnt if k in E.sem_owner and E.sem_owner[k].endswith("q")]
        E.op("s", lambda e: e.nop(), wait=[E.ev(k) for k in qsems], inc=[])
        E.verify()
        _finish(nc, E, es)
        return nc

    pf = ExitStack()
    w1s_t = sb("w1s_t", [128, 8 * 4096], BF16, stk=pf)
    w2s_t = sb("w2s_t", [128, 32 * 1024], BF16, stk=pf)
    xnbf2_t = sb("xnbf2_t", [128, 8 * TA], BF16, stk=pf)
    xp_t = sb("xp_t", [128, 8 * TA], stk=pf)
    hbf_t = sb("hbf_t", [128, 32 * TA], BF16, stk=pf)
    sp2_t = sb("sp2_t", [128, 2 * TA], stk=pf)

    ev_w1l = None
    for c in range(8):
        ev_w1l = E.dma("s", w1s_t[:, c * 4096:(c + 1) * 4096],
                       w1s_s[c * 128:(c + 1) * 128, :],
                       wait=[ev_w1] if c == 0 else (),
                       rd=["d:w1s"], wr=["w1s"], tag=f"ld-w1s{c}", q="sdw1")
    ev_w2l = None
    for c in range(32):
        ev_w2l = E.dma("s", w2s_t[:, c * 1024:(c + 1) * 1024],
                       w2s_s[c * 128:(c + 1) * 128, :],
                       wait=[ev_w2] if c == 0 else (),
                       rd=["d:w2s"], wr=["w2s"], tag=f"ld-w2s{c}", q="sdw2")

    FOB = {0: 4, 1: 4, 2: 5, 3: 5, 4: 6, 5: 6, 6: 7, 7: 7}
    prev = {}
    for it in range(NTA):
        tg = f"f{it}"
        ev_xn = E.dma("s", xnbf2_t[:], fmaj(xnbf_s[it], 8, TA),
                      wait=[prev.get("xn_used"), ev_sdn_tot],
                      rd=[f"d:xn:{it}"], wr=["xnbf2"], tag=f"{tg}-ldxn", q="sdn2")
        ev_xp = E.dma("s", xp_t[:], fmaj(xpart_s[it], 8, TA),
                      wait=[prev.get("xp_used"), ev_sdo_tot],
                      rd=[f"d:xp:{it}"], wr=["xp"], tag=f"{tg}-ldxp", q="sdp2")
        ev_hl = {}
        for oc in range(32):
            bank = P[oc % 4]
            ev_mm = None
            for kc in range(8):
                ev_mm = E.op("t", lambda t, oc=oc, kc=kc, bank=bank: t.matmul(
                    out=bank[:, 0:TA],
                    lhsT=w1s_t[:, kc * 4096 + oc * 128:kc * 4096 + oc * 128 + 128],
                    rhs=xnbf2_t[:, kc * TA:(kc + 1) * TA],
                    start=(kc == 0), stop=(kc == 7)),
                    wait=[ev_xn, prev.get(("ff", oc % 4)), ev_w1l] if kc == 0 else [],
                    inc=["ts"] if kc == 7 else [],
                    rd=["w1s", "xnbf2"], wr=[f"P{oc % 4}"], tag=f"{tg}-h{oc}.{kc}")
            ev_h = E.op("a", lambda a, oc=oc, bank=bank: a.activation(
                out=hbf_t[:, oc * TA:(oc + 1) * TA], in_=bank[:, 0:TA],
                func=AF.Gelu, scale=a1_t[:, oc:oc + 1], bias=b1_t[:, oc:oc + 1]),
                wait=[ev_mm, evS], rd=[f"P{oc % 4}", "a1", "b1"], wr=[f"hbf{oc}"],
                tag=f"{tg}-he{oc}")
            prev[("ff", oc % 4)] = ev_h
            ev_hl[oc] = ev_h
        ev_o = None
        for kc in range(32):
            for m in range(8):
                bk = P[FOB[m]]
                col = (m % 2) * 256
                w0 = []
                if m == 0:
                    w0 = [ev_hl[kc], ev_w2l]
                    if kc == 0:
                        w0.append(prev.get("fo_evac"))
                ev_o = E.op("t", lambda t, kc=kc, m=m, bk=bk, col=col: t.matmul(
                    out=bk[:, col:col + TA],
                    lhsT=w2s_t[:, kc * 1024 + m * 128:kc * 1024 + m * 128 + 128],
                    rhs=hbf_t[:, kc * TA:(kc + 1) * TA],
                    start=(kc == 0 and m % 2 == 0), stop=(kc == 31 and m % 2 == 1)),
                    wait=w0, inc=["ts"] if m == 7 else [],
                    rd=["w2s", f"hbf{kc}"], wr=[f"P{FOB[m]}"], tag=f"{tg}-o{kc}.{m}")
        ev_ev = None
        for m in range(8):
            bk = P[FOB[m]]
            col = (m % 2) * 256
            ev_s = E.op("a", lambda a, m=m, bk=bk, col=col: a.activation(
                out=sp2_t[:, (m % 2) * TA:(m % 2) * TA + TA],
                in_=bk[:, col:col + TA], func=AF.Identity,
                scale=a2_t[:, m:m + 1], bias=b2_t[:, m:m + 1]),
                wait=[ev_o, prev.get(("sp2", m % 2)), evS],
                rd=[f"P{FOB[m]}", "a2", "b2"], wr=[f"sp2{m % 2}"],
                tag=f"{tg}-fe{m}")
            ev_ev = E.op("v", lambda v, m=m: v.tensor_tensor(
                out=xp_t[:, m * TA:(m + 1) * TA],
                in0=sp2_t[:, (m % 2) * TA:(m % 2) * TA + TA],
                in1=xp_t[:, m * TA:(m + 1) * TA], op=ALU.add),
                wait=[ev_s, ev_xp], rd=[f"sp2{m % 2}", "xp"], wr=["xp"],
                tag=f"{tg}-fa{m}")
            prev[("sp2", m % 2)] = ev_ev
        prev["fo_evac"] = ev_ev
        dest = yout if mode == "x1" else x1_s
        ev_out = E.dma("s", fmaj(dest[it], 8, TA), xp_t[:], wait=[ev_ev],
                       rd=["xp"], wr=[f"d:y:{it}"], tag=f"{tg}-sty", q="sdy")
        prev["xn_used"] = ev_xn
        prev["xp_used"] = ev_out
    pf.close()
    E.barrier()

    def _tail():
        qsems = [k for k in E.cnt if k in E.sem_owner
                 and E.sem_owner[k].endswith("q")]
        E.op("s", lambda e: e.nop(), wait=[E.ev(k) for k in qsems], inc=[])

    if mode == "x1":
        _tail()
        E.verify()
        _finish(nc, E, es)
        return nc

    ev_sdy_tot = E.ev("sdy")
    pb0 = ExitStack()
    qw_t = sb("qw_t", [128, 8 * DM], BF16, stk=pb0)
    kw_t = sb("kw_t", [128, 8 * DM], BF16, stk=pb0)
    vw_t = sb("vw_t", [128, 8 * DM], BF16, stk=pb0)
    bw_t = sb("bw_t", [128, 8 * 4], BF16, stk=pb0)
    gm1_t = sb("gm1_t", [128, 8 * 2 * DM], BF16, stk=pb0)
    gm2_t = sb("gm2_t", [128, 16 * 16], BF16, stk=pb0)
    gmb1_t = sb("gmb1_t", [128, 16], F32, stk=pb0)
    gmb2_t = sb("gmb2_t", [16, 1], F32, stk=pb0)
    qkvcd_t = sb("qkvcd_t", [128, 96 * 128], BF16, stk=pb0)
    bsel_t = sb("bsel_t", [4, 512], F32, stk=pb0)
    x1_t = sb("x1_t", [128, 8 * TA], stk=pb0)
    xnb_t = sb("xnb_t", [128, 8 * TA], BF16, stk=pb0)
    sp_t = sb("spb_t", [128, 2 * TA], stk=pb0)
    row_t = sb("rowb_t", [1, 4 * TA], stk=pb0)
    pst_t = sb("pst_t", [128, 2 * (TA + 3)], BF16, stk=pb0)
    ptl_t = sb("ptl_t", [128, 24 * 3], BF16, stk=pb0)
    qkvf_t = sb("qkvf_t", [128, 24 * TA], stk=pb0)
    vvb_t = sb("vvb_t", [128, 8 * TA], BF16, stk=pb0)
    qhb_t = sb("qhb_t", [128, 16 * TA], BF16, stk=pb0)
    kbb_t = sb("kbb_t", [128, 8 * TA], BF16, stk=pb0)
    vbb_t = sb("vbb_t", [128, 8 * TA], BF16, stk=pb0)
    nrm_t = sb("nrm_t", [1, 8 * TA], stk=pb0)
    beta_t = sb("beta_t", [4, TA], stk=pb0)
    gh_t = sb("gh_t", [128, 16 * TA], BF16, stk=pb0)
    g16_t = sb("g16_t", [16, TA], stk=pb0)
    tm_t = sb("tm_t", [128, DM], stk=pb0)
    tmb_t = sb("tmb_t", [128, 3 * DM], BF16, stk=pb0)
    g16tm_t = sb("g16tm_t", [128, 2 * 16], stk=pb0)
    mx1_t = sb("mx1_t", [128, 2], stk=pb0)

    ev_m0 = E.op("v", lambda v: v.memset(mx1_t[:], 0.0), wr=["mx1"], tag="mx1-init")
    E.op("v", lambda v: v.memset(ptl_t[:], 0.0), wr=["ptl"], tag="ptl-init")

    for (d, sr, nm) in [(qw_t, qwT, "qw"), (kw_t, kwT, "kw"), (vw_t, vwT, "vw"),
                        (bw_t, bwT, "bw"), (gm1_t, gm1T, "gm1"), (gm2_t, gm2T, "gm2")]:
        E.dma("p", d[:], sr, wr=[nm], tag=f"ld-{nm}", q="pdB")
    evWB = E.dma("p", qkvcd_t[:],
                 bass.AP(qkvcd.tensor, 0, [[128, 128], [128 * 128, 96], [1, 128]]),
                 wr=["qkvcd"], tag="ld-qkvcd", q="pdB")
    E.dma("s", gmb1_t[:], gmb1, wr=["gmb1"], tag="ld-gmb1", q="sdS2")
    E.dma("s", gmb2_t[:], gmb2, wr=["gmb2"], tag="ld-gmb2", q="sdS2")
    E.dma("s", id_bf_t[:], id_bf, wr=["idb"], tag="ld-idb", q="sdS2")
    E.dma("s", idf_t[:], id_f32, wr=["idf"], tag="ld-idf", q="sdS2")
    E.dma("s", bsel_t[:], bsel, wr=["bsel"], tag="ld-bsel", q="sdS2")
    evS2 = E.ev("sdS2")
    ev_gb1 = evS2
    ev_idf = evS2
    ev_ib = evS2

    T = TA
    prev = {}
    zpad_t = sb("zpad_t", [128, 31], BF16, stk=pb0)
    ev_z0 = E.op("v", lambda v: v.memset(zpad_t[:], 0.0),
                 wr=["zpad"], tag="vzp")
    ev_zd = E.dma("s",
                  bass.AP(vF.tensor, 0, [[31 + L, 128], [128 * (31 + L), 8], [1, 31]]),
                  bass.AP(zpad_t[:].tensor, zpad_t[:].offset,
                          [[31, 128], [0, 8], [1, 31]]),
                  wait=[ev_z0], rd=["zpad"], wr=["d:vFz"], tag="stvz", q="sdf4")
    for it in range(NTA):
        tg = f"b{it}"
        ev_x = E.dma("s", x1_t[:], fmaj(x1_s[it], 8, TA),
                     wait=[prev.get("x_used"), ev_sdy_tot,
                           prev.get("dbg_hold")],
                     rd=[f"d:y:{it}"], wr=["x"], tag=f"{tg}-ldx", q="sdx")
        b0 = P[0]
        mm2ev = {}
        ev_sums = None
        for c in range(8):
            ev_sq = E.op("a", lambda a, c=c, sp_t=sp_t: a.activation(
                out=sp_t[:, (c % 2) * T:(c % 2) * T + T],
                in_=x1_t[:, c * T:(c + 1) * T], func=AF.Square),
                wait=[ev_x, mm2ev.get(c % 2), prev.get("sp_free")],
                rd=["x"], wr=[f"sp{c % 2}"], tag=f"{tg}-sq{c}")
            E.op("t", lambda t, c=c: t.matmul(
                out=b0[0:1, 0:T], lhsT=ones_t[:, 0:1],
                rhs=x1_t[:, c * T:(c + 1) * T], start=(c == 0), stop=False),
                wait=[ev_x, prev.get("b0_free")] if c == 0 else [],
                rd=["x", "ones"], wr=["P0"], tag=f"{tg}-mmx{c}")
            ev_sums = E.op("t", lambda t, c=c, sp_t=sp_t: t.matmul(
                out=b0[0:1, T:2 * T], lhsT=ones_t[:, 0:1],
                rhs=sp_t[:, (c % 2) * T:(c % 2) * T + T],
                start=False, stop=(c == 7)),
                wait=[ev_sq], inc=["ts"], rd=[f"sp{c % 2}", "ones"], wr=["P0"],
                tag=f"{tg}-mmsq{c}")
            mm2ev[c % 2] = ev_sums
        e1 = E.op("v", lambda v, row_t=row_t: v.tensor_scalar(
            out=row_t[0:1, 0:2 * T], in0=b0[0:1, 0:2 * T], scalar1=1.0 / DM,
            scalar2=None, op0=ALU.mult), wait=[ev_sums],
            rd=["P0"], wr=["row"], tag=f"{tg}-ln1")
        prev["b0_free"] = e1
        e2 = E.op("v", lambda v, row_t=row_t: v.tensor_mul(
            out=row_t[0:1, 2 * T:3 * T], in0=row_t[0:1, 0:T],
            in1=row_t[0:1, 0:T]), wait=[e1], rd=["row"], wr=["row"],
            tag=f"{tg}-ln2")
        e3 = E.op("v", lambda v, row_t=row_t: v.tensor_sub(
            out=row_t[0:1, 2 * T:3 * T], in0=row_t[0:1, T:2 * T],
            in1=row_t[0:1, 2 * T:3 * T]), wait=[e2], rd=["row"], wr=["row"],
            tag=f"{tg}-ln3")
        e4a = E.op("a", lambda a, row_t=row_t: a.activation(
            out=row_t[0:1, 3 * T:4 * T], in_=row_t[0:1, 2 * T:3 * T],
            func=AF.Sqrt, bias=eps_t[0:1, 0:1]), wait=[e3, ev_eps],
            rd=["row", "eps"], wr=["row"], tag=f"{tg}-ln4")
        e4 = E.op("v", lambda v, row_t=row_t: v.reciprocal(
            out=row_t[0:1, 3 * T:4 * T], in_=row_t[0:1, 3 * T:4 * T]),
            wait=[e4a], rd=["row"], wr=["row"], tag=f"{tg}-ln5")
        e5 = E.op("v", lambda v, row_t=row_t: v.tensor_mul(
            out=row_t[0:1, 2 * T:3 * T], in0=row_t[0:1, 0:T],
            in1=row_t[0:1, 3 * T:4 * T]), wait=[e4], rd=["row"], wr=["row"],
            tag=f"{tg}-ln6")
        e6 = E.op("v", lambda v, row_t=row_t: v.tensor_copy(
            out=row_t[0:1, 0:T], in_=row_t[0:1, 3 * T:4 * T]), wait=[e5],
            rd=["row"], wr=["row"], tag=f"{tg}-ln7")
        e7 = E.op("v", lambda v, row_t=row_t: v.tensor_scalar(
            out=row_t[0:1, T:2 * T], in0=row_t[0:1, 2 * T:3 * T], scalar1=-1.0,
            scalar2=None, op0=ALU.mult), wait=[e6], rd=["row"], wr=["row"],
            tag=f"{tg}-ln8")
        b1p = P[1]
        ebc = E.op("t", lambda t, row_t=row_t: t.matmul(
            out=b1p[:, 0:2 * T], lhsT=onesr_t[0:1, :], rhs=row_t[0:1, 0:2 * T],
            start=True, stop=True), wait=[e7, prev.get("b1_free")], inc=["ts"],
            rd=["row", "onesr"], wr=["P1"], tag=f"{tg}-lnbc")
        ev_xn = None
        for c in range(8):
            em = E.op("v", lambda v, c=c, sp_t=sp_t: v.tensor_mul(
                out=sp_t[:, (c % 2) * T:(c % 2) * T + T],
                in0=x1_t[:, c * T:(c + 1) * T], in1=b1p[:, 0:T]),
                wait=[ebc, mm2ev.get(c % 2)],
                rd=["x", "P1"], wr=[f"sp{c % 2}"], tag=f"{tg}-xnm{c}")
            ev_xn = E.op("v", lambda v, c=c, sp_t=sp_t: v.tensor_tensor(
                out=xnb_t[:, c * T:(c + 1) * T],
                in0=sp_t[:, (c % 2) * T:(c % 2) * T + T], in1=b1p[:, T:2 * T],
                op=ALU.add), wait=[em], rd=[f"sp{c % 2}", "P1"], wr=["xnb"],
                tag=f"{tg}-xna{c}")
            mm2ev[c % 2] = ev_xn
        prev["b1_free"] = ev_xn
        prev["sp_free"] = ev_xn
        ev_mx = E.op("v", lambda v: v.tensor_reduce(
            out=mx1_t[:, 1:2], in_=xnb_t[:], axis=AXX, op=ALU.max,
            apply_absolute_value=True), wait=[ev_xn],
            rd=["xnb"], wr=["mx1"], tag=f"{tg}-mxr")
        prev["mxw"] = E.op("v", lambda v: v.tensor_tensor(
            out=mx1_t[:, 0:1], in0=mx1_t[:, 0:1], in1=mx1_t[:, 1:2],
            op=ALU.max), wait=[ev_mx, ev_m0], rd=["mx1"], wr=["mx1"],
            tag=f"{tg}-mxa")

        wsel = {0: (qw_t, "qw"), 1: (kw_t, "kw"), 2: (vw_t, "vw")}
        for oc in range(24):
            wt, wky = wsel[oc // 8]
            occ = oc % 8
            bank = P[2 + oc % 2]
            bkk = f"P{2 + oc % 2}"
            ev_mm = None
            for kc in range(8):
                w0 = []
                if kc == 0:
                    w0 = [ev_xn, prev.get(("pjf", oc % 2)), evWB]
                ev_mm = E.op("t", lambda t, wt=wt, occ=occ, kc=kc, bank=bank: t.matmul(
                    out=bank[:, 0:T],
                    lhsT=wt[:, kc * DM + occ * 128:kc * DM + occ * 128 + 128],
                    rhs=xnb_t[:, kc * T:(kc + 1) * T],
                    start=(kc == 0), stop=(kc == 7)),
                    wait=w0, inc=["ts"] if kc == 7 else [],
                    rd=[wky, "xnb"], wr=[bkk], tag=f"{tg}-pj{oc}.{kc}")
            st0 = (oc % 2) * (TA + 3)
            stk_ = f"pst{oc % 2}"
            ev_ev = E.op("a", lambda a, st0=st0, bank=bank: a.activation(
                out=pst_t[:, st0 + 3:st0 + 3 + T], in_=bank[:, 0:T],
                func=AF.Copy), wait=[ev_mm, prev.get(("pstw", oc % 2))],
                rd=[bkk], wr=[stk_], tag=f"{tg}-pje{oc}")
            prev[("pjf", oc % 2)] = ev_ev
            ev_h = E.op("v", lambda v, st0=st0, oc=oc: v.tensor_copy(
                out=pst_t[:, st0:st0 + 3], in_=ptl_t[:, oc * 3:oc * 3 + 3]),
                wait=[ev_ev], rd=["ptl"], wr=[stk_], tag=f"{tg}-pjh{oc}")
            ev_tl = E.op("v", lambda v, st0=st0, oc=oc: v.tensor_copy(
                out=ptl_t[:, oc * 3:oc * 3 + 3],
                in_=pst_t[:, st0 + T:st0 + T + 3]), wait=[ev_h],
                rd=[stk_], wr=["ptl"], tag=f"{tg}-pjt{oc}")
            cbk = P[4 + oc % 2]
            cbkk = f"P{4 + oc % 2}"
            ev_cv = None
            for k in range(4):
                ev_cv = E.op("t", lambda t, oc=oc, k=k, st0=st0, cbk=cbk: t.matmul(
                    out=cbk[:, 0:T],
                    lhsT=qkvcd_t[:, (oc * 4 + k) * 128:(oc * 4 + k) * 128 + 128],
                    rhs=pst_t[:, st0 + k:st0 + k + T],
                    start=(k == 0), stop=(k == 3)),
                    wait=[ev_tl, prev.get(("cvf2", oc % 2))] if k == 0 else [],
                    inc=["ts"] if k == 3 else [],
                    rd=["qkvcd", stk_], wr=[cbkk], tag=f"{tg}-cv{oc}.{k}")
            prev[("pstw", oc % 2)] = ev_cv
            ev_sl = E.op("a", lambda a, oc=oc, cbk=cbk: a.activation(
                out=qkvf_t[:, oc * T:(oc + 1) * T], in_=cbk[:, 0:T],
                func=AF.Silu), wait=[ev_cv, prev.get(("qkvfw", oc)),
                                     prev.get(("qkvfwT", oc))],
                rd=[cbkk], wr=[f"qkvf{oc}"], tag=f"{tg}-sl{oc}")
            prev[("cvf2", oc % 2)] = ev_sl
            if oc >= 16:
                E.op("v", lambda v, oc=oc: v.tensor_copy(
                    out=vvb_t[:, (oc - 16) * T:(oc - 15) * T],
                    in_=qkvf_t[:, oc * T:(oc + 1) * T]),
                    wait=[ev_sl, prev.get(("vvbw", oc - 16))],
                    rd=[f"qkvf{oc}"], wr=[f"vvb{oc - 16}"], tag=f"{tg}-cb{oc}")

        if it == 0:
            E.dma("s", dbg_xn, xnb_t[:], wait=[ev_xn],
                  rd=["xnb"], wr=["d:dbgxn"], tag="dbg-xn", q="sdbg")
            prev["dbg_hold"] = E.dma(
                "s", dbg_qf, qkvf_t[:], wait=[ev_sl],
                rd=[f"qkvf{o}" for o in range(24)], wr=["d:dbgqf"],
                tag="dbg-qf", q="sdbg")

        ev_bm = None
        for kc in range(8):
            ev_bm = E.op("t", lambda t, kc=kc: t.matmul(
                out=P[6][0:4, 0:T], lhsT=bw_t[:, kc * 4:(kc + 1) * 4],
                rhs=xnb_t[:, kc * T:(kc + 1) * T],
                start=(kc == 0), stop=(kc == 7)),
                wait=[ev_xn, prev.get("b6_free")] if kc == 0 else [],
                inc=["ts"] if kc == 7 else [],
                rd=["bw", "xnb"], wr=["P6"], tag=f"{tg}-bm{kc}")
        ev_bs = E.op("a", lambda a: a.activation(
            out=beta_t[:], in_=P[6][0:4, 0:T], func=AF.Sigmoid),
            wait=[ev_bm, prev.get("beta_used")],
            rd=["P6"], wr=["beta"], tag=f"{tg}-bsig")

        ghl = {}
        for oc in range(16):
            bank = P[2 + oc % 2]
            bkk = f"P{2 + oc % 2}"
            ev_mm = None
            for kc in range(8):
                ev_mm = E.op("t", lambda t, oc=oc, kc=kc, bank=bank: t.matmul(
                    out=bank[:, 0:T],
                    lhsT=gm1_t[:, kc * 2 * DM + oc * 128:kc * 2 * DM + oc * 128 + 128],
                    rhs=xnb_t[:, kc * T:(kc + 1) * T],
                    start=(kc == 0), stop=(kc == 7)),
                    wait=[ev_xn, prev.get(("pjf", oc % 2))] if kc == 0 else [],
                    inc=["ts"] if kc == 7 else [],
                    rd=["gm1", "xnb"], wr=[bkk], tag=f"{tg}-g1.{oc}.{kc}")
            ev_gh = E.op("a", lambda a, oc=oc, bank=bank: a.activation(
                out=gh_t[:, oc * T:(oc + 1) * T], in_=bank[:, 0:T],
                func=AF.Gelu, bias=gmb1_t[:, oc:oc + 1]),
                wait=[ev_mm, ev_gb1, prev.get(("ghw", oc))],
                rd=[bkk, "gmb1"], wr=[f"gh{oc}"], tag=f"{tg}-g1e{oc}")
            prev[("pjf", oc % 2)] = ev_gh
            ghl[oc] = ev_gh
        ev_g2 = None
        for kc in range(16):
            ev_g2 = E.op("t", lambda t, kc=kc: t.matmul(
                out=P[7][0:16, 0:T], lhsT=gm2_t[:, kc * 16:(kc + 1) * 16],
                rhs=gh_t[:, kc * T:(kc + 1) * T],
                start=(kc == 0), stop=(kc == 15)),
                wait=[ghl[kc]] + ([prev.get("b7_free")] if kc == 0 else []),
                inc=["ts"] if kc == 15 else [],
                rd=["gm2", f"gh{kc}"], wr=["P7"], tag=f"{tg}-g2.{kc}")
        ev_g16 = E.op("v", lambda v: v.tensor_scalar(
            out=g16_t[:], in0=P[7][0:16, 0:T], scalar1=gmb2_t[:, 0:1],
            scalar2=None, op0=ALU.add),
            wait=[ev_g2, evS2, prev.get("g16_used")],
            rd=["P7", "gmb2"], wr=["g16"], tag=f"{tg}-g2e")
        ev_gt = None
        for hf in range(2):
            ev_gt = E.op("t", lambda t, hf=hf: t.matmul(
                out=P[6][0:128, 256 + hf * 16:256 + hf * 16 + 16],
                lhsT=g16_t[0:16, hf * 128:hf * 128 + 128],
                rhs=idf_t[0:16, 0:16], start=(hf == 0), stop=(hf == 1)),
                wait=[ev_g16, ev_idf, ev_bs] if hf == 0 else [],
                inc=["ts"] if hf == 1 else [],
                rd=["g16", "idf"], wr=["P6"], tag=f"{tg}-gt{hf}")
        ev_gte = None
        for hf in range(2):
            ev_gte = E.op("v", lambda v, hf=hf: v.tensor_copy(
                out=g16tm_t[:, hf * 16:(hf + 1) * 16],
                in_=P[6][0:128, 256 + hf * 16:256 + hf * 16 + 16]),
                wait=[ev_gt, prev.get(("g16tm_used", hf))],
                rd=["P6"], wr=[f"g16tm{hf}"], tag=f"{tg}-gte{hf}")
            E.dma("s", gateT[2 * it + hf], g16tm_t[:, hf * 16:(hf + 1) * 16],
                  wait=[ev_gte], rd=[f"g16tm{hf}"], wr=[f"d:gT:{2 * it + hf}"],
                  tag=f"{tg}-stg{hf}", q=f"sdg{hf}")
            prev[("g16tm_used", hf)] = E.ev(f"sdg{hf}")
        prev["g16_used"] = E.ev("ts")

        REG = [(P[6], 0, "P6"), (P[6], 256, "P6"), (P[7], 0, "P7"),
               (P[7], 256, "P7")]
        ev_nv = None
        for idx in range(8):
            bnk, coff, bky = REG[idx % 4]
            w_extra = [] if ev_nv is None else [ev_nv]
            if idx == 0:
                w_extra += [ev_gte, prev.get("nrm_used")]
            elif idx == 1:
                w_extra += [ev_gte]
            elif idx in (2, 3):
                w_extra += [ev_g16]
            ev_nm = None
            for j in range(2):
                c = idx * 2 + j
                ev_sq = E.op("a", lambda a, c=c, sp_t=sp_t: a.activation(
                    out=sp_t[:, (c % 2) * T:(c % 2) * T + T],
                    in_=qkvf_t[:, c * T:(c + 1) * T], func=AF.Square),
                    wait=[mm2ev.get(c % 2)],
                    rd=[f"qkvf{c}"], wr=[f"sp{c % 2}"], tag=f"{tg}-nsq{c}")
                ev_nm = E.op("t", lambda t, bnk=bnk, coff=coff, j=j, c=c,
                             sp_t=sp_t: t.matmul(
                    out=bnk[0:1, coff:coff + T], lhsT=ones_t[:, 0:1],
                    rhs=sp_t[:, (c % 2) * T:(c % 2) * T + T],
                    start=(j == 0), stop=(j == 1)),
                    wait=[ev_sq] + (w_extra if j == 0 else []), inc=["ts"],
                    rd=[f"sp{c % 2}", "ones"], wr=[bky], tag=f"{tg}-nm{c}")
                mm2ev[c % 2] = ev_nm
            ev_nv = E.op("v", lambda v, idx=idx, bnk=bnk, coff=coff: v.tensor_copy(
                out=nrm_t[0:1, idx * T:(idx + 1) * T],
                in_=bnk[0:1, coff:coff + T]), wait=[ev_nm],
                rd=[bky], wr=["nrm"], tag=f"{tg}-ne{idx}")
        prev["b6_free"] = ev_nv
        prev["b7_free"] = ev_nv
        ev_ns = E.op("a", lambda a: a.activation(
            out=nrm_t[:], in_=nrm_t[:], func=AF.Sqrt), wait=[ev_nv],
            rd=["nrm"], wr=["nrm"], tag=f"{tg}-nsqr")
        ev_nr = E.op("v", lambda v: v.reciprocal(out=nrm_t[:], in_=nrm_t[:]),
                     wait=[ev_ns], rd=["nrm"], wr=["nrm"], tag=f"{tg}-nrc")
        prev["nrm_used"] = ev_nr

        bci = 0
        ev_bcd = {}
        evVS0 = E.ev("vs")
        for idx in range(8):
            bnk = P[2 + bci % 4]
            bky = f"P{2 + bci % 4}"
            ev_b = E.op("t", lambda t, bnk=bnk, idx=idx: t.matmul(
                out=bnk[:, 0:T], lhsT=onesr_t[0:1, :],
                rhs=nrm_t[0:1, idx * T:(idx + 1) * T],
                start=True, stop=True),
                wait=[ev_nr, ev_bcd.get(bci % 4), evVS0 if bci < 4 else None],
                inc=["ts"], rd=["nrm", "onesr"], wr=[bky], tag=f"{tg}-nb{idx}")
            ev_qh = None
            for j in range(2):
                c = idx * 2 + j
                ev_qh = E.op("v", lambda v, c=c, bnk=bnk: v.tensor_mul(
                    out=qhb_t[:, c * T:(c + 1) * T],
                    in0=qkvf_t[:, c * T:(c + 1) * T], in1=bnk[:, 0:T]),
                    wait=[ev_b, prev.get(("qhbw", c)), prev.get(("qhbwT", c))],
                    rd=[f"qkvf{c}", bky], wr=[f"qhb{c}"], tag=f"{tg}-qh{c}")
            ev_bcd[bci % 4] = ev_qh
            bci += 1
        for h in range(4):
            bnk = P[2 + bci % 4]
            bky = f"P{2 + bci % 4}"
            ev_b = E.op("t", lambda t, bnk=bnk, h=h: t.matmul(
                out=bnk[:, 0:T], lhsT=bsel_t[:, h * 128:(h + 1) * 128],
                rhs=beta_t[0:4, 0:T], start=True, stop=True),
                wait=[ev_bs, ev_bcd.get(bci % 4), evS2],
                inc=["ts"], rd=["beta", "bsel"], wr=[bky], tag=f"{tg}-bb{h}")
            ev_vb = None
            for j in range(2):
                cc = 2 * h + j
                E.op("v", lambda v, cc=cc, bnk=bnk: v.tensor_mul(
                    out=kbb_t[:, cc * T:(cc + 1) * T],
                    in0=qhb_t[:, (8 + cc) * T:(9 + cc) * T], in1=bnk[:, 0:T]),
                    wait=[ev_b, ev_bcd.get((8 + cc - 8))] if False else
                         [ev_b, prev.get(("kbbw", cc)), prev.get(("kbbwT", cc))],
                    rd=[f"qhb{8 + cc}", bky], wr=[f"kbb{cc}"], tag=f"{tg}-kb{cc}")
                ev_vb = E.op("v", lambda v, cc=cc, bnk=bnk: v.tensor_mul(
                    out=vbb_t[:, cc * T:(cc + 1) * T],
                    in0=qkvf_t[:, (16 + cc) * T:(17 + cc) * T], in1=bnk[:, 0:T]),
                    wait=[ev_b, prev.get(("vbbw", cc)), prev.get(("vbbwT", cc))],
                    rd=[f"qkvf{16 + cc}", bky], wr=[f"vbb{cc}"], tag=f"{tg}-vb{cc}")
            ev_bcd[bci % 4] = ev_vb
            bci += 1
        prev["beta_used"] = E.ev("ts")

        def fmsl(dr, c0, w, rowlen=L):
            return bass.AP(dr.tensor, c0, [[rowlen, 128], [128 * rowlen, 8], [1, w]])

        evVS1 = E.ev("vs")
        ev1 = E.dma("s", fmsl(qhatF, it * TA, TA), qhb_t[:, 0:8 * T],
                    wait=[evVS1], rd=[f"qhb{c}" for c in range(8)],
                    wr=[f"d:qF:{it}"], tag=f"{tg}-stqF", q="sdf1")
        ev2 = E.dma("s", fmsl(khatF, it * TA, TA), qhb_t[:, 8 * T:16 * T],
                    wait=[evVS1], rd=[f"qhb{c}" for c in range(8, 16)],
                    wr=[f"d:kF:{it}"], tag=f"{tg}-stkF", q="sdf2")
        ev3 = E.dma("s", fmsl(kbF, it * TA, TA), kbb_t[:],
                    wait=[evVS1], rd=[f"kbb{c}" for c in range(8)],
                    wr=[f"d:kbF:{it}"], tag=f"{tg}-stkbF", q="sdf3")
        ev4 = E.dma("s", fmsl(vF, 31 + it * TA, TA, rowlen=31 + L), vvb_t[:],
                    wait=[E.ev("vs")], rd=[f"vvb{c}" for c in range(8)],
                    wr=[f"d:vF:{it}"], tag=f"{tg}-stvF", q="sdf4")
        for c in range(8):
            prev[("qhbw", c)] = ev1
            prev[("qhbw", c + 8)] = ev2
            prev[("kbbw", c)] = ev3
            prev[("vvbw", c)] = ev4

        evVS2 = E.ev("vs")
        for hf in range(2):
            tb = 2 * it + hf

            def tquad(src_ap_fn, nchunks, banks, bkeys, rdkeys, idt, idk, tagp,
                      extra0=(), bf=False):
                done = None
                for c in range(nchunks):
                    bnk = banks[c // 4]
                    q0 = (c % 4) * 128
                    w0 = []
                    if c % 4 == 0:
                        w0 = [prev.get(("tpf", bkeys[c // 4]))] + list(extra0)
                    if bf:
                        ov = bnk[:].bitcast(BF16)[:, q0:q0 + 128]
                    else:
                        ov = bnk[:, q0:q0 + 128]
                    done = E.op("t", lambda t, c=c, ov=ov: t.matmul(
                        out=ov, lhsT=src_ap_fn(c),
                        rhs=idt, start=(c % 4 == 0), stop=(c % 4 == 3),
                        is_transpose=True),
                        wait=w0, inc=["ts"] if c % 4 == 3 else [],
                        rd=[rdkeys(c), idk], wr=[bkeys[c // 4]],
                        tag=f"{tagp}{c}")
                return done

            evt = tquad(lambda c, hf=hf: qhb_t[:, (8 + c) * T + hf * 128:
                                                 (8 + c) * T + hf * 128 + 128],
                        8, [P[2], P[3]], ["P2", "P3"],
                        lambda c: f"qhb{8 + c}", id_bf_t[:], "idb",
                        f"{tg}-tpk{hf}.", extra0=[evVS2, ev_ib], bf=True)
            ee1 = E.op("v", lambda v: v.tensor_copy(
                out=tmb_t[:, 0:512], in_=P[2][:].bitcast(BF16)[:, 0:512]),
                wait=[evt, prev.get("tmb_used0")], rd=["P2"], wr=["tmb0"],
                tag=f"{tg}-tke{hf}a")
            ee2 = E.op("v", lambda v: v.tensor_copy(
                out=tmb_t[:, 512:1024], in_=P[3][:].bitcast(BF16)[:, 0:512]),
                wait=[evt, prev.get("tmb_used0")],
                rd=["P3"], wr=["tmb0"], tag=f"{tg}-tke{hf}b")
            prev[("tpf", "P2")] = ee1
            prev[("tpf", "P3")] = ee2
            evst = E.dma("s", khT[tb], tmb_t[:, 0:1024], wait=[ee1, ee2],
                  rd=["tmb0"], wr=[f"d:khT:{tb}"], tag=f"{tg}-stkT{hf}", q="sdt1")
            prev["tmb_used0"] = evst

            evt = tquad(lambda c, hf=hf: kbb_t[:, c * T + hf * 128:
                                               c * T + hf * 128 + 128],
                        8, [P[2], P[3]], ["P2", "P3"],
                        lambda c: f"kbb{c}", id_bf_t[:], "idb",
                        f"{tg}-tpb{hf}.", bf=True)
            ee1 = E.op("v", lambda v: v.tensor_copy(
                out=tmb_t[:, 1024:1536], in_=P[2][:].bitcast(BF16)[:, 0:512]),
                wait=[evt, prev.get("tmb_used1")],
                rd=["P2"], wr=["tmb1"], tag=f"{tg}-tbe{hf}a")
            ee2 = E.op("v", lambda v: v.tensor_copy(
                out=tmb_t[:, 1536:2048], in_=P[3][:].bitcast(BF16)[:, 0:512]),
                wait=[evt, prev.get("tmb_used1")],
                rd=["P3"], wr=["tmb1"], tag=f"{tg}-tbe{hf}b")
            prev[("tpf", "P2")] = ee1
            prev[("tpf", "P3")] = ee2
            evst = E.dma("s", kbT[tb], tmb_t[:, 1024:2048], wait=[ee1, ee2],
                  rd=["tmb1"], wr=[f"d:kbT:{tb}"], tag=f"{tg}-stbT{hf}", q="sdt2")
            prev["tmb_used1"] = evst

            evt = tquad(lambda c, hf=hf: vbb_t[:, c * T + hf * 128:
                                               c * T + hf * 128 + 128],
                        8, [P[2], P[3]], ["P2", "P3"],
                        lambda c: f"vbb{c}", id_bf_t[:], "idb",
                        f"{tg}-tpv{hf}.", bf=True)
            ee1 = E.op("v", lambda v: v.tensor_copy(
                out=tmb_t[:, 2048:2560], in_=P[2][:].bitcast(BF16)[:, 0:512]),
                wait=[evt, prev.get("tmb_used2")],
                rd=["P2"], wr=["tmb2"], tag=f"{tg}-tve{hf}a")
            ee2 = E.op("v", lambda v: v.tensor_copy(
                out=tmb_t[:, 2560:3072], in_=P[3][:].bitcast(BF16)[:, 0:512]),
                wait=[evt, prev.get("tmb_used2")],
                rd=["P3"], wr=["tmb2"], tag=f"{tg}-tve{hf}b")
            prev[("tpf", "P2")] = ee1
            prev[("tpf", "P3")] = ee2
            evst = E.dma("s", vbT[tb], tmb_t[:, 2048:3072], wait=[ee1, ee2],
                  rd=["tmb2"], wr=[f"d:vbT:{tb}"], tag=f"{tg}-stvT{hf}", q="sdt3")
            prev["tmb_used2"] = evst

            evt = tquad(lambda c, hf=hf: qkvf_t[:, (16 + c) * T + hf * 128:
                                                (16 + c) * T + hf * 128 + 128],
                        8, [P[4], P[5]], ["P4", "P5"],
                        lambda c: f"qkvf{16 + c}", idf_t[:], "idf",
                        f"{tg}-tpw{hf}.", extra0=[evVS2, ev_idf])
            ee1 = E.op("v", lambda v: v.tensor_copy(
                out=tm_t[:, 0:512], in_=P[4][:, 0:512]),
                wait=[evt, prev.get("tm_used")], rd=["P4"], wr=["tm"],
                tag=f"{tg}-twe{hf}a")
            ee2 = E.op("v", lambda v: v.tensor_copy(
                out=tm_t[:, 512:1024], in_=P[5][:, 0:512]),
                wait=[evt, prev.get("tm_used")],
                rd=["P5"], wr=["tm"], tag=f"{tg}-twe{hf}b")
            prev[("tpf", "P4")] = ee1
            prev[("tpf", "P5")] = ee2
            evst = E.dma("s", vT[tb], tm_t[:, 0:1024], wait=[ee1, ee2],
                  rd=["tm"], wr=[f"d:vT:{tb}"], tag=f"{tg}-stwT{hf}", q="sdt4")
            prev["tm_used"] = evst
            
        evTS = E.ev("ts")
        evVS3 = E.ev("vs")
        for oc in range(24):
            prev[("qkvfw", oc)] = evVS3
            prev[("qkvfwT", oc)] = evTS
        for c in range(16):
            prev[("qhbwT", c)] = evTS
        for c in range(8):
            prev[("kbbwT", c)] = evTS
            prev[("vbbwT", c)] = evTS
            prev[("vbbw", c)] = evTS
        for oc in range(16):
            prev[("ghw", oc)] = evTS
        prev["x_used"] = ev_x
        prev[("pjf", 0)] = evVS3
        prev[("pjf", 1)] = evVS3
        prev[("cvf2", 0)] = evVS3
        prev[("cvf2", 1)] = evVS3
        prev[("pstw", 0)] = evTS
        prev[("pstw", 1)] = evTS

    ev_d1 = E.dma("s", bass.AP(cc_in[0].tensor, 0, [[1, 128]]), mx1_t[:, 0:1],
                  wait=[prev.get("mxw")], rd=["mx1"], wr=["d:cc0i"],
                  tag="cc0-st", q="sdcc")
    ev_cc0 = E.op("p", lambda p: p.collective_compute(
        "AllReduce", ALU.max, replica_groups=[list(range(NCORES))],
        ins=[cc_in[0][0:128]], outs=[cc_out[0][0:128]]),
        wait=[ev_d1], inc=["ps"], rd=["d:cc0i"], wr=["d:cc0o"], tag="cc0")
    ev_l1 = E.dma("s", nrm_t[0:1, 0:128],
                  bass.AP(cc_out[0].tensor, 0, [[128, 1], [1, 128]]),
                  wait=[ev_cc0, E.ev("vs")], rd=["d:cc0o", "nrm"], wr=["nrm"],
                  tag="cc0-ld", q="sdcc")
    ev_r1 = E.op("v", lambda v: v.tensor_reduce(
        out=nrm_t[0:1, 128:129], in_=nrm_t[0:1, 0:128], axis=AXX, op=ALU.max),
        wait=[ev_l1], rd=["nrm"], wr=["nrm"], tag="cc0-red")
    ev_r2 = E.op("v", lambda v: v.tensor_scalar(
        out=nrm_t[0:1, 129:130], in0=nrm_t[0:1, 128:129],
        scalar1=1.0 / 32767.0, scalar2=1e-12, op0=ALU.mult, op1=ALU.max),
        wait=[ev_r1], rd=["nrm"], wr=["nrm"], tag="cc0-s")
    ev_r3 = E.op("v", lambda v: v.reciprocal(
        out=nrm_t[0:1, 130:131], in_=nrm_t[0:1, 129:130]),
        wait=[ev_r2], rd=["nrm"], wr=["nrm"], tag="cc0-rs")
    ev_r4 = E.op("v", lambda v: v.tensor_scalar(
        out=nrm_t[0:1, 131:132], in0=nrm_t[0:1, 129:130], scalar1=-1.0,
        scalar2=None, op0=ALU.mult), wait=[ev_r3],
        rd=["nrm"], wr=["nrm"], tag="cc0-ns")
    ev_bc1 = E.op("t", lambda t: t.matmul(
        out=P[0][:, 0:3], lhsT=onesr_t[0:1, :], rhs=nrm_t[0:1, 129:132],
        start=True, stop=True), wait=[ev_r4, prev.get("b0_free")], inc=["ts"],
        rd=["nrm", "onesr"], wr=["P0"], tag="cc0-bc")
    ev_sc1 = E.op("v", lambda v: v.tensor_copy(out=sc_t[:, 0:3], in_=P[0][:, 0:3]),
                  wait=[ev_bc1], rd=["P0"], wr=["sc"], tag="cc0-ev")

    pb0.close()
    E.barrier()

    pba = ExitStack()
    qhF_t = sb("qhF_t", [128, 8 * TB], BF16, stk=pba)
    khF_t = sb("khF_t", [128, 8 * TB], BF16, stk=pba)
    kbF_t = sb("kbF_t", [128, 8 * TB], BF16, stk=pba)
    khT_t = sb("khT_t", [128, DM], BF16, stk=pba)
    kbT_t = sb("kbT_t", [128, DM], BF16, stk=pba)
    vbT_t = sb("vbT_t", [128, DM], BF16, stk=pba)
    ms_t = sb("ms_t", [128, 512], F32, stk=pba)
    mo_t = sb("mo_t", [128, 512], F32, stk=pba)
    ma_t = sb("ma_t", [128, 512], F32, stk=pba)
    idr_t = sb("idr_t", [128, 512], BF16, stk=pba)
    bstd_t = sb("bstd_t", [128, 512], BF16, stk=pba)
    bor_t = sb("bor_t", [128, 512], BF16, stk=pba)
    cstd_t = sb("cstd_t", [128, 512], BF16, stk=pba)
    cor_t = sb("cor_t", [128, 512], BF16, stk=pba)
    tor_t = sb("tor_t", [128, 512], BF16, stk=pba)
    alor_t = sb("alor_t", [128, 512], BF16, stk=pba)
    unew_t = sb("unew_t", [128, DM], F32, stk=pba)
    unb_t = sb("unb_t", [128, DM], BF16, stk=pba)
    wfm_t = sb("wfm_t", [128, 8 * TB], BF16, stk=pba)
    dout_t = sb("dout_t", [128, DM], F32, stk=pba)
    s_t = sb("s_t", [128, 4 * 512], F32, stk=pba)
    sb_t = sb("sbf_t", [128, 4 * 512], BF16, stk=pba)
    sn_t = sb("snb_t", [128, 4 * 512], BF16, stk=pba)

    ev_ms = E.dma("s", ms_t[:], maskstd, wr=["ms"], tag="ld-ms", q="sdM")
    E.dma("s", mo_t[:], maskor, wr=["mo"], tag="ld-mo", q="sdM")
    E.dma("s", ma_t[:], maskal, wr=["ma"], tag="ld-ma", q="sdM")
    E.dma("s", idr_t[:], idrep4, wr=["idr"], tag="ld-idr", q="sdM")
    evM = E.ev("sdM")
    ev_s0 = E.op("v", lambda v: v.memset(s_t[:], 0.0), wr=["S"], tag="S-init")
    ev_s1 = E.op("v", lambda v: v.memset(sb_t[:], 0.0), wr=["Sb"], tag="Sb-init")
    ev_s2 = E.op("v", lambda v: v.memset(sn_t[:], 0.0), wr=["Sn"], tag="Sn-init")

    prev = {}
    for tb in range(NTB):
        tg = f"d{tb}"
        it = tb // 2

        def fmsl2(dr, c0, w, rowlen=L):
            return bass.AP(dr.tensor, c0, [[rowlen, 128], [128 * rowlen, 8], [1, w]])

        lq = E.dma("s", qhF_t[:], fmsl2(qhatF, tb * TB, TB),
                   wait=[prev.get("qhF_used"), E.ev("sdf1")],
                   rd=[f"d:qF:{it}"], wr=["qhF"], tag=f"{tg}-lq", q="sdl1")
        lk = E.dma("s", khF_t[:], fmsl2(khatF, tb * TB, TB),
                   wait=[prev.get("khF_used"), E.ev("sdf2")],
                   rd=[f"d:kF:{it}"], wr=["khF"], tag=f"{tg}-lk", q="sdl2")
        lb = E.dma("s", kbF_t[:], fmsl2(kbF, tb * TB, TB),
                   wait=[prev.get("kbF_used"), E.ev("sdf3")],
                   rd=[f"d:kbF:{it}"], wr=["kbF"], tag=f"{tg}-lb", q="sdl3")
        lkt = E.dma("s", khT_t[:], khT[tb],
                    wait=[prev.get("khT_used"), E.ev("sdt1")],
                    rd=[f"d:khT:{tb}"], wr=["khT"], tag=f"{tg}-lkt", q="sdl4")
        lbt = E.dma("s", kbT_t[:], kbT[tb],
                    wait=[prev.get("kbT_used"), E.ev("sdt2")],
                    rd=[f"d:kbT:{tb}"], wr=["kbT"], tag=f"{tg}-lbt", q="sdl5")
        lvt = E.dma("s", vbT_t[:], vbT[tb],
                    wait=[prev.get("vbT_used"), E.ev("sdt3")],
                    rd=[f"d:vbT:{tb}"], wr=["vbT"], tag=f"{tg}-lvt", q="sdl6")

        evg = {}
        for h in range(4):
            for kc in range(2):
                c = 2 * h + kc
                E.op("t", lambda t, h=h, c=c, kc=kc: t.matmul(
                    out=P[0][:, h * 128:h * 128 + 128],
                    lhsT=kbF_t[:, c * TB:(c + 1) * TB],
                    rhs=khF_t[:, c * TB:(c + 1) * TB],
                    start=(h == 0 and kc == 0), stop=(h == 3 and kc == 1)),
                    wait=[lb, lk, prev.get("P0f")] if (h == 0 and kc == 0) else [],
                    inc=["ts"] if (h == 3 and kc == 1) else [],
                    rd=["kbF", "khF"], wr=["P0"], tag=f"{tg}-g0.{h}.{kc}")
                evg[0] = E.op("t", lambda t, h=h, c=c, kc=kc: t.matmul(
                    out=P[1][:, h * 128:h * 128 + 128],
                    lhsT=khF_t[:, c * TB:(c + 1) * TB],
                    rhs=kbF_t[:, c * TB:(c + 1) * TB],
                    start=(h == 0 and kc == 0), stop=(h == 3 and kc == 1)),
                    wait=[lb, lk, prev.get("P1f")] if (h == 0 and kc == 0) else [],
                    inc=["ts"] if (h == 3 and kc == 1) else [],
                    rd=["kbF", "khF"], wr=["P1"], tag=f"{tg}-g1.{h}.{kc}")
        evgram = evg[0]
        eb1 = E.op("v", lambda v: v.tensor_mul(
            out=bstd_t[:], in0=P[0][:, 0:512], in1=ms_t[:]),
            wait=[evgram, evM, prev.get("bstd_used")],
            rd=["P0", "ms"], wr=["bstd"], tag=f"{tg}-bstd")
        prev["P0f"] = eb1
        eb2 = E.op("v", lambda v: v.tensor_mul(
            out=bor_t[:], in0=P[1][:, 0:512], in1=mo_t[:]),
            wait=[evgram, prev.get("bor_used")],
            rd=["P1", "mo"], wr=["bor"], tag=f"{tg}-bor")
        prev["P1f"] = eb2
        et0 = E.op("v", lambda v: v.tensor_tensor(
            out=tor_t[:], in0=bor_t[:], in1=idr_t[:], op=ALU.add),
            wait=[eb2, prev.get("tor_used")],
            rd=["bor", "idr"], wr=["tor"], tag=f"{tg}-t0")
        ecs = E.op("v", lambda v: v.tensor_copy(out=cstd_t[:], in_=bstd_t[:]),
                   wait=[eb1, prev.get("cstd_used")],
                   rd=["bstd"], wr=["cstd"], tag=f"{tg}-cs0")
        eco = E.op("v", lambda v: v.tensor_copy(out=cor_t[:], in_=bor_t[:]),
                   wait=[eb2, prev.get("cor_used")],
                   rd=["bor"], wr=["cor"], tag=f"{tg}-co0")
        for j in range(6):
            evs = evo = evt2 = None
            for h in range(4):
                evs = E.op("t", lambda t, h=h: t.matmul(
                    out=P[2][:, h * 128:h * 128 + 128],
                    lhsT=cor_t[:, h * 128:h * 128 + 128],
                    rhs=cstd_t[:, h * 128:h * 128 + 128],
                    start=(h == 0), stop=(h == 3)),
                    wait=[ecs, eco, prev.get("P2f")] if h == 0 else [],
                    inc=["ts"] if h == 3 else [],
                    rd=["cor", "cstd"], wr=["P2"], tag=f"{tg}-ts{j}.{h}")
                evo = E.op("t", lambda t, h=h: t.matmul(
                    out=P[3][:, h * 128:h * 128 + 128],
                    lhsT=cstd_t[:, h * 128:h * 128 + 128],
                    rhs=cor_t[:, h * 128:h * 128 + 128],
                    start=(h == 0), stop=(h == 3)),
                    wait=[prev.get("P3f")] if h == 0 else [],
                    inc=["ts"] if h == 3 else [],
                    rd=["cstd", "cor"], wr=["P3"], tag=f"{tg}-to{j}.{h}")
            ecs = E.op("v", lambda v: v.tensor_copy(out=cstd_t[:], in_=P[2][:, 0:512]),
                       wait=[evs, evo], rd=["P2"], wr=["cstd"], tag=f"{tg}-cse{j}")
            prev["P2f"] = ecs
            eco = E.op("v", lambda v: v.tensor_copy(out=cor_t[:], in_=P[3][:, 0:512]),
                       wait=[evo], rd=["P3"], wr=["cor"], tag=f"{tg}-coe{j}")
            prev["P3f"] = eco
            for h in range(4):
                evt2 = E.op("t", lambda t, h=h: t.matmul(
                    out=P[4][:, h * 128:h * 128 + 128],
                    lhsT=cstd_t[:, h * 128:h * 128 + 128],
                    rhs=tor_t[:, h * 128:h * 128 + 128],
                    start=(h == 0), stop=(h == 3)),
                    wait=[ecs, et0, prev.get("P4f")] if h == 0 else [],
                    inc=["ts"] if h == 3 else [],
                    rd=["cstd", "tor"], wr=["P4"], tag=f"{tg}-ta{j}.{h}")
            et0 = E.op("v", lambda v: v.tensor_tensor(
                out=tor_t[:], in0=P[4][:, 0:512], in1=tor_t[:], op=ALU.add),
                wait=[evt2], rd=["P4", "tor"], wr=["tor"], tag=f"{tg}-tae{j}")
            prev["P4f"] = et0

        eva = None
        for h in range(4):
            for kc in range(2):
                c = 2 * h + kc
                eva = E.op("t", lambda t, h=h, c=c, kc=kc: t.matmul(
                    out=P[5][:, h * 128:h * 128 + 128],
                    lhsT=khF_t[:, c * TB:(c + 1) * TB],
                    rhs=qhF_t[:, c * TB:(c + 1) * TB],
                    start=(h == 0 and kc == 0), stop=(h == 3 and kc == 1)),
                    wait=[lq, prev.get("P5f")] if (h == 0 and kc == 0) else [],
                    inc=["ts"] if (h == 3 and kc == 1) else [],
                    rd=["khF", "qhF"], wr=["P5"], tag=f"{tg}-al.{h}.{kc}")
        eal = E.op("v", lambda v: v.tensor_mul(
            out=alor_t[:], in0=P[5][:, 0:512], in1=ma_t[:]),
            wait=[eva, prev.get("alor_used")],
            rd=["P5", "ma"], wr=["alor"], tag=f"{tg}-ale")
        prev["P5f"] = eal

        UB = {0: (6, 0), 1: (6, 256), 2: (7, 0), 3: (7, 256)}
        ev_un = {}
        for h in range(4):
            bi, co = UB[h]
            E.op("t", lambda t, h=h, bi=bi, co=co: t.matmul(
                out=P[bi][:, co:co + 256],
                lhsT=tor_t[:, h * 128:h * 128 + 128],
                rhs=vbT_t[:, h * 256:(h + 1) * 256],
                start=(h % 2 == 0), stop=False),
                wait=[et0, lvt] + ([prev.get(f"P{bi}f")] if co == 0 else []),
                rd=["tor", "vbT"], wr=[f"P{bi}"], tag=f"{tg}-u{h}")
            eww = None
            for kc in range(2):
                bq = P[2 + kc]
                eww = E.op("t", lambda t, h=h, kc=kc, bq=bq: t.matmul(
                    out=bq[:, h * 128:h * 128 + 128],
                    lhsT=kbT_t[:, h * 256 + kc * 128:h * 256 + kc * 128 + 128],
                    rhs=tor_t[:, h * 128:h * 128 + 128],
                    start=True, stop=True),
                    wait=[lbt, et0, prev.get(f"wP{2 + kc}")],
                    inc=["ts"], rd=["kbT", "tor"], wr=[f"P{2 + kc}"],
                    tag=f"{tg}-w{h}.{kc}")
        eww1 = E.op("v", lambda v: v.tensor_copy(
            out=wfm_t[:, 0:4 * TB], in_=bass.AP(
                P[2][:].tensor, P[2][:].offset, [[512, 128], [1, 512]])),
            wait=[E.ev("ts")], rd=["P2"], wr=["wfm0"], tag=f"{tg}-we1")
        eww2 = E.op("v", lambda v: v.tensor_copy(
            out=wfm_t[:, 4 * TB:8 * TB], in_=bass.AP(
                P[3][:].tensor, P[3][:].offset, [[512, 128], [1, 512]])),
            wait=[E.ev("ts")], rd=["P3"], wr=["wfm1"], tag=f"{tg}-we2")
        prev["wP2"] = eww1
        prev["wP3"] = eww2
        ev_uc = None
        for h in range(4):
            bi, co = UB[h]
            for kc in range(2):
                ev_uc = E.op("t", lambda t, h=h, kc=kc, bi=bi, co=co: t.matmul(
                    out=P[bi][:, co:co + 256],
                    lhsT=wfm_t[:, kc * 4 * TB + h * TB:kc * 4 * TB + (h + 1) * TB],
                    rhs=sn_t[:, h * 512 + kc * 256:h * 512 + kc * 256 + 256],
                    start=False, stop=(kc == 1 and h % 2 == 1)),
                    wait=[eww1 if kc == 0 else eww2, ev_s2,
                          prev.get("Sn_rd_guard")],
                    inc=["ts"] if kc == 1 else [],
                    rd=[f"wfm{kc}", "Sn"], wr=[f"P{bi}"], tag=f"{tg}-uc{h}.{kc}")
            ev_un[h] = ev_uc
        eu1 = E.op("v", lambda v: v.tensor_copy(out=unew_t[:, 0:512],
                                                in_=P[6][:, 0:512]),
                   wait=[ev_un[1], prev.get("unew_used")],
                   rd=["P6"], wr=["unew"], tag=f"{tg}-ue1")
        eu2 = E.op("v", lambda v: v.tensor_copy(out=unew_t[:, 512:1024],
                                                in_=P[7][:, 0:512]),
                   wait=[ev_un[3]], rd=["P7"], wr=["unew"], tag=f"{tg}-ue2")
        prev["P6f"] = eu1
        prev["P7f"] = eu2
        eub = E.op("v", lambda v: v.tensor_copy(out=unb_t[:], in_=unew_t[:]),
                   wait=[eu2, prev.get("unb_used")],
                   rd=["unew"], wr=["unb"], tag=f"{tg}-ub")

        OB = {0: (0, 0), 1: (0, 256), 2: (1, 0), 3: (1, 256)}
        ev_o = {}
        for h in range(4):
            bi, co = OB[h]
            for kc in range(2):
                c = 2 * h + kc
                E.op("t", lambda t, h=h, c=c, kc=kc, bi=bi, co=co: t.matmul(
                    out=P[bi][:, co:co + 256],
                    lhsT=qhF_t[:, c * TB:(c + 1) * TB],
                    rhs=sb_t[:, h * 512 + kc * 256:h * 512 + kc * 256 + 256],
                    start=(kc == 0 and h % 2 == 0), stop=False),
                    wait=[lq, ev_s1, eb1 if bi == 0 else eb2,
                          prev.get("S_rd_guard")] if kc == 0 else [],
                    rd=["qhF", "Sb"], wr=[f"P{bi}"], tag=f"{tg}-oq{h}.{kc}")
            ev_o[h] = E.op("t", lambda t, h=h, bi=bi, co=co: t.matmul(
                out=P[bi][:, co:co + 256],
                lhsT=alor_t[:, h * 128:h * 128 + 128],
                rhs=unb_t[:, h * 256:(h + 1) * 256],
                start=False, stop=(h % 2 == 1)),
                wait=[eal, eub], inc=["ts"],
                rd=["alor", "unb"], wr=[f"P{bi}"], tag=f"{tg}-oa{h}")
        eo1 = E.op("v", lambda v: v.tensor_copy(out=dout_t[:, 0:512],
                                                in_=P[0][:, 0:512]),
                   wait=[ev_o[1], prev.get("dout_used")],
                   rd=["P0"], wr=["dout"], tag=f"{tg}-oe1")
        eo2 = E.op("v", lambda v: v.tensor_copy(out=dout_t[:, 512:1024],
                                                in_=P[1][:, 0:512]),
                   wait=[ev_o[3]], rd=["P1"], wr=["dout"], tag=f"{tg}-oe2")
        prev["P0f"] = eo1
        prev["P1f"] = eo2
        ev_do = E.dma("s", deltaT[tb], dout_t[:], wait=[eo2],
                      rd=["dout"], wr=[f"d:dT:{tb}"], tag=f"{tg}-std", q="sdd2")
        prev["dout_used"] = ev_do

        ev_ds = {}
        for h in range(4):
            for kc in range(2):
                bq = P[2 + h]
                ev_ds[(h, kc)] = E.op("t", lambda t, h=h, kc=kc, bq=bq: t.matmul(
                    out=bq[:, kc * 256:kc * 256 + 256],
                    lhsT=khT_t[:, h * 256 + kc * 128:h * 256 + kc * 128 + 128],
                    rhs=unb_t[:, h * 256:(h + 1) * 256],
                    start=(kc == 0), stop=(kc == 1)),
                    wait=[lkt, eub, eww1 if h <= 1 else eww2,
                          prev.get(f"dsP{2 + h}")] if kc == 0 else [],
                    inc=["ts"] if kc == 1 else [],
                    rd=["khT", "unb"], wr=[f"P{2 + h}"], tag=f"{tg}-ds{h}.{kc}")
        ev_sa = None
        for h in range(4):
            ev_sa = E.op("v", lambda v, h=h: v.tensor_tensor(
                out=s_t[:, h * 512:(h + 1) * 512],
                in0=bass.AP(P[2 + h][:].tensor, P[2 + h][:].offset,
                            [[512, 128], [1, 512]]),
                in1=s_t[:, h * 512:(h + 1) * 512], op=ALU.add),
                wait=[ev_ds[(h, 1)], ev_s0, E.ev("ts")],
                rd=[f"P{2 + h}", "S"], wr=["S"], tag=f"{tg}-sa{h}")
            prev[f"dsP{2 + h}"] = ev_sa
        esb = E.op("v", lambda v: v.tensor_copy(out=sb_t[:], in_=s_t[:]),
                   wait=[ev_sa, E.ev("ts")], rd=["S"], wr=["Sb"],
                   tag=f"{tg}-sbc")
        esn = E.op("v", lambda v: v.tensor_scalar(
            out=sn_t[:], in0=s_t[:], scalar1=-1.0, scalar2=None, op0=ALU.mult),
            wait=[ev_sa, E.ev("ts")], rd=["S"], wr=["Sn"], tag=f"{tg}-snc")
        prev["S_rd_guard"] = esb
        prev["Sn_rd_guard"] = esn
        prev["Sb_w_guard"] = E.ev("ts")
        prev["unew_used"] = E.ev("ts")
        prev["unb_used"] = E.ev("ts")
        prev["qhF_used"] = E.ev("ts")
        prev["khF_used"] = E.ev("ts")
        prev["kbF_used"] = E.ev("ts")
        prev["khT_used"] = E.ev("ts")
        prev["kbT_used"] = E.ev("ts")
        prev["vbT_used"] = E.ev("ts")
        prev["bstd_used"] = E.ev("ts")
        prev["bor_used"] = E.ev("ts")
        prev["tor_used"] = E.ev("ts")
        prev["cstd_used"] = E.ev("ts")
        prev["cor_used"] = E.ev("ts")
        prev["alor_used"] = E.ev("ts")
        prev["wP2"] = ev_sa
        prev["wP3"] = ev_sa

    pba.close()
    E.barrier()

    pbb = ExitStack()
    fircd_t = sb("fircd_t", [128, 304 * 128], BF16, stk=pbb)
    opw_t = sb("opw_t", [128, 8 * DM], BF16, stk=pbb)
    onw_t = sb("onw_t", [128, DM], F32, stk=pbb)
    alr_t = sb("alr_t", [128, 16], F32, stk=pbb)
    vfb_t = sb("vfb_t", [128, 8 * (TA + 31)], BF16, stk=pbb)
    firf_t = sb("firf_t", [128, 16 * TA], F32, stk=pbb)
    vT_t = sb("vT_t", [128, DM], F32, stk=pbb)
    dT_t = sb("dT_t", [128, DM], F32, stk=pbb)
    gT_t = sb("gT_t", [128, 16], F32, stk=pbb)
    fsT_t = sb("fsT_t", [128, DM], F32, stk=pbb)
    flT_t = sb("flT_t", [128, DM], F32, stk=pbb)
    sqs_t = sb("sqs_t", [128, TA], F32, stk=pbb)
    s1c_t = sb("s1c_t", [128, 16], F32, stk=pbb)
    s2c_t = sb("s2c_t", [128, 16], F32, stk=pbb)
    mxc_t = sb("mxc_t", [128, 16], F32, stk=pbb)
    st1_t = sb("st1_t", [128, 16], F32, stk=pbb)
    st2_t = sb("st2_t", [128, 16], F32, stk=pbb)
    mx4_t = sb("mx4_t", [128, 4], F32, stk=pbb)
    wts_t = sb("wts_t", [128, 16], F32, stk=pbb)
    oacc_t = sb("oacc_t", [128, DM], F32, stk=pbb)
    rr_t = sb("rr_t", [128, 4], F32, stk=pbb)
    obf_t = sb("obf_t", [128, DM], BF16, stk=pbb)
    ofm_t = sb("ofm_t", [128, 8 * TB], BF16, stk=pbb)
    a1o_t = sb("a1o_t", [128, 8 * TA], F32, stk=pbb)

    for fci in range(4):
        E.dma("p", fircd_t[:, fci * 76 * 128:(fci + 1) * 76 * 128],
              bass.AP(fircd.tensor, fci * 76 * 128 * 128,
                      [[128, 128], [128 * 128, 76], [1, 128]]),
              wr=["fircd"], tag=f"ld-fircd{fci}", q="pdF")
    evOP = E.dma("p", opw_t[:], oprojT, wr=["opw"], tag="ld-opw", q="pdF")
    evF = evOP = E.ev("pdF")
    E.dma("s", onw_t[:], onw_rep, wr=["onw"], tag="ld-onw", q="sdM2")
    E.dma("s", alr_t[:], alpha_rep, wr=["alr"], tag="ld-alr", q="sdM2")
    evM2 = E.ev("sdM2")

    def fmsl3(dr, c0, w, rowlen=L):
        return bass.AP(dr.tensor, c0, [[rowlen, 128], [128 * rowlen, 8], [1, w]])

    prev = {}
    adst = yout if mode == "attn1" else attn1_s
    for it in range(NTA):
        tg = f"e{it}"
        lv = E.dma("s", vfb_t[:], fmsl3(vF, it * TA, TA + 31, rowlen=31 + L),
                   wait=[prev.get("vfb_used"), E.ev("sdf4")],
                   rd=[f"d:vF:{it}", f"d:vF:{max(0, it - 1)}", "d:vFz"],
                   wr=["vfb"], tag=f"{tg}-lv", q="sdv")
        ev_fs = {}
        ev_fl = {}
        for ch in range(8):
            bs_ = P[ch % 2]
            bl_ = P[2 + ch % 2]
            evk = None
            for j in range(7):
                off = 25 + j
                evk = E.op("t", lambda t, ch=ch, j=j, off=off, bs_=bs_: t.matmul(
                    out=bs_[:, 0:TA],
                    lhsT=fircd_t[:, (ch * 38 + j) * 128:(ch * 38 + j) * 128 + 128],
                    rhs=vfb_t[:, ch * (TA + 31) + off:ch * (TA + 31) + off + TA],
                    start=(j == 0), stop=(j == 6)),
                    wait=[lv, evF, prev.get(("fsf", ch % 2))] if j == 0 else [],
                    inc=["ts"] if j == 6 else [],
                    rd=["fircd", "vfb"], wr=[f"P{ch % 2}"], tag=f"{tg}-fs{ch}.{j}")
            ev_fs[ch] = evk
            evk = None
            for j in range(31):
                off = 1 + j
                evk = E.op("t", lambda t, ch=ch, j=j, off=off, bl_=bl_: t.matmul(
                    out=bl_[:, 0:TA],
                    lhsT=fircd_t[:, (ch * 38 + 7 + j) * 128:(ch * 38 + 7 + j) * 128 + 128],
                    rhs=vfb_t[:, ch * (TA + 31) + off:ch * (TA + 31) + off + TA],
                    start=(j == 0), stop=(j == 30)),
                    wait=[lv, prev.get(("flf", ch % 2))] if j == 0 else [],
                    inc=["ts"] if j == 30 else [],
                    rd=["fircd", "vfb"], wr=[f"P{2 + ch % 2}"], tag=f"{tg}-fl{ch}.{j}")
            ev_fl[ch] = evk
            es_ = E.op("a", lambda a, ch=ch, bs_=bs_: a.activation(
                out=firf_t[:, ch * TA:(ch + 1) * TA], in_=bs_[:, 0:TA],
                func=AF.Copy), wait=[ev_fs[ch], prev.get(("firfw", ch))],
                rd=[f"P{ch % 2}"], wr=[f"firf{ch}"], tag=f"{tg}-fse{ch}")
            prev[("fsf", ch % 2)] = es_
            el_ = E.op("a", lambda a, ch=ch, bl_=bl_: a.activation(
                out=firf_t[:, (8 + ch) * TA:(9 + ch) * TA], in_=bl_[:, 0:TA],
                func=AF.Copy), wait=[ev_fl[ch], prev.get(("firfw", 8 + ch))],
                rd=[f"P{2 + ch % 2}"], wr=[f"firf{8 + ch}"], tag=f"{tg}-fle{ch}")
            prev[("flf", ch % 2)] = el_

        for hf in range(2):
            tb = 2 * it + hf
            tgh = f"{tg}.{hf}"
            lvt = E.dma("s", vT_t[:], vT[tb],
                        wait=[prev.get("vT_used"), E.ev("sdt4")],
                        rd=[f"d:vT:{tb}"], wr=["vTl"], tag=f"{tgh}-lvt", q="sdl1")
            ldt = E.dma("s", dT_t[:], deltaT[tb],
                        wait=[prev.get("dT_used"), E.ev("sdd2")],
                        rd=[f"d:dT:{tb}"], wr=["dTl"], tag=f"{tgh}-ldt", q="sdl2")
            lgt = E.dma("s", gT_t[:], gateT[tb],
                        wait=[prev.get("gT_used"), E.ev("sdg0"), E.ev("sdg1")],
                        rd=[f"d:gT:{tb}"], wr=["gTl"], tag=f"{tgh}-lgt", q="sdl3")
            for (src_base, dstb, dk) in ((0, fsT_t, "fsT"), (8, flT_t, "flT")):
                evt = None
                for c in range(8):
                    bnk = P[4 + c // 4]
                    q0 = (c % 4) * 128
                    w0 = []
                    if c % 4 == 0:
                        w0 = [prev.get(("tpf5", 4 + c // 4)),
                              E.ev("as")]
                    evt = E.op("t", lambda t, c=c, bnk=bnk, q0=q0,
                               src_base=src_base, hf=hf: t.matmul(
                        out=bnk[:, q0:q0 + 128],
                        lhsT=firf_t[:, (src_base + c) * TA + hf * 128:
                                    (src_base + c) * TA + hf * 128 + 128],
                        rhs=idf_t[:], start=(c % 4 == 0), stop=(c % 4 == 3),
                        is_transpose=True),
                        wait=w0, inc=["ts"] if c % 4 == 3 else [],
                        rd=[f"firf{src_base + c}", "idf"], wr=[f"P{4 + c // 4}"],
                        tag=f"{tgh}-tp{dk}{c}")
                e1 = E.op("v", lambda v, dstb=dstb: v.tensor_copy(
                    out=dstb[:, 0:512], in_=P[4][:, 0:512]),
                    wait=[evt, prev.get((dk, "used"))],
                    rd=["P4"], wr=[dk], tag=f"{tgh}-te{dk}a")
                e2 = E.op("v", lambda v, dstb=dstb: v.tensor_copy(
                    out=dstb[:, 512:1024], in_=P[5][:, 0:512]),
                    wait=[evt], rd=["P5"], wr=[dk], tag=f"{tgh}-te{dk}b")
                prev[("tpf5", 4)] = e1
                prev[("tpf5", 5)] = e2
            ev_tm = e2

            BRS = {0: (fsT_t, "fsT", [ev_tm]), 1: (flT_t, "flT", [ev_tm]),
                   2: (dT_t, "dTl", [ldt]), 3: (vT_t, "vTl", [lvt])}
            ev_s1 = ev_s2 = ev_mx = None
            for h in range(4):
                for br in range(4):
                    bt, bk_, wv = BRS[br]
                    col = br * 4 + h
                    ev_s1 = E.op("a", lambda a, bt=bt, h=h, col=col: a.activation(
                        out=sqs_t[:, 0:TA][:, 0:256], in_=bt[:, h * 256:(h + 1) * 256],
                        func=AF.Identity, accum_out=s1c_t[:, col:col + 1]),
                        wait=wv + [prev.get("sqs0")],
                        rd=[bk_], wr=["sqs", "s1c"], tag=f"{tgh}-s1.{col}")
                    prev["sqs0"] = ev_s1
                    ev_s2 = E.op("a", lambda a, bt=bt, h=h, col=col: a.activation(
                        out=sqs_t[:, 0:256], in_=bt[:, h * 256:(h + 1) * 256],
                        func=AF.Square, accum_out=s2c_t[:, col:col + 1]),
                        wait=[ev_s1], rd=[bk_], wr=["sqs", "s2c"],
                        tag=f"{tgh}-s2.{col}")
                    prev["sqs0"] = ev_s2
                    ev_mx = E.op("v", lambda v, bt=bt, h=h, col=col: v.tensor_reduce(
                        out=mxc_t[:, col:col + 1], in_=bt[:, h * 256:(h + 1) * 256],
                        axis=AXX, op=ALU.max),
                        wait=wv + [prev.get("mxc_used")],
                        rd=[bk_], wr=["mxc"], tag=f"{tgh}-mx.{col}")
            t1 = E.op("v", lambda v: v.tensor_scalar(
                out=st1_t[:], in0=s1c_t[:], scalar1=1.0 / 256, scalar2=None,
                op0=ALU.mult), wait=[ev_s2, prev.get("st1_used")],
                rd=["s1c"], wr=["st1"], tag=f"{tgh}-mu")
            t2 = E.op("v", lambda v: v.tensor_mul(
                out=st2_t[:], in0=s1c_t[:], in1=st1_t[:]),
                wait=[t1, prev.get("st2_used")],
                rd=["s1c", "st1"], wr=["st2"], tag=f"{tgh}-s1mu")
            t3 = E.op("v", lambda v: v.tensor_sub(
                out=st2_t[:], in0=s2c_t[:], in1=st2_t[:]), wait=[t2],
                rd=["s2c", "st2"], wr=["st2"], tag=f"{tgh}-var0")
            t4 = E.op("v", lambda v: v.tensor_scalar(
                out=st2_t[:], in0=st2_t[:], scalar1=1.0 / 255, scalar2=0.0,
                op0=ALU.mult, op1=ALU.max), wait=[t3],
                rd=["st2"], wr=["st2"], tag=f"{tgh}-var")
            t5 = E.op("a", lambda a: a.activation(
                out=st2_t[:], in_=st2_t[:], func=AF.Sqrt), wait=[t4],
                rd=["st2"], wr=["st2"], tag=f"{tgh}-std")
            t6 = E.op("v", lambda v: v.tensor_add(
                out=st1_t[:], in0=st1_t[:], in1=st2_t[:]), wait=[t5],
                rd=["st1", "st2"], wr=["st1"], tag=f"{tgh}-ms")
            t7 = E.op("v", lambda v: v.tensor_add(
                out=st1_t[:], in0=st1_t[:], in1=mxc_t[:]), wait=[t6, ev_mx],
                rd=["st1", "mxc"], wr=["st1"], tag=f"{tgh}-msx")
            prev["mxc_used"] = t7
            t8 = E.op("v", lambda v: v.tensor_scalar(
                out=st1_t[:], in0=st1_t[:], scalar1=1.0 / 3, scalar2=None,
                op0=ALU.mult), wait=[t7], rd=["st1"], wr=["st1"],
                tag=f"{tgh}-st3")
            t9 = E.op("v", lambda v: v.tensor_mul(
                out=st1_t[:], in0=st1_t[:], in1=alr_t[:]), wait=[t8, evM2],
                rd=["st1", "alr"], wr=["st1"], tag=f"{tgh}-al")
            t10 = E.op("v", lambda v: v.tensor_add(
                out=st1_t[:], in0=st1_t[:], in1=gT_t[:]), wait=[t9, lgt],
                rd=["st1", "gTl"], wr=["st1"], tag=f"{tgh}-lg")
            t11 = E.op("v", lambda v: v.tensor_reduce(
                out=mx4_t[:], in_=sap(st1_t, 0, [[1, 4], [4, 4]]),
                axis=AXX, op=ALU.max), wait=[t10, prev.get("mx4_used")],
                rd=["st1"], wr=["mx4"], tag=f"{tgh}-sm1")
            t12 = E.op("v", lambda v: v.tensor_tensor(
                out=sap(st1_t, 0, [[4, 4], [1, 4]]),
                in0=sap(st1_t, 0, [[4, 4], [1, 4]]),
                in1=sap(mx4_t, 0, [[0, 4], [1, 4]]), op=ALU.subtract),
                wait=[t11], rd=["st1", "mx4"], wr=["st1"], tag=f"{tgh}-sm2")
            t13 = E.op("a", lambda a: a.activation(
                out=st1_t[:], in_=st1_t[:], func=AF.Exp), wait=[t12],
                rd=["st1"], wr=["st1"], tag=f"{tgh}-sm3")
            t14 = E.op("v", lambda v: v.tensor_reduce(
                out=mx4_t[:], in_=sap(st1_t, 0, [[1, 4], [4, 4]]),
                axis=AXX, op=ALU.add), wait=[t13],
                rd=["st1"], wr=["mx4"], tag=f"{tgh}-sm4")
            t15 = E.op("v", lambda v: v.reciprocal(
                out=mx4_t[:], in_=mx4_t[:]), wait=[t14],
                rd=["mx4"], wr=["mx4"], tag=f"{tgh}-sm5")
            t16 = E.op("v", lambda v: v.tensor_tensor(
                out=sap(wts_t, 0, [[4, 4], [1, 4]]),
                in0=sap(st1_t, 0, [[4, 4], [1, 4]]),
                in1=sap(mx4_t, 0, [[0, 4], [1, 4]]), op=ALU.mult),
                wait=[t15, prev.get("wts_used")],
                rd=["st1", "mx4"], wr=["wts"], tag=f"{tgh}-sm6")
            prev["mx4_used"] = t16
            prev["st1_used"] = t16
            prev["st2_used"] = t5
            ev_cb = None
            for h in range(4):
                for br in range(4):
                    bt, bk_, wv = BRS[br]
                    col = br * 4 + h
                    if br == 0:
                        ev_cb = E.op("v", lambda v, bt=bt, h=h, col=col: v.tensor_scalar(
                            out=oacc_t[:, h * 256:(h + 1) * 256],
                            in0=bt[:, h * 256:(h + 1) * 256],
                            scalar1=wts_t[:, col:col + 1], scalar2=None,
                            op0=ALU.mult),
                            wait=[t16, prev.get("oacc_used")],
                            rd=[bk_, "wts"], wr=["oacc"], tag=f"{tgh}-cb{col}")
                    else:
                        ev_cb = E.op("v", lambda v, bt=bt, h=h, col=col: v.scalar_tensor_tensor(
                            out=oacc_t[:, h * 256:(h + 1) * 256],
                            in0=bt[:, h * 256:(h + 1) * 256],
                            scalar=wts_t[:, col:col + 1],
                            in1=oacc_t[:, h * 256:(h + 1) * 256],
                            op0=ALU.mult, op1=ALU.add),
                            wait=[ev_cb], rd=[bk_, "wts", "oacc"], wr=["oacc"],
                            tag=f"{tgh}-cb{col}")
            prev["fsT_used"] = ev_cb
            prev["flT_used"] = ev_cb
            prev["vT_used"] = ev_cb
            prev["dT_used"] = ev_cb
            prev[("fsT", "used")] = ev_cb
            prev[("flT", "used")] = ev_cb
            ev_rr = None
            for h in range(4):
                ev_q = E.op("a", lambda a, h=h: a.activation(
                    out=sqs_t[:, 0:256], in_=oacc_t[:, h * 256:(h + 1) * 256],
                    func=AF.Square, accum_out=rr_t[:, h:h + 1]),
                    wait=[ev_cb, prev.get("sqs0"), prev.get("rr_used")],
                    rd=["oacc"], wr=["sqs", "rr"], tag=f"{tgh}-os{h}")
                prev["sqs0"] = ev_q
            ev_r1 = E.op("a", lambda a: a.activation(
                out=rr_t[:], in_=rr_t[:], func=AF.Sqrt, scale=1.0 / 256,
                bias=eps_t[:, 0:1]), wait=[ev_q],
                rd=["rr", "eps"], wr=["rr"], tag=f"{tgh}-or")
            ev_r2 = E.op("v", lambda v: v.reciprocal(out=rr_t[:], in_=rr_t[:]),
                         wait=[ev_r1], rd=["rr"], wr=["rr"], tag=f"{tgh}-orc")
            prev["rr_used"] = ev_r2
            ev_on = None
            for h in range(4):
                ev_on = E.op("v", lambda v, h=h: v.tensor_scalar(
                    out=oacc_t[:, h * 256:(h + 1) * 256],
                    in0=oacc_t[:, h * 256:(h + 1) * 256],
                    scalar1=rr_t[:, h:h + 1], scalar2=None, op0=ALU.mult),
                    wait=[ev_r2], rd=["oacc", "rr"], wr=["oacc"],
                    tag=f"{tgh}-on{h}")
            prev["oacc_used"] = ev_on
            ev_ob = E.op("v", lambda v: v.tensor_mul(
                out=obf_t[:], in0=oacc_t[:], in1=onw_t[:]),
                wait=[ev_on, evM2, prev.get("obf_used")],
                rd=["oacc", "onw"], wr=["obf"], tag=f"{tgh}-ob")
            evt = None
            for c in range(8):
                bnk = P[4 + c // 4]
                q0 = (c % 4) * 128
                w0 = []
                if c % 4 == 0:
                    w0 = [ev_ob, prev.get(("tpf5", 4 + c // 4))]
                evt = E.op("t", lambda t, c=c, bnk=bnk, q0=q0: t.matmul(
                    out=bnk[:].bitcast(BF16)[:, q0:q0 + 128],
                    lhsT=obf_t[:, c * 128:(c + 1) * 128],
                    rhs=id_bf_t[:], start=(c % 4 == 0), stop=(c % 4 == 3),
                    is_transpose=True),
                    wait=w0, inc=["ts"] if c % 4 == 3 else [],
                    rd=["obf", "idb"], wr=[f"P{4 + c // 4}"],
                    tag=f"{tgh}-otp{c}")
            e1 = E.op("v", lambda v: v.tensor_copy(
                out=ofm_t[:, 0:512], in_=P[4][:].bitcast(BF16)[:, 0:512]),
                wait=[evt, prev.get("ofm_used")], rd=["P4"], wr=["ofm"],
                tag=f"{tgh}-oea")
            e2 = E.op("v", lambda v: v.tensor_copy(
                out=ofm_t[:, 512:1024], in_=P[5][:].bitcast(BF16)[:, 0:512]),
                wait=[evt], rd=["P5"], wr=["ofm"], tag=f"{tgh}-oeb")
            prev[("tpf5", 4)] = e1
            prev[("tpf5", 5)] = e2
            prev["obf_used"] = evt
            ev_pr = None
            for oc in range(8):
                bnk = P[6 + oc // 4]
                q0 = (oc % 4) * 128
                for kc in range(8):
                    w0 = []
                    if oc == 0 and kc == 0:
                        w0 = [e2, evOP, prev.get("P6p")]
                    elif oc == 4 and kc == 0:
                        w0 = [prev.get("P7p")]
                    ev_pr = E.op("t", lambda t, oc=oc, kc=kc, bnk=bnk, q0=q0: t.matmul(
                        out=bnk[:, q0:q0 + 128],
                        lhsT=opw_t[:, kc * DM + oc * 128:kc * DM + oc * 128 + 128],
                        rhs=ofm_t[:, kc * 128:(kc + 1) * 128],
                        start=(oc % 4 == 0 and kc == 0),
                        stop=(oc % 4 == 3 and kc == 7)),
                        wait=w0, inc=["ts"] if (oc % 4 == 3 and kc == 7) else [],
                        rd=["opw", "ofm"], wr=[f"P{6 + oc // 4}"],
                        tag=f"{tgh}-pr{oc}.{kc}")
            ev_pe1 = E.op("v", lambda v, hf=hf: v.tensor_copy(
                out=sap(a1o_t, hf * 128, [[256, 4], [1, 128]]),
                in_=P[6][:, 0:512]),
                wait=[ev_pr, prev.get("a1o_used")], rd=["P6"], wr=["a1o"],
                tag=f"{tgh}-pe1")
            ev_pe2 = E.op("v", lambda v, hf=hf: v.tensor_copy(
                out=sap(a1o_t, 4 * 256 + hf * 128, [[256, 4], [1, 128]]),
                in_=P[7][:, 0:512]),
                wait=[ev_pr], rd=["P7"], wr=["a1o"], tag=f"{tgh}-pe2")
            prev["P6p"] = ev_pe1
            prev["P7p"] = ev_pe2
            prev["ofm_used"] = ev_pr
        ev_a1 = E.dma("s", fmaj(adst[it], 8, TA), a1o_t[:],
                      wait=[ev_pe2], rd=["a1o"], wr=[f"d:a1:{it}"],
                      tag=f"{tg}-sta1", q="sda1")
        prev["a1o_used"] = ev_a1
        prev["vfb_used"] = E.ev("ts")
        prev["gT_used"] = t10
        for c in range(16):
            prev[("firfw", c)] = E.ev("ts")

    pbb.close()
    E.barrier()

    if mode == "attn1":
        _tail()
        E.verify()
        _finish(nc, E, es)
        return nc

    pcp = ExitStack()
    wnat_t = sb("wnat_t", [128, FFI], F32, stk=pcp)
    wi_t = sb("wi_t", [128, FFI], I32, stk=pcp)
    wfq_t = sb("wfq_t", [128, FFI], F32, stk=pcp)
    wdq_t = sb("wdq_t", [128, FFI], F32, stk=pcp)
    bmq_t = sb("bmq_t", [128, 96], F32, stk=pcp)
    rrq_t = sb("rrq_t", [128, 96], F32, stk=pcp)
    nsq_t = sb("nsq_t", [128, 96], F32, stk=pcp)
    wqb_t = sb("wqb_t", [128, FFI], BF16, stk=pcp)
    stq_t = sb("stq_t", [128, 2 * 512], BF16, stk=pcp)
    prev = {}

    def qbrd(t, nblk):
        a = t[:]
        return bass.AP(a.tensor, a.offset, [a.ap[0], [1, nblk], [0, 32]])

    def qtri(t, D):
        a = t[:]
        return bass.AP(a.tensor, a.offset, [a.ap[0], [32, D // 32], [1, 32]])

    def lpbq_prep(wn, O, D, dst, wkey):
        nch, nkc, nblk = O // 128, D // 128, D // 32
        for c in range(nch):
            tg = f"qp-{wkey}{c}"
            ev_l = E.dma("p", wnat_t[:, 0:D], wn[c * 128:(c + 1) * 128, :],
                         wait=[prev.get("nat_used")], wr=["wnat"],
                         tag=f"{tg}-ld", q="pdq")
            ev_b = E.op("v", lambda v, D=D, nblk=nblk: v.tensor_reduce(
                out=bmq_t[:, 0:nblk], in_=qtri(wnat_t, D), axis=AXX, op=ALU.max,
                apply_absolute_value=True),
                wait=[ev_l, prev.get("bm_used")],
                rd=["wnat"], wr=["bm"], tag=f"{tg}-bm")
            ev_s = E.op("v", lambda v, nblk=nblk: v.tensor_scalar(
                out=bmq_t[:, 0:nblk], in0=bmq_t[:, 0:nblk], scalar1=1.0 / 7.0,
                scalar2=1e-12, op0=ALU.mult, op1=ALU.max), wait=[ev_b],
                rd=["bm"], wr=["bm"], tag=f"{tg}-s")
            ev_n = E.op("v", lambda v, nblk=nblk: v.tensor_scalar(
                out=nsq_t[:, 0:nblk], in0=bmq_t[:, 0:nblk], scalar1=-1.0,
                scalar2=None, op0=ALU.mult), wait=[ev_s, prev.get("ns_used")],
                rd=["bm"], wr=["nsb"], tag=f"{tg}-ns")
            ev_r = E.op("v", lambda v, nblk=nblk: v.reciprocal(
                out=rrq_t[:, 0:nblk], in_=bmq_t[:, 0:nblk]),
                wait=[ev_s, prev.get("rr_used")],
                rd=["bm"], wr=["rr"], tag=f"{tg}-r")
            ev_y = E.op("v", lambda v, D=D, nblk=nblk: v.tensor_tensor(
                out=qtri(wdq_t, D), in0=qtri(wnat_t, D), in1=qbrd(rrq_t, nblk),
                op=ALU.mult), wait=[ev_r, prev.get("wd_used")],
                rd=["wnat", "rr"], wr=["wd"], tag=f"{tg}-y")
            ev_i = E.op("v", lambda v, D=D: v.tensor_copy(
                out=wi_t[:, 0:D], in_=wdq_t[:, 0:D]),
                wait=[ev_y, prev.get("wi_used")],
                rd=["wd"], wr=["wi"], tag=f"{tg}-i")
            ev_f = E.op("v", lambda v, D=D: v.tensor_copy(
                out=wfq_t[:, 0:D], in_=wi_t[:, 0:D]),
                wait=[ev_i, prev.get("wf_used")],
                rd=["wi"], wr=["wf"], tag=f"{tg}-f")
            ev_d = E.op("v", lambda v, D=D: v.tensor_sub(
                out=wdq_t[:, 0:D], in0=wdq_t[:, 0:D], in1=wfq_t[:, 0:D]),
                wait=[ev_f], rd=["wd", "wf"], wr=["wd"], tag=f"{tg}-d")
            ev_f2 = E.op("v", lambda v, D=D: v.scalar_tensor_tensor(
                out=wfq_t[:, 0:D], in0=wdq_t[:, 0:D], scalar=0.5,
                in1=wfq_t[:, 0:D], op0=ALU.is_ge, op1=ALU.add), wait=[ev_d],
                rd=["wd", "wf"], wr=["wf"], tag=f"{tg}-f2")
            ev_h = E.op("v", lambda v, D=D: v.scalar_tensor_tensor(
                out=wdq_t[:, 0:D], in0=wdq_t[:, 0:D], scalar=-0.5,
                in1=wfq_t[:, 0:D], op0=ALU.is_le, op1=ALU.subtract),
                wait=[ev_f2], rd=["wd", "wf"], wr=["wd"], tag=f"{tg}-h")
            ev_q = E.op("v", lambda v, D=D, nblk=nblk: v.tensor_tensor(
                out=qtri(wqb_t, D), in0=qtri(wdq_t, D), in1=qbrd(nsq_t, nblk),
                op=ALU.mult), wait=[ev_h, prev.get("wq_used")],
                rd=["wd", "nsb"], wr=["wq"], tag=f"{tg}-q")
            prev["nat_used"] = ev_y
            prev["bm_used"] = ev_r
            prev["rr_used"] = ev_y
            prev["ns_used"] = ev_q
            prev["wi_used"] = ev_f
            prev["wf_used"] = ev_h
            prev["wd_used"] = ev_q
            for kq in range(nkc // 4):
                bnk = P[kq % 2]
                done = None
                for qq in range(4):
                    kc = kq * 4 + qq
                    ov = bnk[:].bitcast(BF16)[:, qq * 128:qq * 128 + 128]
                    done = E.op("t", lambda t, kc=kc, ov=ov, qq=qq: t.matmul(
                        out=ov, lhsT=wqb_t[:, kc * 128:(kc + 1) * 128],
                        rhs=id_bf_t[:], start=(qq == 0), stop=(qq == 3),
                        is_transpose=True),
                        wait=[ev_q, prev.get(("pqf", kq % 2))] if qq == 0 else [],
                        inc=["ts"] if qq == 3 else [],
                        rd=["wq", "idb"], wr=[f"P{kq % 2}"], tag=f"{tg}-tp{kc}")
                ev_e = E.op("v", lambda v, kq=kq: v.tensor_copy(
                    out=stq_t[:, (kq % 2) * 512:(kq % 2) * 512 + 512],
                    in_=P[kq % 2][:].bitcast(BF16)[:, 0:512]),
                    wait=[done, prev.get(("stq_used", kq % 2))],
                    rd=[f"P{kq % 2}"], wr=[f"stq{kq % 2}"], tag=f"{tg}-te{kq}")
                prev[("pqf", kq % 2)] = ev_e
                ev_st = E.dma(
                    "s",
                    bass.AP(dst.tensor, kq * 4 * 128 * O + c * 128,
                            [[O, 128], [128 * O, 4], [1, 128]]),
                    stq_t[:, (kq % 2) * 512:(kq % 2) * 512 + 512],
                    wait=[ev_e], rd=[f"stq{kq % 2}"],
                    wr=[f"d:{wkey}:{c}.{kq}"], tag=f"{tg}-st{kq}",
                    q=f"sdq{kq % 2}")
                prev[("stq_used", kq % 2)] = ev_st
            prev["wq_used"] = E.ev("ts")

    lpbq_prep(wupN, FFI, DM, wupT_dq, "wu")
    lpbq_prep(wgateN, FFI, DM, wgateT_dq, "wg")
    lpbq_prep(wdownN, DM, FFI, wdownT_dq, "wdn")
    ev_sdq_tot = E.ev("sdq0")
    ev_sdq_tot1 = E.ev("sdq1")

    pcp.close()
    E.barrier()

    pc = ExitStack()
    wupT_t = sb("wupT_t", [128, 8 * FFI], BF16, stk=pc)
    wgT_t = sb("wgT_t", [128, 8 * FFI], BF16, stk=pc)
    xpc_t = sb("xpc_t", [128, 8 * TA], F32, stk=pc)
    spc_t = sb("spc_t", [128, 2 * TA], F32, stk=pc)
    rpc_t = sb("rpc_t", [1, 4 * TA], F32, stk=pc)
    ypc_t = sb("ypc_t", [128, TA], F32, stk=pc)
    ipc_t = sb("ipc_t", [128, TA], I32, stk=pc)
    fpc_t = sb("fpc_t", [128, TA], F32, stk=pc)
    xqb_t = sb("xqb_t", [128, 8 * TA], BF16, stk=pc)
    upe_t = sb("upe_t", [128, 4 * TA], F32, stk=pc)
    mx2_t = sb("mx2_t", [128, 4], F32, stk=pc)

    ev_m2 = E.op("v", lambda v: v.memset(mx2_t[:], 0.0), wr=["mx2"],
                 tag="mx2-init")
    ev_wu = None
    for kc in range(8):
        ev_wu = E.dma("p", wupT_t[:, kc * FFI:(kc + 1) * FFI],
                      wupT_dq[kc * 128:(kc + 1) * 128, :],
                      wait=[ev_sdq_tot, ev_sdq_tot1] if kc == 0 else [],
                      rd=[f"d:wu:{c}.{kc // 4}" for c in range(24)],
                      wr=["wupT"], tag=f"ld-wu{kc}", q="pdq")
    for kc in range(8):
        ev_wu = E.dma("p", wgT_t[:, kc * FFI:(kc + 1) * FFI],
                      wgateT_dq[kc * 128:(kc + 1) * 128, :],
                      rd=[f"d:wg:{c}.{kc // 4}" for c in range(24)],
                      wr=["wgT"], tag=f"ld-wg{kc}", q="pdq")

    ev_sda1_tot = E.ev("sda1")
    for it in range(NTA):
        tg = f"c{it}"
        ev_x = E.dma("s", xpc_t[:], fmaj(x1_s[it], 8, TA),
                     wait=[prev.get("x_used"), ev_sda1_tot],
                     rd=[f"d:y:{it}"], wr=["x"], tag=f"{tg}-ldx", q="sdx")
        b0c = P[4]
        mm2ev = {}
        ev_sums = None
        for c in range(8):
            ev_sq = E.op("a", lambda a, c=c: a.activation(
                out=spc_t[:, (c % 2) * TA:(c % 2) * TA + TA],
                in_=xpc_t[:, c * TA:(c + 1) * TA], func=AF.Square),
                wait=[ev_x, mm2ev.get(c % 2), prev.get("sp_free")],
                rd=["x"], wr=[f"sp{c % 2}"], tag=f"{tg}-sq{c}")
            E.op("t", lambda t, c=c: t.matmul(
                out=b0c[0:1, 0:TA], lhsT=ones_t[:, 0:1],
                rhs=xpc_t[:, c * TA:(c + 1) * TA], start=(c == 0), stop=False),
                wait=[ev_x, prev.get("b4_free")] if c == 0 else [],
                rd=["x", "ones"], wr=["P4"], tag=f"{tg}-mmx{c}")
            ev_sums = E.op("t", lambda t, c=c: t.matmul(
                out=b0c[0:1, TA:2 * TA], lhsT=ones_t[:, 0:1],
                rhs=spc_t[:, (c % 2) * TA:(c % 2) * TA + TA],
                start=False, stop=(c == 7)),
                wait=[ev_sq], inc=["ts"], rd=[f"sp{c % 2}", "ones"], wr=["P4"],
                tag=f"{tg}-mmsq{c}")
            mm2ev[c % 2] = ev_sums
        e1 = E.op("v", lambda v: v.tensor_scalar(
            out=rpc_t[0:1, 0:2 * TA], in0=b0c[0:1, 0:2 * TA], scalar1=1.0 / DM,
            scalar2=None, op0=ALU.mult), wait=[ev_sums],
            rd=["P4"], wr=["row"], tag=f"{tg}-ln1")
        prev["b4_free"] = e1
        e2 = E.op("v", lambda v: v.tensor_mul(
            out=rpc_t[0:1, 2 * TA:3 * TA], in0=rpc_t[0:1, 0:TA],
            in1=rpc_t[0:1, 0:TA]), wait=[e1], rd=["row"], wr=["row"],
            tag=f"{tg}-ln2")
        e3 = E.op("v", lambda v: v.tensor_sub(
            out=rpc_t[0:1, 2 * TA:3 * TA], in0=rpc_t[0:1, TA:2 * TA],
            in1=rpc_t[0:1, 2 * TA:3 * TA]), wait=[e2], rd=["row"], wr=["row"],
            tag=f"{tg}-ln3")
        e4a = E.op("a", lambda a: a.activation(
            out=rpc_t[0:1, 3 * TA:4 * TA], in_=rpc_t[0:1, 2 * TA:3 * TA],
            func=AF.Sqrt, bias=eps_t[0:1, 0:1]), wait=[e3],
            rd=["row", "eps"], wr=["row"], tag=f"{tg}-ln4")
        e4 = E.op("v", lambda v: v.reciprocal(
            out=rpc_t[0:1, 3 * TA:4 * TA], in_=rpc_t[0:1, 3 * TA:4 * TA]),
            wait=[e4a], rd=["row"], wr=["row"], tag=f"{tg}-ln5")
        e5 = E.op("v", lambda v: v.tensor_mul(
            out=rpc_t[0:1, 2 * TA:3 * TA], in0=rpc_t[0:1, 0:TA],
            in1=rpc_t[0:1, 3 * TA:4 * TA]), wait=[e4], rd=["row"], wr=["row"],
            tag=f"{tg}-ln6")
        e6 = E.op("v", lambda v: v.tensor_copy(
            out=rpc_t[0:1, 0:TA], in_=rpc_t[0:1, 3 * TA:4 * TA]), wait=[e5],
            rd=["row"], wr=["row"], tag=f"{tg}-ln7")
        e7 = E.op("v", lambda v: v.tensor_scalar(
            out=rpc_t[0:1, TA:2 * TA], in0=rpc_t[0:1, 2 * TA:3 * TA],
            scalar1=-1.0, scalar2=None, op0=ALU.mult), wait=[e6],
            rd=["row"], wr=["row"], tag=f"{tg}-ln8")
        b1c = P[5]
        ebc = E.op("t", lambda t: t.matmul(
            out=b1c[:, 0:2 * TA], lhsT=onesr_t[0:1, :],
            rhs=rpc_t[0:1, 0:2 * TA],
            start=True, stop=True), wait=[e7, prev.get("b5_free")], inc=["ts"],
            rd=["row", "onesr"], wr=["P5"], tag=f"{tg}-lnbc")
        ev_xq = None
        for c in range(8):
            em = E.op("v", lambda v, c=c: v.tensor_mul(
                out=spc_t[:, (c % 2) * TA:(c % 2) * TA + TA],
                in0=xpc_t[:, c * TA:(c + 1) * TA], in1=b1c[:, 0:TA]),
                wait=[ebc, mm2ev.get(c % 2)],
                rd=["x", "P5"], wr=[f"sp{c % 2}"], tag=f"{tg}-xnm{c}")
            ey0 = E.op("v", lambda v, c=c: v.tensor_tensor(
                out=ypc_t[:], in0=spc_t[:, (c % 2) * TA:(c % 2) * TA + TA],
                in1=b1c[:, TA:2 * TA], op=ALU.add),
                wait=[em, prev.get("y_used")],
                rd=[f"sp{c % 2}", "P5"], wr=["yq"], tag=f"{tg}-xna{c}")
            mm2ev[c % 2] = ey0
            ey1 = E.op("a", lambda a: a.activation(
                out=ypc_t[:], in_=ypc_t[:], func=AF.Identity,
                scale=sc_t[:, 1:2]), wait=[ey0, ev_sc1],
                rd=["yq", "sc"], wr=["yq"], tag=f"{tg}-qy{c}")
            ei = E.op("v", lambda v: v.tensor_copy(
                out=ipc_t[:], in_=ypc_t[:]), wait=[ey1, prev.get("i_used")],
                rd=["yq"], wr=["iq"], tag=f"{tg}-qi{c}")
            ef = E.op("v", lambda v: v.tensor_copy(
                out=fpc_t[:], in_=ipc_t[:]), wait=[ei, prev.get("f_used")],
                rd=["iq"], wr=["fq"], tag=f"{tg}-qf{c}")
            ed = E.op("v", lambda v: v.tensor_sub(
                out=ypc_t[:], in0=ypc_t[:], in1=fpc_t[:]), wait=[ef],
                rd=["yq", "fq"], wr=["yq"], tag=f"{tg}-qd{c}")
            ef2 = E.op("v", lambda v: v.scalar_tensor_tensor(
                out=fpc_t[:], in0=ypc_t[:], scalar=0.5, in1=fpc_t[:],
                op0=ALU.is_ge, op1=ALU.add), wait=[ed],
                rd=["yq", "fq"], wr=["fq"], tag=f"{tg}-qf2{c}")
            eh = E.op("v", lambda v: v.scalar_tensor_tensor(
                out=ypc_t[:], in0=ypc_t[:], scalar=-0.5, in1=fpc_t[:],
                op0=ALU.is_le, op1=ALU.subtract), wait=[ef2],
                rd=["yq", "fq"], wr=["yq"], tag=f"{tg}-qh{c}")
            prev["i_used"] = ef
            prev["f_used"] = eh
            ev_xq = E.op("a", lambda a, c=c: a.activation(
                out=xqb_t[:, c * TA:(c + 1) * TA], in_=ypc_t[:],
                func=AF.Identity, scale=sc_t[:, 2:3]),
                wait=[eh, prev.get("xq_used")],
                rd=["yq", "sc"], wr=["xq"], tag=f"{tg}-qs{c}")
            prev["y_used"] = ev_xq
        prev["sp_free"] = ev_xq
        prev["b5_free"] = ev_xq
        prev["x_used"] = ev_xq

        ev_red = None
        for oc in range(24):
            for (wt, wky, boff, mcol, stash, dk, qn) in (
                    (wupT_t, "wupT", 0, 0, up_s, "u", "sdu"),
                    (wgT_t, "wgT", 2, 1, gate_s, "g", "sdg2")):
                bnk = P[boff + oc % 2]
                bky = f"P{boff + oc % 2}"
                ev_mm = None
                for kc in range(8):
                    w0 = []
                    if kc == 0:
                        w0 = [ev_xq, ev_wu, prev.get(("pf", bky))]
                    ev_mm = E.op("t", lambda t, oc=oc, kc=kc, bnk=bnk,
                                 wt=wt: t.matmul(
                        out=bnk[:, 0:TA],
                        lhsT=wt[:, kc * FFI + oc * 128:kc * FFI + oc * 128 + 128],
                        rhs=xqb_t[:, kc * TA:(kc + 1) * TA],
                        start=(kc == 0), stop=(kc == 7)),
                        wait=w0, inc=["ts"] if kc == 7 else [],
                        rd=[wky, "xq"], wr=[bky], tag=f"{tg}-{dk}{oc}.{kc}")
                slot = boff + oc % 2
                ev_ev = E.op("a", lambda a, bnk=bnk, slot=slot: a.activation(
                    out=upe_t[:, slot * TA:(slot + 1) * TA], in_=bnk[:, 0:TA],
                    func=AF.Identity),
                    wait=[ev_mm, prev.get(("ue", slot)),
                          prev.get(("uer", slot))],
                    rd=[bky], wr=[f"ue{slot}"], tag=f"{tg}-{dk}e{oc}")
                ev_st = E.dma("s",
                              bass.AP(stash.tensor,
                                      (it * FFI + oc * 128) * TA,
                                      [[TA, 128], [1, TA]]),
                              upe_t[:, slot * TA:(slot + 1) * TA],
                              wait=[ev_ev], rd=[f"ue{slot}"],
                              wr=[f"d:{dk}s:{it}.{oc}"],
                              tag=f"{tg}-{dk}st{oc}", q=f"{qn}{oc % 2}")
                ev_rd = E.op("v", lambda v, slot=slot: v.tensor_reduce(
                    out=mx2_t[:, 2:3], in_=upe_t[:, slot * TA:(slot + 1) * TA],
                    axis=AXX, op=ALU.max, apply_absolute_value=True),
                    wait=[ev_ev, ev_red if ev_red is not None else ev_m2],
                    rd=[f"ue{slot}"], wr=["mx2t"], tag=f"{tg}-{dk}r{oc}")
                ev_red = E.op("v", lambda v, mcol=mcol: v.tensor_tensor(
                    out=mx2_t[:, mcol:mcol + 1], in0=mx2_t[:, mcol:mcol + 1],
                    in1=mx2_t[:, 2:3], op=ALU.max), wait=[ev_rd, ev_m2],
                    rd=["mx2t", "mx2"], wr=["mx2"], tag=f"{tg}-{dk}m{oc}")
                prev[("pf", bky)] = ev_ev
                prev[("ue", slot)] = ev_st
                prev[("uer", slot)] = ev_rd
        prev["xq_used"] = E.ev("ts")
        prev["mx_tail"] = ev_red

    ev_d1 = E.dma("s", bass.AP(cc_in[1].tensor, 0, [[1, 128]]), mx2_t[:, 0:1],
                  wait=[prev.get("mx_tail")], rd=["mx2"], wr=["d:cc1i"],
                  tag="cc1-st0", q="sdcc")
    ev_d2 = E.dma("s", bass.AP(cc_in[1].tensor, 128, [[1, 128]]), mx2_t[:, 1:2],
                  rd=["mx2"], wr=["d:cc1i"], tag="cc1-st1", q="sdcc")
    ev_cc1 = E.op("p", lambda p: p.collective_compute(
        "AllReduce", ALU.max, replica_groups=[list(range(NCORES))],
        ins=[cc_in[1][0:256]], outs=[cc_out[1][0:256]]),
        wait=[ev_d2], inc=["ps"], rd=["d:cc1i"], wr=["d:cc1o"],
        tag="cc1")

    def scale_tail(cco, cc_done, n, dstc, tagp, bank):
        ev_l = E.dma("s", rowc_t[0:1, 0:n * 128],
                     bass.AP(cco.tensor, 0, [[n * 128, 1], [1, n * 128]]),
                     wait=[cc_done, E.ev("vs"), prev.get("rowc_used")],
                     rd=[f"d:{tagp}o"], wr=["rowc"], tag=f"{tagp}-ld", q="sdcc")
        base = n * 128
        evs = []
        for i in range(n):
            er = E.op("v", lambda v, i=i, base=base: v.tensor_reduce(
                out=rowc_t[0:1, base + 4 * i:base + 4 * i + 1],
                in_=rowc_t[0:1, i * 128:(i + 1) * 128], axis=AXX, op=ALU.max),
                wait=[ev_l], rd=["rowc"], wr=["rowc"], tag=f"{tagp}-red{i}")
            es_ = E.op("v", lambda v, i=i, base=base: v.tensor_scalar(
                out=rowc_t[0:1, base + 4 * i + 1:base + 4 * i + 2],
                in0=rowc_t[0:1, base + 4 * i:base + 4 * i + 1],
                scalar1=1.0 / 32767.0, scalar2=1e-12, op0=ALU.mult,
                op1=ALU.max), wait=[er], rd=["rowc"], wr=["rowc"],
                tag=f"{tagp}-s{i}")
            err_ = E.op("v", lambda v, i=i, base=base: v.reciprocal(
                out=rowc_t[0:1, base + 4 * i + 2:base + 4 * i + 3],
                in_=rowc_t[0:1, base + 4 * i + 1:base + 4 * i + 2]),
                wait=[es_], rd=["rowc"], wr=["rowc"], tag=f"{tagp}-r{i}")
            en_ = E.op("v", lambda v, i=i, base=base: v.tensor_scalar(
                out=rowc_t[0:1, base + 4 * i + 3:base + 4 * i + 4],
                in0=rowc_t[0:1, base + 4 * i + 1:base + 4 * i + 2],
                scalar1=-1.0, scalar2=None, op0=ALU.mult), wait=[err_],
                rd=["rowc"], wr=["rowc"], tag=f"{tagp}-n{i}")
            evs.append(en_)
        eb = E.op("t", lambda t, n=n, base=base, bank=bank: t.matmul(
            out=bank[:, 0:2 * n],
            lhsT=onesr_t[0:1, :],
            rhs=bass.AP(rowc_t[:].tensor, rowc_t[:].offset + base + 2,
                        [rowc_t[:].ap[0], [4, n], [1, 2]]),
            start=True, stop=True), wait=evs + [prev.get("scb_free")],
            inc=["ts"], rd=["rowc", "onesr"], wr=["Psc"], tag=f"{tagp}-bc")
        ev = E.op("v", lambda v, n=n, dstc=dstc, bank=bank: v.tensor_copy(
            out=sc_t[:, dstc:dstc + 2 * n], in_=bank[:, 0:2 * n]),
            wait=[eb], rd=["Psc"], wr=["sc"], tag=f"{tagp}-ev")
        prev["scb_free"] = ev
        prev["rowc_used"] = ev
        return ev

    ev_sc2 = scale_tail(cc_out[1], ev_cc1, 2, 3, "cc1", P[6])

    pc.close()
    E.barrier()

    pda = ExitStack()
    g2_t = sb("g2_t", [128, 2 * 512], F32, stk=pda)
    ya_t = sb("ya_t", [128, 2 * 512], F32, stk=pda)
    ia_t = sb("ia_t", [128, 2 * 512], I32, stk=pda)
    fa_t = sb("fa_t", [128, 2 * 512], F32, stk=pda)
    gq2_t = sb("gq2_t", [128, 2 * 512], F32, stk=pda)
    gs2_t = sb("gs2_t", [128, 2 * 512], F32, stk=pda)
    mx3_t = sb("mx3_t", [128, 4], F32, stk=pda)
    ev_m3 = E.op("v", lambda v: v.memset(mx3_t[:], 0.0), wr=["mx3"],
                 tag="mx3-init")

    def qdq_chain(src_ev, src_ap, skey, rcol, nscol, dst, dkey, tg, slot,
                  ya_q, ia_q, fa_q, sc_ev, ya_wait=None):
        sl0 = slot * 512
        e1 = E.op("a", lambda a, ya_q=ya_q, sl0=sl0, src_ap=src_ap,
                  rcol=rcol: a.activation(
            out=ya_q[:, sl0:sl0 + 512], in_=src_ap, func=AF.Identity,
            scale=sc_t[:, rcol:rcol + 1]),
            wait=[src_ev, sc_ev, ya_wait, prev.get(("yafr", slot))],
            rd=[skey, "sc"], wr=[f"ya{slot}"], tag=f"{tg}-y")
        e2 = E.op("v", lambda v, ya_q=ya_q, ia_q=ia_q, sl0=sl0: v.tensor_copy(
            out=ia_q[:, sl0:sl0 + 512], in_=ya_q[:, sl0:sl0 + 512]),
            wait=[e1, prev.get(("ia_used", slot))],
            rd=[f"ya{slot}"], wr=[f"ia{slot}"], tag=f"{tg}-i")
        e3 = E.op("v", lambda v, ia_q=ia_q, fa_q=fa_q, sl0=sl0: v.tensor_copy(
            out=fa_q[:, sl0:sl0 + 512], in_=ia_q[:, sl0:sl0 + 512]),
            wait=[e2, prev.get(("fa_used", slot))],
            rd=[f"ia{slot}"], wr=[f"fa{slot}"], tag=f"{tg}-f")
        e4 = E.op("v", lambda v, ya_q=ya_q, fa_q=fa_q, sl0=sl0: v.tensor_sub(
            out=ya_q[:, sl0:sl0 + 512], in0=ya_q[:, sl0:sl0 + 512],
            in1=fa_q[:, sl0:sl0 + 512]), wait=[e3],
            rd=[f"ya{slot}", f"fa{slot}"], wr=[f"ya{slot}"], tag=f"{tg}-d")
        e5 = E.op("v", lambda v, ya_q=ya_q, fa_q=fa_q,
                  sl0=sl0: v.scalar_tensor_tensor(
            out=fa_q[:, sl0:sl0 + 512], in0=ya_q[:, sl0:sl0 + 512],
            scalar=0.5, in1=fa_q[:, sl0:sl0 + 512],
            op0=ALU.is_ge, op1=ALU.add), wait=[e4],
            rd=[f"ya{slot}", f"fa{slot}"], wr=[f"fa{slot}"], tag=f"{tg}-f2")
        e6 = E.op("v", lambda v, ya_q=ya_q, fa_q=fa_q,
                  sl0=sl0: v.scalar_tensor_tensor(
            out=ya_q[:, sl0:sl0 + 512], in0=ya_q[:, sl0:sl0 + 512],
            scalar=-0.5, in1=fa_q[:, sl0:sl0 + 512],
            op0=ALU.is_le, op1=ALU.subtract), wait=[e5],
            rd=[f"ya{slot}", f"fa{slot}"], wr=[f"ya{slot}"], tag=f"{tg}-h")
        prev[("ia_used", slot)] = e3
        prev[("fa_used", slot)] = e6
        e7 = E.op("a", lambda a, dst=dst, ya_q=ya_q, sl0=sl0,
                  nscol=nscol: a.activation(
            out=dst, in_=ya_q[:, sl0:sl0 + 512], func=AF.Identity,
            scale=sc_t[:, nscol:nscol + 1]),
            wait=[e6, prev.get(dkey + "_used")],
            rd=[f"ya{slot}", "sc"], wr=[dkey], tag=f"{tg}-o")
        return e7

    def ffn_tile(dr, it, od):
        return bass.AP(dr.tensor, (it * FFI + od * 256) * TA,
                       [[TA, 128], [128 * TA, 2], [1, TA]])

    ev_g2tots = [E.ev("sdg20"), E.ev("sdg21")]
    ev_red = None
    for it in range(NTA):
        for od in range(12):
            tg = f"da{it}.{od}"
            slot = od % 2
            sl0 = slot * 512
            ld = E.dma("s", g2_t[:, sl0:sl0 + 512], ffn_tile(gate_s, it, od),
                       wait=[prev.get(("g2", slot))] + ev_g2tots,
                       rd=[f"d:gs:{it}.{od}"], wr=[f"g2{slot}"],
                       tag=f"{tg}-ld", q=f"sdda{slot}")
            e_gq = qdq_chain(ld, g2_t[:, sl0:sl0 + 512], f"g2{slot}", 5, 6,
                             gq2_t[:, sl0:sl0 + 512], f"gq2{slot}", tg, slot,
                             ya_t, ia_t, fa_t, ev_sc2)
            prev[("g2", slot)] = e_gq
            e_sg = E.op("a", lambda a, sl0=sl0: a.activation(
                out=ya_t[:, sl0:sl0 + 512], in_=gq2_t[:, sl0:sl0 + 512],
                func=AF.Sigmoid),
                wait=[e_gq], rd=[f"gq2{slot}"], wr=[f"ya{slot}"],
                tag=f"{tg}-sg")
            e_ym = E.op("a", lambda a, sl0=sl0: a.activation(
                out=ya_t[:, sl0:sl0 + 512], in_=ya_t[:, sl0:sl0 + 512],
                func=AF.Identity, scale=c64k_t[:, 0:1]), wait=[e_sg],
                rd=[f"ya{slot}", "c64k"], wr=[f"ya{slot}"], tag=f"{tg}-ym")
            e2 = E.op("v", lambda v, sl0=sl0: v.tensor_copy(
                out=ia_t[:, sl0:sl0 + 512], in_=ya_t[:, sl0:sl0 + 512]),
                wait=[e_ym, prev.get(("ia_used", slot))],
                rd=[f"ya{slot}"], wr=[f"ia{slot}"], tag=f"{tg}-i2")
            e3 = E.op("v", lambda v, sl0=sl0: v.tensor_copy(
                out=fa_t[:, sl0:sl0 + 512], in_=ia_t[:, sl0:sl0 + 512]),
                wait=[e2, prev.get(("fa_used", slot))],
                rd=[f"ia{slot}"], wr=[f"fa{slot}"], tag=f"{tg}-f3")
            e4 = E.op("v", lambda v, sl0=sl0: v.tensor_sub(
                out=ya_t[:, sl0:sl0 + 512], in0=ya_t[:, sl0:sl0 + 512],
                in1=fa_t[:, sl0:sl0 + 512]), wait=[e3],
                rd=[f"ya{slot}", f"fa{slot}"], wr=[f"ya{slot}"],
                tag=f"{tg}-d2")
            e5 = E.op("v", lambda v, sl0=sl0: v.scalar_tensor_tensor(
                out=fa_t[:, sl0:sl0 + 512], in0=ya_t[:, sl0:sl0 + 512],
                scalar=0.5, in1=fa_t[:, sl0:sl0 + 512],
                op0=ALU.is_ge, op1=ALU.add), wait=[e4],
                rd=[f"ya{slot}", f"fa{slot}"], wr=[f"fa{slot}"],
                tag=f"{tg}-f4")
            e6 = E.op("v", lambda v, sl0=sl0: v.scalar_tensor_tensor(
                out=ya_t[:, sl0:sl0 + 512], in0=ya_t[:, sl0:sl0 + 512],
                scalar=-0.5, in1=fa_t[:, sl0:sl0 + 512],
                op0=ALU.is_le, op1=ALU.subtract), wait=[e5],
                rd=[f"ya{slot}", f"fa{slot}"], wr=[f"ya{slot}"],
                tag=f"{tg}-h2")
            prev[("ia_used", slot)] = e3
            prev[("fa_used", slot)] = e6
            e7 = E.op("v", lambda v, sl0=sl0: v.tensor_scalar(
                out=ya_t[:, sl0:sl0 + 512], in0=ya_t[:, sl0:sl0 + 512],
                scalar1=-65535.0, scalar2=None, op0=ALU.max), wait=[e6],
                rd=[f"ya{slot}"], wr=[f"ya{slot}"], tag=f"{tg}-cl")
            e8 = E.op("v", lambda v, sl0=sl0: v.scalar_tensor_tensor(
                out=gs2_t[:, sl0:sl0 + 512], in0=ya_t[:, sl0:sl0 + 512],
                scalar=-1.0 / 65536.0, in1=gq2_t[:, sl0:sl0 + 512],
                op0=ALU.mult, op1=ALU.mult),
                wait=[e7, prev.get(("gs2_used", slot))],
                rd=[f"ya{slot}", f"gq2{slot}"], wr=[f"gs2{slot}"],
                tag=f"{tg}-gs")
            prev[("yafr", slot)] = e8
            prev[f"gq2{slot}_used"] = e8
            ev_st = E.dma("p", ffn_tile(gs_s, it, od),
                          gs2_t[:, sl0:sl0 + 512],
                          wait=[e8], rd=[f"gs2{slot}"],
                          wr=[f"d:gss:{it}.{od}"],
                          tag=f"{tg}-st", q=f"sddb{slot}")
            ev_rd = E.op("v", lambda v, sl0=sl0: v.tensor_reduce(
                out=mx3_t[:, 2:3], in_=gs2_t[:, sl0:sl0 + 512], axis=AXX,
                op=ALU.max, apply_absolute_value=True),
                wait=[e8, ev_red if ev_red is not None else ev_m3],
                rd=[f"gs2{slot}"], wr=["mx3t"], tag=f"{tg}-mr")
            ev_red = E.op("v", lambda v: v.tensor_tensor(
                out=mx3_t[:, 0:1], in0=mx3_t[:, 0:1], in1=mx3_t[:, 2:3],
                op=ALU.max), wait=[ev_rd, ev_m3],
                rd=["mx3t", "mx3"], wr=["mx3"], tag=f"{tg}-mm")
            prev[("gs2_used", slot)] = ev_st
            prev[("gs2r", slot)] = ev_rd

    ev_d3 = E.dma("s", bass.AP(cc_in[2].tensor, 0, [[1, 128]]), mx3_t[:, 0:1],
                  wait=[ev_red], rd=["mx3"], wr=["d:cc2i"],
                  tag="cc2-st", q="sdcc")
    ev_cc2 = E.op("p", lambda p: p.collective_compute(
        "AllReduce", ALU.max, replica_groups=[list(range(NCORES))],
        ins=[cc_in[2][0:128]], outs=[cc_out[2][0:128]]),
        wait=[ev_d3], inc=["ps"], rd=["d:cc2i"], wr=["d:cc2o"], tag="cc2")
    ev_sc3 = scale_tail(cc_out[2], ev_cc2, 1, 7, "cc2", P[6])

    pda.close()
    E.barrier()

    pdb = ExitStack()
    g2b_t = sb("g2b_t", [128, 2 * 512], F32, stk=pdb)
    u2b_t = sb("u2b_t", [128, 2 * 512], F32, stk=pdb)
    yab_t = sb("yab_t", [128, 2 * 512], F32, stk=pdb)
    iab_t = sb("iab_t", [128, 2 * 512], I32, stk=pdb)
    fab_t = sb("fab_t", [128, 2 * 512], F32, stk=pdb)
    gq2b_t = sb("gq2b_t", [128, 2 * 512], F32, stk=pdb)
    gs2b_t = sb("gs2b_t", [128, 2 * 512], F32, stk=pdb)
    mx3b_t = sb("mx3b_t", [128, 4], F32, stk=pdb)
    ev_m3b = E.op("v", lambda v: v.memset(mx3b_t[:], 0.0), wr=["mx3"],
                  tag="mx3b-init")
    for s_ in range(2):
        prev[("ia_used", s_)] = None
        prev[("fa_used", s_)] = None
        prev[("yafr", s_)] = None
        prev[("g2", s_)] = None
        prev[("u2", s_)] = None
        prev[("gs2_used", s_)] = None
        prev[f"gq2{s_}_used"] = None
        prev[f"uq2{s_}_used"] = None
        prev[f"gsq2{s_}_used"] = None

    ev_utots = [E.ev("sdu0"), E.ev("sdu1")]
    ev_gstots = [E.ev("sddb0"), E.ev("sddb1")]
    ev_red = None
    for it in range(NTA):
        for od in range(12):
            tg = f"db{it}.{od}"
            slot = od % 2
            sl0 = slot * 512
            ldg = E.dma("s", g2b_t[:, sl0:sl0 + 512], ffn_tile(gs_s, it, od),
                        wait=[prev.get(("g2", slot))] + ev_gstots,
                        rd=[f"d:gss:{it}.{od}"], wr=[f"g2{slot}"],
                        tag=f"{tg}-lg", q=f"sddc{slot}")
            ldu = E.dma("s", u2b_t[:, sl0:sl0 + 512], ffn_tile(up_s, it, od),
                        wait=[prev.get(("u2", slot))] + ev_utots,
                        rd=[f"d:us:{it}.{od}"], wr=[f"u2{slot}"],
                        tag=f"{tg}-lu", q=f"sddd{slot}")
            e_uq = qdq_chain(ldu, u2b_t[:, sl0:sl0 + 512], f"u2{slot}", 3, 4,
                             gq2b_t[:, sl0:sl0 + 512], f"uq2{slot}", tg + "u",
                             slot, yab_t, iab_t, fab_t, ev_sc3)
            prev[("u2", slot)] = e_uq
            e_gq = qdq_chain(ldg, g2b_t[:, sl0:sl0 + 512], f"g2{slot}", 7, 8,
                             gs2b_t[:, sl0:sl0 + 512], f"gsq2{slot}",
                             tg + "g", slot, yab_t, iab_t, fab_t, ev_sc3,
                             ya_wait=e_uq)
            prev[("g2", slot)] = e_gq
            e_o = E.op("v", lambda v, sl0=sl0: v.tensor_mul(
                out=gs2b_t[:, sl0:sl0 + 512], in0=gs2b_t[:, sl0:sl0 + 512],
                in1=gq2b_t[:, sl0:sl0 + 512]),
                wait=[e_gq], rd=[f"gsq2{slot}", f"uq2{slot}"],
                wr=[f"gsq2{slot}"], tag=f"{tg}-o")
            prev[("yafr", slot)] = e_gq
            prev[f"uq2{slot}_used"] = e_o
            ev_st = E.dma("p", ffn_tile(op_s, it, od),
                          gs2b_t[:, sl0:sl0 + 512],
                          wait=[e_o], rd=[f"gsq2{slot}"],
                          wr=[f"d:ops:{it}.{od}"],
                          tag=f"{tg}-st", q=f"sdde{slot}")
            ev_rd = E.op("v", lambda v, sl0=sl0: v.tensor_reduce(
                out=mx3b_t[:, 2:3], in_=gs2b_t[:, sl0:sl0 + 512], axis=AXX,
                op=ALU.max, apply_absolute_value=True),
                wait=[e_o, ev_red if ev_red is not None else ev_m3b],
                rd=[f"gsq2{slot}"], wr=["mx3t"], tag=f"{tg}-mr")
            ev_red = E.op("v", lambda v: v.tensor_tensor(
                out=mx3b_t[:, 0:1], in0=mx3b_t[:, 0:1], in1=mx3b_t[:, 2:3],
                op=ALU.max), wait=[ev_rd, ev_m3b],
                rd=["mx3t", "mx3"], wr=["mx3"], tag=f"{tg}-mm")
            prev[f"gsq2{slot}_used"] = ev_st

    ev_d4 = E.dma("s", bass.AP(cc_in[3].tensor, 0, [[1, 128]]), mx3b_t[:, 0:1],
                  wait=[ev_red], rd=["mx3"], wr=["d:cc3i"],
                  tag="cc3-st", q="sdcc")
    ev_cc3 = E.op("p", lambda p: p.collective_compute(
        "AllReduce", ALU.max, replica_groups=[list(range(NCORES))],
        ins=[cc_in[3][0:128]], outs=[cc_out[3][0:128]]),
        wait=[ev_d4], inc=["ps"], rd=["d:cc3i"], wr=["d:cc3o"], tag="cc3")
    ev_sc4 = scale_tail(cc_out[3], ev_cc3, 1, 9, "cc3", P[6])

    pdb.close()
    E.barrier()

    pdc = ExitStack()
    wdnT_t = sb("wdnT_t", [128, 24 * DM], BF16, stk=pdc)
    o2c_t = sb("o2c_t", [128, 2 * 512], F32, stk=pdc)
    yac_t = sb("yac_t", [128, 2 * 512], F32, stk=pdc)
    iac_t = sb("iac_t", [128, 2 * 512], I32, stk=pdc)
    fac_t = sb("fac_t", [128, 2 * 512], F32, stk=pdc)
    oqb_t = sb("oqb_t", [128, 24 * TA], BF16, stk=pdc)
    xf_t = sb("xf_t", [128, 8 * TA], F32, stk=pdc)
    af_t = sb("af_t", [128, 8 * TA], F32, stk=pdc)
    yo_t = sb("yo_t", [128, 8 * TA], F32, stk=pdc)
    ofe_t = sb("ofe_t", [128, 2 * TA], F32, stk=pdc)
    for s_ in range(2):
        prev[("ia_used", s_)] = None
        prev[("fa_used", s_)] = None
        prev[("yafr", s_)] = None
        prev[("o2", s_)] = None
    prev["oqb_used"] = None

    ev_wd = None
    for kc in range(24):
        ev_wd = E.dma("p", wdnT_t[:, kc * DM:(kc + 1) * DM],
                      wdownT_dq[kc * 128:(kc + 1) * 128, :],
                      wait=[ev_sdq_tot, ev_sdq_tot1] if kc == 0 else [],
                      rd=[f"d:wdn:{c}.{kc // 4}" for c in range(8)],
                      wr=["wdnT"], tag=f"ld-wdn{kc}", q="pdq")

    ev_optots = [E.ev("sdde0"), E.ev("sdde1")]
    for it in range(NTA):
        tg = f"dc{it}"
        ev_x = E.dma("s", xf_t[:], fmaj(x1_s[it], 8, TA),
                     wait=[prev.get("xf_used")],
                     rd=[f"d:y:{it}"], wr=["xf"], tag=f"{tg}-ldx", q="sdx")
        ev_a = E.dma("s", af_t[:], fmaj(attn1_s[it], 8, TA),
                     wait=[prev.get("af_used"), E.ev("sda1")],
                     rd=[f"d:a1:{it}"], wr=["af"], tag=f"{tg}-lda", q="sdl1")
        ev_oq = None
        for od in range(12):
            slot = od % 2
            sl0 = slot * 512
            ld = E.dma("s", o2c_t[:, sl0:sl0 + 512], ffn_tile(op_s, it, od),
                       wait=[prev.get(("o2", slot))] + ev_optots,
                       rd=[f"d:ops:{it}.{od}"], wr=[f"o2{slot}"],
                       tag=f"{tg}-lo{od}", q=f"sddf{slot}")
            ev_oq = qdq_chain(
                ld, o2c_t[:, sl0:sl0 + 512], f"o2{slot}", 9, 10,
                oqb_t[:, od * 512:(od + 1) * 512], "oqb",
                f"{tg}-q{od}", slot, yac_t, iac_t, fac_t, ev_sc4,
                ya_wait=prev.get("oqb_used"))
            prev[("o2", slot)] = ev_oq
            prev[("yafr", slot)] = ev_oq
        ev_pr = None
        e_a1 = e_a2 = None
        for oc in range(8):
            bnk = P[oc % 2]
            bky = f"P{oc % 2}"
            for kc in range(24):
                w0 = []
                if kc == 0:
                    w0 = [ev_oq, ev_wd, prev.get(("dpf", oc % 2))]
                ev_pr = E.op("t", lambda t, oc=oc, kc=kc, bnk=bnk: t.matmul(
                    out=bnk[:, 0:TA],
                    lhsT=wdnT_t[:, kc * DM + oc * 128:kc * DM + oc * 128 + 128],
                    rhs=oqb_t[:, kc * TA:(kc + 1) * TA],
                    start=(kc == 0), stop=(kc == 23)),
                    wait=w0, inc=["ts"] if kc == 23 else [],
                    rd=["wdnT", "oqb"], wr=[bky], tag=f"{tg}-dn{oc}.{kc}")
            ev_ev = E.op("a", lambda a, oc=oc, bnk=bnk: a.activation(
                out=ofe_t[:, (oc % 2) * TA:(oc % 2) * TA + TA],
                in_=bnk[:, 0:TA], func=AF.Identity),
                wait=[ev_pr, prev.get(("ofe", oc % 2))],
                rd=[bky], wr=[f"ofe{oc % 2}"], tag=f"{tg}-de{oc}")
            prev[("dpf", oc % 2)] = ev_ev
            e_a1 = E.op("v", lambda v, oc=oc: v.tensor_tensor(
                out=yo_t[:, oc * TA:(oc + 1) * TA],
                in0=ofe_t[:, (oc % 2) * TA:(oc % 2) * TA + TA],
                in1=xf_t[:, oc * TA:(oc + 1) * TA], op=ALU.add),
                wait=[ev_ev, ev_x, prev.get("yo_used")],
                rd=[f"ofe{oc % 2}", "xf"], wr=["yo"], tag=f"{tg}-a1.{oc}")
            e_a2 = E.op("v", lambda v, oc=oc: v.tensor_tensor(
                out=yo_t[:, oc * TA:(oc + 1) * TA],
                in0=yo_t[:, oc * TA:(oc + 1) * TA],
                in1=af_t[:, oc * TA:(oc + 1) * TA], op=ALU.add),
                wait=[e_a1, ev_a], rd=["yo", "af"], wr=["yo"],
                tag=f"{tg}-a2.{oc}")
            prev[("ofe", oc % 2)] = e_a1
        prev["xf_used"] = e_a1
        prev["af_used"] = e_a2
        prev["oqb_used"] = E.ev("ts")
        ev_yo = E.dma("s", fmaj(yout[it], 8, TA), yo_t[:],
                      wait=[e_a2], rd=["yo"], wr=[f"d:yout:{it}"],
                      tag=f"{tg}-sty", q="sdy")
        prev["yo_used"] = ev_yo

    pdc.close()
    _tail()
    E.verify()
    _finish(nc, E, es)
    return nc


def _finish(nc, E, es):
    with nc.Block() as block:
        @block.tensor
        def _(eng):
            for f in E.q["t"]:
                f(eng)

        @block.vector
        def _(eng):
            for f in E.q["v"]:
                f(eng)

        @block.scalar
        def _(eng):
            for f in E.q["a"]:
                f(eng)

        @block.gpsimd
        def _(eng):
            for f in E.q["p"]:
                f(eng)

        @block.sync
        def _(eng):
            for f in E.q["s"]:
                f(eng)

    es.close()


def _prep_inputs(inp, core):
    f32 = np.float32
    x = np.asarray(inp["L0_attn_x"][core], f32)
    x0 = np.ascontiguousarray(x.T).reshape(DM, NTA, TA).transpose(1, 0, 2)
    d = {}
    d["x0"] = np.ascontiguousarray(x0)
    d["w_in"] = np.ascontiguousarray(np.asarray(inp["L0_attn_W_in"], f32))
    d["w_x"] = np.ascontiguousarray(np.asarray(inp["L0_attn_W_x"], f32).reshape(16, 128, 33))
    d["w_out"] = np.ascontiguousarray(np.asarray(inp["L0_attn_W_out"], f32))
    cw = np.asarray(inp["L0_attn_conv_w"], f32)
    cd = np.zeros((64, 128, 128), f32)
    for ch in range(16):
        for k in range(4):
            cd[ch * 4 + k] = np.diag(cw[ch * 128:(ch + 1) * 128, k])
    d["l0cd"] = cd
    d["conv_b"] = np.ascontiguousarray(np.asarray(inp["L0_attn_conv_b"], f32).reshape(16, 128).T)
    d["dt_w"] = np.ascontiguousarray(np.asarray(inp["L0_attn_dt_w"], f32).reshape(16, 128).T)
    d["dt_b"] = np.ascontiguousarray(np.asarray(inp["L0_attn_dt_b"], f32).reshape(16, 128).T)
    d["a_log"] = np.ascontiguousarray(
        np.asarray(inp["L0_attn_A_log"], f32).reshape(16, 128, 16)
        .transpose(1, 0, 2).reshape(128, 256))
    d["d_res"] = np.ascontiguousarray(np.asarray(inp["L0_attn_D"], f32).reshape(16, 128).T)
    d["w1n"] = np.ascontiguousarray(np.asarray(inp["L0_ffn_w1"], f32))
    d["w1t"] = np.ascontiguousarray(np.asarray(inp["L0_ffn_w1"], f32).T)
    d["w2n"] = np.ascontiguousarray(np.asarray(inp["L0_ffn_w2"], f32))
    d["w2t"] = np.ascontiguousarray(np.asarray(inp["L0_ffn_w2"], f32).T)
    d["b1"] = np.ascontiguousarray(np.asarray(inp["L0_ffn_b1"], f32).reshape(32, 128).T)
    d["b2"] = np.ascontiguousarray(np.asarray(inp["L0_ffn_b2"], f32).reshape(8, 128).T)
    d["ones_f"] = np.ones((128, 2), f32)
    d["ones_r"] = np.ones((1, 128), f32)
    def kcmaj(a):
        R, C = a.shape
        n = R // 128
        return np.ascontiguousarray(
            a.reshape(n, 128, C).transpose(1, 0, 2).reshape(R, C))

    d["qwT"] = kcmaj(np.asarray(inp["L1_attn_q_proj_w"], f32).T)
    d["kwT"] = kcmaj(np.asarray(inp["L1_attn_k_proj_w"], f32).T)
    d["vwT"] = kcmaj(np.asarray(inp["L1_attn_v_proj_w"], f32).T)
    d["bwT"] = kcmaj(np.asarray(inp["L1_attn_b_proj_w"], f32).T)
    d["gm1T"] = kcmaj(np.asarray(inp["L1_attn_gmix_w1"], f32).T)
    d["gm2T"] = kcmaj(np.asarray(inp["L1_attn_gmix_w2"], f32).T)
    d["gmb1"] = np.ascontiguousarray(np.asarray(inp["L1_attn_gmix_b1"], f32).reshape(16, 128).T)
    d["gmb2"] = np.ascontiguousarray(np.asarray(inp["L1_attn_gmix_b2"], f32).reshape(16, 1))
    qkvcd = np.zeros((96, 128, 128), f32)
    for wi, wn in enumerate(["L1_attn_q_conv_w", "L1_attn_k_conv_w", "L1_attn_v_conv_w"]):
        cwv = np.asarray(inp[wn], f32)
        for ch in range(8):
            for k in range(4):
                qkvcd[(wi * 8 + ch) * 4 + k] = np.diag(cwv[ch * 128:(ch + 1) * 128, k])
    d["qkvcd"] = qkvcd
    fw_s = np.asarray(inp["L1_attn_fir_short_w"], f32).reshape(1024, 7)
    fw_l = np.asarray(inp["L1_attn_fir_long_w"], f32).reshape(1024, 31)
    fircd = np.zeros((304, 128, 128), f32)
    for ch in range(8):
        for k in range(7):
            fircd[ch * 38 + k] = np.diag(fw_s[ch * 128:(ch + 1) * 128, k])
        for k in range(31):
            fircd[ch * 38 + 7 + k] = np.diag(fw_l[ch * 128:(ch + 1) * 128, k])
    d["fircd"] = fircd
    d["oprojT"] = kcmaj(np.asarray(inp["L1_attn_o_proj_w"], f32).T)
    d["onw_rep"] = np.ascontiguousarray(
        np.broadcast_to(np.asarray(inp["L1_attn_o_norm_w"], f32)[None, :].repeat(128, 0)
                        .reshape(128, 256)[:, None, :], (128, 4, 256)).reshape(128, 1024))
    alr = np.zeros((16,), f32)
    alv = np.asarray(inp["L1_attn_alpha"], f32)[:, 0]
    for c4 in range(4):
        for h in range(4):
            alr[c4 * 4 + h] = alv[h]
    d["alpha_rep"] = np.broadcast_to(alr, (128, 16)).copy()
    tril_s = np.tril(np.ones((128, 128), f32), -1)
    d["maskstd"] = np.tile(-tril_s, (1, 4)).astype(f32)
    d["maskor"] = np.tile(-tril_s.T, (1, 4)).astype(f32)
    d["maskal"] = np.tile(np.triu(np.ones((128, 128), f32), 0), (1, 4)).astype(f32)
    ident = np.eye(128)
    d["idrep4"] = np.tile(ident, (1, 4)).astype(ml_dtypes.bfloat16)
    d["id_bf"] = ident.astype(ml_dtypes.bfloat16)
    bselv = np.zeros((4, 512), f32)
    for h in range(4):
        bselv[h, h * 128:(h + 1) * 128] = 1.0
    d["bsel"] = bselv
    d["id_f32"] = ident.astype(f32)
    d["wupN"] = np.ascontiguousarray(np.asarray(inp["L1_ffn_w_up"], f32))
    d["wgateN"] = np.ascontiguousarray(np.asarray(inp["L1_ffn_w_gate"], f32))
    d["wdownN"] = np.ascontiguousarray(np.asarray(inp["L1_ffn_w_down"], f32))
    return d


KMODE = "full"


def kernel(**inputs):
    nc = bass.Bass("TRN2", target_bir_lowering=False, debug=False,
                   detect_race_conditions=False)
    build(nc, mode=KMODE)
    in_maps = [_prep_inputs(inputs, c) for c in range(NCORES)]
    res = run_bass_kernel_spmd(nc, in_maps, core_ids=list(range(NCORES)))
    outs = []
    for c in range(NCORES):
        y = np.asarray(res.results[c]["yout"])
        y = y.transpose(1, 0, 2).reshape(DM, L).T
        outs.append(y)
    return np.stack(outs).astype(np.float32)


# version 04 (final)
# speedup vs baseline: 1.1486x; 1.0530x over previous
import numpy as np
import ml_dtypes
from contextlib import ExitStack

import concourse.bass as bass
import concourse.mybir as mybir
from concourse.bass_utils import run_bass_kernel_spmd

F32 = mybir.dt.float32
BF16 = mybir.dt.bfloat16
I32 = mybir.dt.int32
AF = mybir.ActivationFunctionType
ALU = mybir.AluOpType
AXX = mybir.AxisListType.X

NCORES = 8
L = 4096
DM = 1024
DI = 2048
NST = 16
TA = 256
NTA = L // TA
TB = 128
NTB = L // TB
NH = 4
DK = 256
FFI = 3072

ENGS = ("t", "v", "a", "p", "s")


def sap(t, off_extra, dims):
    a = t[:]
    return bass.AP(a.tensor, a.offset + off_extra, [a.ap[0]] + dims)


def fmaj(dram_slice, nch, T):
    a = dram_slice
    return bass.AP(a.tensor, a.offset, [[T, 128], [128 * T, nch], [1, T]])


class Emit:
    def __init__(self, nc, es):
        self.nc = nc
        self.q = {e: [] for e in ENGS}
        self.cnt = {}
        self.hands = {}
        self.hwm = {e: {} for e in ENGS}
        self.es = es
        self.ops = []
        self.eng_pos = {}
        self.sem_incs = {}
        self.sem_owner = {}
        self.check = True

    def sem(self, name):
        if name not in self.hands:
            self.hands[name] = self.es.enter_context(self.nc.semaphore(name))
            self.cnt.setdefault(name, 0)
        return self.hands[name]

    def ev(self, name):
        self.sem(name)
        return (name, self.cnt[name])

    def _rec(self, eng, waits, incs, rd, wr, tag):
        i = self.eng_pos.get(eng, 0)
        self.eng_pos[eng] = i + 1
        oid = len(self.ops)
        self.ops.append(dict(id=oid, eng=eng, idx=i, waits=list(waits),
                             rd=list(rd or ()), wr=list(wr or ()), tag=tag))
        for (s, n) in incs:
            ow = self.sem_owner.setdefault(s, eng)
            assert ow == eng, f"sem {s} inc'd from {eng} and {ow}"
            self.sem_incs.setdefault(s, []).append((self.cnt[s], oid))
        return oid

    def op(self, eng, fn, wait=(), inc=None, dinc=None, rd=None, wr=None, tag=""):
        if inc is None:
            inc = {"v": ["vs"], "a": ["as"], "t": [], "p": [], "s": []}[eng]
        allw = [w for w in wait if w is not None and w[1] > 0]
        waits = []
        for (s, v) in allw:
            h = self.hwm[eng].setdefault(s, 0)
            if v > h:
                waits.append((self.sem(s), v))
                self.hwm[eng][s] = v
        incs = []
        evt = None
        recinc = []
        for s in inc:
            self.cnt[s] = self.cnt.get(s, 0) + 1
            incs.append((self.sem(s), 1))
            recinc.append((s, 1))
            evt = (s, self.cnt[s])
        if dinc is not None:
            self.cnt[dinc] = self.cnt.get(dinc, 0) + 16
            incs.append((self.sem(dinc), 16))
            recinc.append((dinc, 16))
            evt = (dinc, self.cnt[dinc])
        self._rec(eng, allw, recinc, rd, wr, tag)

        def run(e, waits=waits, incs=incs, fn=fn):
            for (h, v) in waits:
                e.wait_ge(h, v)
            r = fn(e)
            for (h, n) in incs:
                r = r.then_inc(h, n)
        self.q[eng].append(run)
        return evt

    def barrier(self):
        s = "bar"
        self.sem(s)
        tgt = self.cnt.get(s, 0) + len(ENGS)
        incops = []
        for e in ENGS:
            self.cnt[s] += 1
            h = self.hands[s]
            self.q[e].append(lambda en, h=h: en.nop().then_inc(h, 1))
            incops.append(self._rec(e, [], [], None, None, f"bar-inc-{e}"))
        for e in ENGS:
            h = self.hands[s]
            self.q[e].append(lambda en, h=h, tgt=tgt: en.wait_ge(h, tgt))
            self.hwm[e][s] = tgt
            self._rec(e, [("__bar__", incops)], [], None, None, f"bar-wait-{e}")

    def dma(self, eng, out, in_, wait=(), rd=None, wr=None, tag="", q=None):
        sem = q or {"s": "sd", "p": "pd", "a": "ad", "v": "vd"}[eng]
        allw = [w for w in wait if w is not None and w[1] > 0]
        waits = []
        for (s, v) in allw:
            h = self.hwm[eng].setdefault(s, 0)
            if v > h:
                waits.append((self.sem(s), v))
                self.hwm[eng][s] = v
        self.cnt[sem] = self.cnt.get(sem, 0) + 16
        evt = (sem, self.cnt[sem])
        iss = self._rec(eng, allw, [], None, None, tag + "-iss")
        self.ops[iss]["dma_issue_for"] = sem
        tr = self._rec(sem + "q", [("__op__", iss)], [(sem, 16)], rd, wr, tag)
        self.ops[tr]["issue_op"] = iss
        h = self.sem(sem)

        def run(e, waits=waits, h=h, out=out, in_=in_):
            for (hh, v) in waits:
                e.wait_ge(hh, v)
            e.dma_start(out=out, in_=in_).then_inc(h, 16)
        self.q[eng].append(run)
        return evt

    def verify(self):
        n = len(self.ops)
        engs = sorted({o["eng"] for o in self.ops})
        eidx = {e: k for k, e in enumerate(engs)}
        ne = len(engs)
        clock = [None] * n
        import bisect
        sem_tl = {s: ([c for c, _ in lst], [o for _, o in lst])
                  for s, lst in self.sem_incs.items()}
        per_eng_ops = {}
        for o in self.ops:
            per_eng_ops.setdefault(o["eng"], []).append(o)
        for o in self.ops:
            c = [-1] * ne
            if o["idx"] > 0:
                pc = clock[per_eng_ops[o["eng"]][o["idx"] - 1]["id"]]
                c = [max(a, b) for a, b in zip(c, pc)]
            for w in o["waits"]:
                if w[0] == "__bar__":
                    for oid in w[1]:
                        wc = clock[oid]
                        c = [max(a, b) for a, b in zip(c, wc)]
                elif w[0] == "__op__":
                    wc = clock[w[1]]
                    c = [max(a, b) for a, b in zip(c, wc)]
                else:
                    s, v = w
                    if s not in sem_tl:
                        raise AssertionError(f"wait on never-inc'd sem {s} at {o['tag']}")
                    cums, oids = sem_tl[s]
                    k = bisect.bisect_left(cums, v)
                    if k >= len(cums):
                        raise AssertionError(
                            f"wait ({s},{v}) never satisfied at {o['tag']}")
                    wc = clock[oids[k]]
                    c = [max(a, b) for a, b in zip(c, wc)]
            c[eidx[o["eng"]]] = max(c[eidx[o["eng"]]], o["idx"])
            clock[o["id"]] = c

        def hb(x, y):
            return clock[y["id"]][eidx[x["eng"]]] >= x["idx"]

        qerrs = []
        for o in self.ops:
            for w in o["waits"]:
                if isinstance(w[0], str) and not w[0].startswith("__") \
                        and w[0] in sem_tl and self.sem_owner.get(w[0], "").endswith("q") is False:
                    pass
        for o in self.ops:
            for w in o["waits"]:
                s_, v = (w if not isinstance(w[0], str) or not w[0].startswith("__")
                         else (None, None))
                if s_ is None or s_ not in sem_tl:
                    continue
                cums, oids = sem_tl[s_]
                k = bisect.bisect_left(cums, v)
                if k >= len(cums):
                    continue
                top = self.ops[oids[k]]
                if "issue_op" not in top:
                    continue
                if k + 1 < len(oids):
                    nxt = self.ops[oids[k + 1]]
                    iss = self.ops[nxt["issue_op"]]
                    if not hb(o, iss):
                        qerrs.append(
                            f"UNSOUND DMA WAIT ({s_},{v}) at {o['tag']}: next "
                            f"dma {nxt['tag']} issued concurrently")
        if qerrs:
            seen = []
            for e in qerrs:
                if e not in seen:
                    seen.append(e)
            raise AssertionError("QWAITS:\n" + "\n".join(seen[:60]) +
                                 f"\n({len(seen)} unique)")

        bufs = {}
        errs = []
        for o in self.ops:
            for k in o["rd"]:
                st = bufs.setdefault(k, {"w": None, "r": []})
                if st["w"] is not None and not hb(st["w"], o):
                    errs.append(f"RAW {k}: read {o['tag']} !after write {st['w']['tag']}")
                st["r"].append(o)
            for k in o["wr"]:
                st = bufs.setdefault(k, {"w": None, "r": []})
                if st["w"] is not None and not hb(st["w"], o):
                    errs.append(f"WAW {k}: write {o['tag']} !after write {st['w']['tag']}")
                for r in st["r"]:
                    if r is not o and not hb(r, o):
                        errs.append(f"WAR {k}: write {o['tag']} !after read {r['tag']}")
                st["w"] = o
                st["r"] = [o] if k in o["rd"] else []
        if errs:
            seen = set()
            out = []
            for e in errs:
                if e not in seen:
                    seen.add(e)
                    out.append(e)
            raise AssertionError("RACES:\n" + "\n".join(out[:80]) +
                                 f"\n({len(out)} unique)")


def build(nc, mode="full"):
    es = ExitStack()
    E = Emit(nc, es)
    dram = nc.dram_tensor

    x0 = dram("x0", [NTA, DM, TA], F32, kind="ExternalInput").ap()
    w_in = dram("w_in", [DM, 2 * DI], F32, kind="ExternalInput").ap()
    w_x = dram("w_x", [16, 128, 33], F32, kind="ExternalInput").ap()
    w_out = dram("w_out", [DI, DM], F32, kind="ExternalInput").ap()
    l0cd = dram("l0cd", [64, 128, 128], F32, kind="ExternalInput").ap()
    conv_b = dram("conv_b", [128, 16], F32, kind="ExternalInput").ap()
    dt_w = dram("dt_w", [128, 16], F32, kind="ExternalInput").ap()
    dt_b = dram("dt_b", [128, 16], F32, kind="ExternalInput").ap()
    a_log = dram("a_log", [128, 16 * NST], F32, kind="ExternalInput").ap()
    d_res = dram("d_res", [128, 16], F32, kind="ExternalInput").ap()
    w1n = dram("w1n", [2 * DI, DM], F32, kind="ExternalInput").ap()
    w1t = dram("w1t", [DM, 2 * DI], F32, kind="ExternalInput").ap()
    w2n = dram("w2n", [DM, 2 * DI], F32, kind="ExternalInput").ap()
    w2t = dram("w2t", [2 * DI, DM], F32, kind="ExternalInput").ap()
    b1 = dram("b1", [128, 32], F32, kind="ExternalInput").ap()
    b2 = dram("b2", [128, 8], F32, kind="ExternalInput").ap()
    ones_f = dram("ones_f", [128, 2], F32, kind="ExternalInput").ap()
    ones_r = dram("ones_r", [1, 128], F32, kind="ExternalInput").ap()
    qwT = dram("qwT", [DM, DM], F32, kind="ExternalInput").ap()
    kwT = dram("kwT", [DM, DM], F32, kind="ExternalInput").ap()
    vwT = dram("vwT", [DM, DM], F32, kind="ExternalInput").ap()
    bwT = dram("bwT", [DM, 4], F32, kind="ExternalInput").ap()
    gm1T = dram("gm1T", [DM, 2 * DM], F32, kind="ExternalInput").ap()
    gm2T = dram("gm2T", [2 * DM, 16], F32, kind="ExternalInput").ap()
    gmb1 = dram("gmb1", [128, 16], F32, kind="ExternalInput").ap()
    gmb2 = dram("gmb2", [16, 1], F32, kind="ExternalInput").ap()
    qkvcd = dram("qkvcd", [96, 128, 128], F32, kind="ExternalInput").ap()
    fircd = dram("fircd", [304, 128, 128], F32, kind="ExternalInput").ap()
    oprojT = dram("oprojT", [DM, DM], F32, kind="ExternalInput").ap()
    onw_rep = dram("onw_rep", [128, DM], F32, kind="ExternalInput").ap()
    alpha_rep = dram("alpha_rep", [128, 16], F32, kind="ExternalInput").ap()
    maskstd = dram("maskstd", [128, 512], F32, kind="ExternalInput").ap()
    maskor = dram("maskor", [128, 512], F32, kind="ExternalInput").ap()
    maskal = dram("maskal", [128, 512], F32, kind="ExternalInput").ap()
    idrep4 = dram("idrep4", [128, 512], BF16, kind="ExternalInput").ap()
    id_bf = dram("id_bf", [128, 128], BF16, kind="ExternalInput").ap()
    bsel = dram("bsel", [4, 512], F32, kind="ExternalInput").ap()
    id_f32 = dram("id_f32", [128, 128], F32, kind="ExternalInput").ap()
    wupN = dram("wupN", [FFI, DM], F32, kind="ExternalInput").ap()
    wgateN = dram("wgateN", [FFI, DM], F32, kind="ExternalInput").ap()
    wdownN = dram("wdownN", [DM, FFI], F32, kind="ExternalInput").ap()

    yout = dram("yout", [NTA, DM, TA], F32, kind="ExternalOutput").ap()
    dbg = dram("dbg", [12, 128, 4096], F32, kind="ExternalOutput").ap() \
        if mode == "dbg" else None
    dbgb = dram("dbgb", [2, 128, 4096], BF16, kind="ExternalOutput").ap() \
        if mode == "dbg" else None

    xnbf_s = dram("xnbf_s", [NTA, DM, TA], BF16).ap()
    xpart_s = dram("xpart_s", [NTA, DM, TA], F32).ap()
    bc_b = dram("bc_b", [NTA, 2, NST, TA], F32).ap()
    w1s_s = dram("w1s_s", [DM, 2 * DI], BF16).ap()
    w2s_s = dram("w2s_s", [2 * DI, DM], BF16).ap()
    x1_s = dram("x1_s", [NTA, DM, TA], F32).ap()
    dbg_xn = dram("dbg_xn", [128, 8 * TA], BF16).ap()
    dbg_qf = dram("dbg_qf", [128, 24 * TA], F32).ap()
    qhatF = dram("qhatF", [DM, L], BF16).ap()
    khatF = dram("khatF", [DM, L], BF16).ap()
    kbF = dram("kbF", [DM, L], BF16).ap()
    khT = dram("khT", [NTB, 128, DM], BF16).ap()
    kbT = dram("kbT", [NTB, 128, DM], BF16).ap()
    vbT = dram("vbT", [NTB, 128, DM], BF16).ap()
    vT = dram("vT", [NTB, 128, DM], F32).ap()
    vF = dram("vF", [DM, 31 + L], BF16).ap()
    deltaT = dram("deltaT", [NTB, 128, DM], F32).ap()
    gateT = dram("gateT", [NTB, 128, 16], F32).ap()
    attn1_s = dram("attn1_s", [NTA, DM, TA], F32).ap()
    wupT_dq = dram("wupT_dq", [DM, FFI], BF16).ap()
    wgateT_dq = dram("wgateT_dq", [DM, FFI], BF16).ap()
    wdownT_dq = dram("wdownT_dq", [FFI, DM], BF16).ap()
    up_s = dram("up_s", [NTA, FFI, TA], F32).ap()
    gate_s = dram("gate_s", [NTA, FFI, TA], F32).ap()
    gs_s = dram("gs_s", [NTA, FFI, TA], F32).ap()
    op_s = dram("op_s", [NTA, FFI, TA], F32).ap()
    cc_in = [dram(f"cc{i}_in", [256], F32).ap() for i in range(4)]
    cc_out = [dram(f"cc{i}_out", [256], F32).ap() for i in range(4)]

    def sb(name, shape, dtyp=F32, stk=None):
        return (stk or es).enter_context(nc.sbuf_tensor(name, shape, dtyp))

    P = [es.enter_context(nc.psum_tensor(f"pbk{i}", [128, 512], F32))
         for i in range(8)]

    cb_t = sb("cb_t", [128, 16])
    dtw_t = sb("dtw_t", [128, 16])
    dtb_t = sb("dtb_t", [128, 16])
    acol_t = sb("acol_t", [128, 16 * NST])
    dres_t = sb("dres_t", [128, 16])
    b1_t = sb("b1_t", [128, 32])
    b2_t = sb("b2_t", [128, 8])
    a1_t = sb("a1_t", [128, 32])
    a2_t = sb("a2_t", [128, 8])
    ones_t = sb("ones_t", [128, 2])
    onesr_t = sb("onesr_t", [1, 128])
    eps_t = sb("eps_t", [128, 1])
    id_bf_t = sb("id_bf_t", [128, 128], BF16)
    idf_t = sb("idf_t", [128, 128], F32)
    sc_t = sb("sc_t", [128, 16])
    c64k_t = sb("c64k_t", [128, 1])
    rowc_t = sb("rowc_t", [1, 1024])
    ev_eps = E.op("v", lambda v: v.memset(eps_t[:], 1e-5), wr=["eps"], tag="eps-init")
    E.op("v", lambda v: v.memset(c64k_t[:], 65536.0), wr=["c64k"], tag="c64k-init")

    for (d, sr, nm) in [(cb_t, conv_b, "cb"), (dtw_t, dt_w, "dtw"),
                        (dtb_t, dt_b, "dtb"), (dres_t, d_res, "dres"),
                        (b1_t, b1, "b1"), (b2_t, b2, "b2"),
                        (ones_t, ones_f, "ones"), (onesr_t, ones_r, "onesr")]:
        E.dma("s", d[:], sr, wr=[nm], tag=f"ld-{nm}", q="sdS")
    ev0 = E.dma("s", acol_t[:], a_log, wr=["acol"], tag="ld-acol", q="sdS")
    evS = ev0
    eva = E.op("a", lambda a: a.activation(out=acol_t[:], in_=acol_t[:], func=AF.Exp),
               wait=[ev0], rd=["acol"], wr=["acol"], tag="acol-exp")
    E.op("v", lambda v: v.tensor_scalar(out=acol_t[:], in0=acol_t[:],
                                        scalar1=-1.0, scalar2=None, op0=ALU.mult),
         wait=[eva], rd=["acol"], wr=["acol"], tag="acol-neg")

    p0 = ExitStack()
    p0buf = sb("p0buf", [128, 4096], stk=p0)
    p0bf = sb("p0bf", [128, 4096], BF16, stk=p0)
    p0col = sb("p0col", [128, 1], stk=p0)

    def prep_sign(nat, natT, ncn, natF, a_dst, adk, nct, tF, s_dst, sdk):
        ev3 = None
        for c in range(ncn):
            ev = E.dma("s", p0buf[:, :natF], nat[c * 128:(c + 1) * 128, :],
                       wait=[ev3], wr=["p0buf"], tag=f"p0-ld-{adk}{c}", q="sdP")
            ev2 = E.op("v", lambda v: v.tensor_reduce(
                out=p0col[:], in_=p0buf[:, :natF], axis=AXX, op=ALU.add,
                apply_absolute_value=True), wait=[ev],
                rd=["p0buf"], wr=["p0col"], tag=f"p0-red-{adk}{c}")
            ev3 = E.op("v", lambda v, c=c: v.tensor_scalar(
                out=a_dst[:, c:c + 1], in0=p0col[:], scalar1=1.0 / natF,
                scalar2=None, op0=ALU.mult), wait=[ev2],
                rd=["p0col"], wr=[adk], tag=f"p0-al-{adk}{c}")
        last = None
        for c in range(nct):
            ev = E.dma("s", p0buf[:, :tF], natT[c * 128:(c + 1) * 128, :],
                       wait=[ev3, last], wr=["p0buf"], tag=f"p0-ldT-{sdk}{c}", q="sdP")
            ev2 = E.op("a", lambda a: a.activation(
                out=p0bf[:, :tF], in_=p0buf[:, :tF], func=AF.Sign), wait=[ev],
                rd=["p0buf"], wr=["p0bf"], tag=f"p0-sgn-{sdk}{c}")
            last = E.dma("s", s_dst[c * 128:(c + 1) * 128, :], p0bf[:, :tF],
                         wait=[ev2], rd=["p0bf"], wr=[sdk], tag=f"p0-st-{sdk}{c}", q="sdP")
        return last

    ev_w1 = prep_sign(w1n, w1t, 32, DM, a1_t, "a1", 8, 2 * DI, w1s_s, "d:w1s")
    ev_w2 = prep_sign(w2n, w2t, 8, 2 * DI, a2_t, "a2", 32, DM, w2s_s, "d:w2s")
    ev_w1 = E.ev("sdP")
    ev_w2 = ev_w1
    p0.close()
    E.barrier()

    pa = ExitStack()
    w_in_t = sb("w_in_t", [128, 8 * 4096], BF16, stk=pa)
    w_x_t = sb("w_x_t", [128, 16 * 33], BF16, stk=pa)
    l0cd_t = sb("l0cd_t", [128, 64 * 128], BF16, stk=pa)
    wout_db = sb("wout_db", [128, 2 * DM], BF16, stk=pa)
    x_t = sb("x_t", [128, 8 * TA], stk=pa)
    xnbf_t = sb("xnbf_t", [128, 8 * TA], BF16, stk=pa)
    xm_st = sb("xm_st", [128, 2 * (TA + 3)], BF16, stk=pa)
    xtail_t = sb("xtail_t", [128, 16 * 3], BF16, stk=pa)
    silz_t = sb("silz_t", [128, 16 * TA], stk=pa)
    xc_t = sb("xc_t", [128, 16 * TA], stk=pa)
    xcbf_st = sb("xcbf_st", [128, 2 * TA], BF16, stk=pa)
    ssm_t = sb("ssm_t", [33, TA], stk=pa)
    delta_t = sb("delta_t", [128, 2 * TA], stk=pa)
    dx_t = sb("dx_t", [128, 2 * TA], stk=pa)
    sp_t = sb("sp_t", [128, 4 * TA], stk=pa)
    da_t = sb("da_t", [128, 16 * TA], stk=pa)
    dbx_t = sb("dbx_t", [128, 8 * TA], stk=pa)
    h_t = sb("h_t", [128, 8 * TA], stk=pa)
    brep_t = sb("brep_t", [128, NST * TA], BF16, stk=pa)
    crep_t = sb("crep_t", [128, NST * TA], BF16, stk=pa)
    hst_t = sb("hst_t", [128, 16 * NST], stk=pa)
    t8_t = sb("t8_t", [128, 16], stk=pa)
    y_t = sb("y_t", [128, 2 * TA], stk=pa)
    y2bf_t = sb("y2bf_t", [128, 2 * TA], BF16, stk=pa)
    row_t = sb("row_t", [1, 4 * TA], stk=pa)

    E.op("v", lambda v: v.memset(hst_t[:], 0.0), wr=["hst"], tag="hst-init")
    E.op("v", lambda v: v.memset(xtail_t[:], 0.0), wr=["xtail"], tag="xtail-init")

    ev_wil = None
    for c in range(8):
        ev_wil = E.dma("p", w_in_t[:, c * 4096:(c + 1) * 4096],
                       w_in[c * 128:(c + 1) * 128, :], wr=["w_in"],
                       tag=f"ld-w_in{c}", q="pdW")
    ev_wx = E.dma("p", w_x_t[:],
                  bass.AP(w_x.tensor, 0, [[33, 128], [128 * 33, 16], [1, 33]]),
                  wr=["wxw"], tag="ld-wx", q="pdW")
    ev_cd = E.dma("p", l0cd_t[:],
                  bass.AP(l0cd.tensor, 0, [[128, 128], [128 * 128, 64], [1, 128]]),
                  wr=["l0cd"], tag="ld-l0cd", q="pdW")
    evW = ev_cd
    ev_wil = evW
    ev_wx = evW
    ev_cd = evW

    WOB = {0: 5, 1: 5, 2: 6, 3: 6, 4: 0, 5: 0, 6: 1, 7: 1}
    prev = {}
    T = TA
    for it in range(NTA):
        tg = f"t{it}"
        ev_x = E.dma("s", x_t[:], fmaj(x0[it], 8, TA),
                     wait=[prev.get("x_used")], wr=["x"], tag=f"{tg}-ldx", q="sdx")
        b0 = P[0]
        mm2ev = {}
        ev_sums = None
        for c in range(8):
            ev_sq = E.op("a", lambda a, c=c, sp_t=sp_t: a.activation(
                out=sp_t[:, (c % 2) * T:(c % 2) * T + T],
                in_=x_t[:, c * T:(c + 1) * T], func=AF.Square),
                wait=[ev_x, mm2ev.get(c % 2), prev.get("sp_free")],
                rd=["x"], wr=[f"sp{c % 2}"], tag=f"{tg}-sq{c}")
            E.op("t", lambda t, c=c: t.matmul(
                out=b0[0:1, 0:T], lhsT=ones_t[:, 0:1],
                rhs=x_t[:, c * T:(c + 1) * T], start=(c == 0), stop=False),
                wait=[ev_x, prev.get("b0_free"), evS] if c == 0 else [],
                rd=["x", "ones"], wr=["P0"], tag=f"{tg}-mmx{c}")
            ev_sums = E.op("t", lambda t, c=c, sp_t=sp_t: t.matmul(
                out=b0[0:1, T:2 * T], lhsT=ones_t[:, 0:1],
                rhs=sp_t[:, (c % 2) * T:(c % 2) * T + T],
                start=False, stop=(c == 7)),
                wait=[ev_sq], inc=["ts"], rd=[f"sp{c % 2}", "ones"], wr=["P0"],
                tag=f"{tg}-mmsq{c}")
            mm2ev[c % 2] = ev_sums
        e1 = E.op("v", lambda v, row_t=row_t: v.tensor_scalar(
            out=row_t[0:1, 0:2 * T], in0=b0[0:1, 0:2 * T], scalar1=1.0 / DM,
            scalar2=None, op0=ALU.mult), wait=[ev_sums],
            rd=["P0"], wr=["row"], tag=f"{tg}-ln1")
        prev["b0_rows_done"] = e1
        e2 = E.op("v", lambda v, row_t=row_t: v.tensor_mul(
            out=row_t[0:1, 2 * T:3 * T], in0=row_t[0:1, 0:T],
            in1=row_t[0:1, 0:T]), wait=[e1], rd=["row"], wr=["row"],
            tag=f"{tg}-ln2")
        e3 = E.op("v", lambda v, row_t=row_t: v.tensor_sub(
            out=row_t[0:1, 2 * T:3 * T], in0=row_t[0:1, T:2 * T],
            in1=row_t[0:1, 2 * T:3 * T]), wait=[e2], rd=["row"], wr=["row"],
            tag=f"{tg}-ln3")
        e4a = E.op("a", lambda a, row_t=row_t: a.activation(
            out=row_t[0:1, 3 * T:4 * T], in_=row_t[0:1, 2 * T:3 * T],
            func=AF.Sqrt, bias=eps_t[0:1, 0:1]), wait=[e3, ev_eps],
            rd=["row", "eps"], wr=["row"], tag=f"{tg}-ln4")
        e4 = E.op("v", lambda v, row_t=row_t: v.reciprocal(
            out=row_t[0:1, 3 * T:4 * T], in_=row_t[0:1, 3 * T:4 * T]),
            wait=[e4a], rd=["row"], wr=["row"], tag=f"{tg}-ln5")
        e5 = E.op("v", lambda v, row_t=row_t: v.tensor_mul(
            out=row_t[0:1, 2 * T:3 * T], in0=row_t[0:1, 0:T],
            in1=row_t[0:1, 3 * T:4 * T]), wait=[e4], rd=["row"], wr=["row"],
            tag=f"{tg}-ln6")
        e6 = E.op("v", lambda v, row_t=row_t: v.tensor_copy(
            out=row_t[0:1, 0:T], in_=row_t[0:1, 3 * T:4 * T]), wait=[e5],
            rd=["row"], wr=["row"], tag=f"{tg}-ln7")
        e7 = E.op("v", lambda v, row_t=row_t: v.tensor_scalar(
            out=row_t[0:1, T:2 * T], in0=row_t[0:1, 2 * T:3 * T], scalar1=-1.0,
            scalar2=None, op0=ALU.mult), wait=[e6], rd=["row"], wr=["row"],
            tag=f"{tg}-ln8")
        b1p = P[1]
        ebc = E.op("t", lambda t, row_t=row_t: t.matmul(
            out=b1p[:, 0:2 * T], lhsT=onesr_t[0:1, :], rhs=row_t[0:1, 0:2 * T],
            start=True, stop=True), wait=[e7, prev.get("b1_free"), evS], inc=["ts"],
            rd=["row", "onesr"], wr=["P1"], tag=f"{tg}-lnbc")
        ev_xn = None
        for c in range(8):
            em = E.op("v", lambda v, c=c, sp_t=sp_t: v.tensor_mul(
                out=sp_t[:, (c % 2) * T:(c % 2) * T + T],
                in0=x_t[:, c * T:(c + 1) * T], in1=b1p[:, 0:T]),
                wait=[ebc, mm2ev.get(c % 2), prev.get("xnbf_stash")],
                rd=["x", "P1"], wr=[f"sp{c % 2}"], tag=f"{tg}-xnm{c}")
            ev_xn = E.op("v", lambda v, c=c, sp_t=sp_t: v.tensor_tensor(
                out=xnbf_t[:, c * T:(c + 1) * T],
                in0=sp_t[:, (c % 2) * T:(c % 2) * T + T], in1=b1p[:, T:2 * T],
                op=ALU.add), wait=[em], rd=[f"sp{c % 2}", "P1"], wr=["xnbf"],
                tag=f"{tg}-xna{c}")
            mm2ev[c % 2] = ev_xn
        prev["b1_free"] = ev_xn
        prev["sp_free"] = ev_xn
        prev["xnbf_stash"] = E.dma("s", fmaj(xnbf_s[it], 8, TA), xnbf_t[:],
                                   wait=[ev_xn], rd=["xnbf"],
                                   wr=[f"d:xn:{it}"], tag=f"{tg}-stxn", q="sdn")

        ev_ssm_mm = None
        for oc in range(32):
            bank = P[oc % 2]
            bkk = f"P{oc % 2}"
            ev_mm = None
            for kc in range(8):
                w0 = []
                if kc == 0:
                    w0 = [ev_xn, prev.get(("xzf", oc % 2)), ev_wil]
                    if oc == 0:
                        w0.append(prev.get("b0_rows_done"))
                    if oc == 1:
                        w0.append(prev.get("b1_free"))
                ev_mm = E.op("t", lambda t, oc=oc, kc=kc, bank=bank: t.matmul(
                    out=bank[:, 0:T],
                    lhsT=w_in_t[:, kc * 4096 + oc * 128:kc * 4096 + oc * 128 + 128],
                    rhs=xnbf_t[:, kc * T:(kc + 1) * T],
                    start=(kc == 0), stop=(kc == 7)),
                    wait=w0, inc=["ts"] if kc == 7 else [],
                    rd=["w_in", "xnbf"], wr=[bkk], tag=f"{tg}-xz{oc}.{kc}")
            if oc < 16:
                ch = oc
                st0 = (ch % 2) * (TA + 3)
                stk_ = f"xmst{ch % 2}"
                ev_ev = E.op("a", lambda a, st0=st0, bank=bank: a.activation(
                    out=xm_st[:, st0 + 3:st0 + 3 + T], in_=bank[:, 0:T],
                    func=AF.Copy), wait=[ev_mm, prev.get(("xmst", ch % 2))],
                    rd=[bkk], wr=[stk_], tag=f"{tg}-xme{ch}")
                prev[("xzf", oc % 2)] = ev_ev
                ev_h = E.op("v", lambda v, st0=st0, ch=ch: v.tensor_copy(
                    out=xm_st[:, st0:st0 + 3], in_=xtail_t[:, ch * 3:ch * 3 + 3]),
                    wait=[ev_ev], rd=["xtail"], wr=[stk_], tag=f"{tg}-xmh{ch}")
                ev_tl = E.op("v", lambda v, st0=st0, ch=ch: v.tensor_copy(
                    out=xtail_t[:, ch * 3:ch * 3 + 3],
                    in_=xm_st[:, st0 + T:st0 + T + 3]), wait=[ev_h],
                    rd=[stk_], wr=["xtail"], tag=f"{tg}-xmt{ch}")
                cbk = P[2 + ch % 2]
                cbkk = f"P{2 + ch % 2}"
                ev_cv = None
                for k in range(4):
                    ev_cv = E.op("t", lambda t, ch=ch, k=k, st0=st0, cbk=cbk: t.matmul(
                        out=cbk[:, 0:T],
                        lhsT=l0cd_t[:, (ch * 4 + k) * 128:(ch * 4 + k) * 128 + 128],
                        rhs=xm_st[:, st0 + k:st0 + k + T],
                        start=(k == 0), stop=(k == 3)),
                        wait=[ev_tl, ev_cd, prev.get(("cvf", ch % 2))] if k == 0 else [],
                        inc=["ts"] if k == 3 else [],
                        rd=["l0cd", stk_], wr=[cbkk], tag=f"{tg}-cv{ch}.{k}")
                prev[("xmst", ch % 2)] = ev_cv
                ev_xc = E.op("a", lambda a, ch=ch, cbk=cbk: a.activation(
                    out=xc_t[:, ch * T:(ch + 1) * T], in_=cbk[:, 0:T],
                    func=AF.Silu, bias=cb_t[:, ch:ch + 1]),
                    wait=[ev_cv, prev.get(("xcu", ch)), evS],
                    rd=[cbkk, "cb"], wr=[f"xc{ch}"], tag=f"{tg}-xc{ch}")
                prev[("cvf", ch % 2)] = ev_xc
                ev_cb = E.op("v", lambda v, ch=ch: v.tensor_copy(
                    out=xcbf_st[:, (ch % 2) * T:(ch % 2) * T + T],
                    in_=xc_t[:, ch * T:(ch + 1) * T]),
                    wait=[ev_xc, prev.get(("xcbf", ch % 2))],
                    rd=[f"xc{ch}"], wr=[f"xcbf{ch % 2}"], tag=f"{tg}-xcb{ch}")
                ev_ssm_mm = E.op("t", lambda t, ch=ch: t.matmul(
                    out=P[4][0:33, 0:T],
                    lhsT=w_x_t[:, ch * 33:(ch + 1) * 33],
                    rhs=xcbf_st[:, (ch % 2) * T:(ch % 2) * T + T],
                    start=(ch == 0), stop=(ch == 15)),
                    wait=[ev_cb, ev_wx] + ([prev.get("b4_free")] if ch == 0 else []),
                    inc=["ts"], rd=["wxw", f"xcbf{ch % 2}"], wr=["P4"],
                    tag=f"{tg}-ssm{ch}")
                prev[("xcbf", ch % 2)] = ev_ssm_mm
            else:
                ch = oc - 16
                ev_z = E.op("a", lambda a, ch=ch, bank=bank: a.activation(
                    out=silz_t[:, ch * T:(ch + 1) * T], in_=bank[:, 0:T],
                    func=AF.Silu), wait=[ev_mm, prev.get(("szu", ch))],
                    rd=[bkk], wr=[f"silz{ch}"], tag=f"{tg}-sz{ch}")
                prev[("xzf", oc % 2)] = ev_z

        ev_se = E.op("v", lambda v: v.tensor_copy(out=ssm_t[:], in_=P[4][0:33, 0:T]),
                     wait=[ev_ssm_mm, prev.get("ssm_used"),
                           prev.get("ssm_used_pe")],
                     rd=["P4"], wr=["ssm"], tag=f"{tg}-sse")
        ev_bb = E.dma("s", bc_b[it, 0], ssm_t[1:17, :], wait=[ev_se],
                      rd=["ssm"], wr=[f"d:bcB:{it}"], tag=f"{tg}-bB", q="sdbB")
        ev_cc = E.dma("s", bc_b[it, 1], ssm_t[17:33, :], wait=[ev_se],
                      rd=["ssm"], wr=[f"d:bcC:{it}"], tag=f"{tg}-bC", q="sdbC")
        ev_rb = E.op("t", lambda t: t.matmul(
            out=P[4][:, 256:256 + T], lhsT=onesr_t[0:1, :], rhs=ssm_t[0:1, :],
            start=True, stop=True), wait=[ev_se, evS], inc=["ts"],
            rd=["ssm", "onesr"], wr=["P4"], tag=f"{tg}-rawbc")
        ev_br = E.dma("p", brep_t[:],
                      bass.AP(bc_b.tensor, bc_b[it, 0].offset, [[0, 128], [1, NST * TA]]),
                      wait=[ev_bb, prev.get("brep_used")],
                      rd=[f"d:bcB:{it}"], wr=["brep"], tag=f"{tg}-brep", q="sdr")
        ev_cr = E.dma("p", crep_t[:],
                      bass.AP(bc_b.tensor, bc_b[it, 1].offset, [[0, 128], [1, NST * TA]]),
                      wait=[ev_cc, prev.get("crep_used")],
                      rd=[f"d:bcC:{it}"], wr=["crep"], tag=f"{tg}-crep", q="sdc")

        prev[("sp0_scan", 0)] = ev_xn
        prev[("sp1_scan", 0)] = ev_xn
        last_ed = {}
        last_db = None
        last_q = None
        for ch in range(16):
            par = ch % 2
            s0 = par * 2 * T
            s1 = par * 2 * T + T
            dl = par * T
            k0 = f"sp0p{par}" if par else "sp0"
            k1 = f"sp1p{par}" if par else "sp1"
            kd = f"delta{par}"
            ea1 = E.op("a", lambda a, ch=ch, s0=s0, sp_t=sp_t: a.activation(
                out=sp_t[:, s0:s0 + T], in_=P[4][:, 256:256 + T],
                func=AF.Identity,
                scale=dtw_t[:, ch:ch + 1], bias=dtb_t[:, ch:ch + 1]),
                wait=[ev_rb, prev.get(("sp0_scan", par)), evS],
                rd=["P4", "dtw", "dtb"], wr=[k0], tag=f"{tg}-d1.{ch}")
            ea2 = E.op("a", lambda a, s0=s0, s1=s1, sp_t=sp_t: a.activation(
                out=sp_t[:, s1:s1 + T], in_=sp_t[:, s0:s0 + T], func=AF.Abs),
                wait=[ea1, prev.get(("sp1_scan", par))],
                rd=[k0], wr=[k1], tag=f"{tg}-d2.{ch}")
            ea3 = E.op("a", lambda a, s1=s1, sp_t=sp_t: a.activation(
                out=sp_t[:, s1:s1 + T], in_=sp_t[:, s1:s1 + T], func=AF.Exp,
                scale=-1.0),
                wait=[ea2], rd=[k1], wr=[k1], tag=f"{tg}-d3.{ch}")
            ea4 = E.op("a", lambda a, s1=s1, sp_t=sp_t: a.activation(
                out=sp_t[:, s1:s1 + T], in_=sp_t[:, s1:s1 + T], func=AF.Ln,
                bias=1.0),
                wait=[ea3], rd=[k1], wr=[k1], tag=f"{tg}-d4.{ch}")
            ea5 = E.op("a", lambda a, s0=s0, sp_t=sp_t: a.activation(
                out=sp_t[:, s0:s0 + T], in_=sp_t[:, s0:s0 + T], func=AF.Relu),
                wait=[ea4], rd=[k0], wr=[k0], tag=f"{tg}-d5.{ch}")
            ed = E.op("v", lambda v, s0=s0, s1=s1, dl=dl, sp_t=sp_t: v.tensor_add(
                out=delta_t[:, dl:dl + T], in0=sp_t[:, s0:s0 + T],
                in1=sp_t[:, s1:s1 + T]),
                wait=[ea5, prev.get(("delta_used", par)), evS],
                rd=[k0, k1], wr=[kd], tag=f"{tg}-dd.{ch}")
            prev[("sp0_scan", par)] = ed
            prev[("sp1_scan", par)] = ed
            last_ed[par] = ed
            edx = E.op("v", lambda v, ch=ch, dl=dl: v.tensor_mul(
                out=dx_t[:, dl:dl + T], in0=delta_t[:, dl:dl + T],
                in1=xc_t[:, ch * T:(ch + 1) * T]),
                wait=[ed], rd=[kd, f"xc{ch}"], wr=[f"dx{par}"],
                tag=f"{tg}-dx.{ch}")
            ev_y = None
            for nh in range(2):
                nb = nh * 8
                d8 = par * 8 * T
                ev_da = None
                for n in range(8):
                    ev_da = E.op("a", lambda a, ch=ch, n=n, nb=nb,
                                 d8=d8, dl=dl: a.activation(
                        out=da_t[:, d8 + n * T:d8 + (n + 1) * T],
                        in_=delta_t[:, dl:dl + T], func=AF.Exp,
                        scale=acol_t[:, ch * 16 + nb + n:ch * 16 + nb + n + 1]),
                        wait=[ed] + ([prev.get(("da_used", par))]
                                     if n == 0 else []),
                        rd=[kd, "acol"], wr=[f"da{par}"],
                        tag=f"{tg}-da.{ch}.{nb}.{n}")
                ev_db = E.op("v", lambda v, nb=nb, dl=dl: v.tensor_tensor(
                    out=sap(dbx_t, 0, [[T, 8], [1, T]]),
                    in0=sap(dx_t, dl, [[0, 8], [1, T]]),
                    in1=sap(brep_t, nb * T, [[T, 8], [1, T]]),
                    op=ALU.mult), wait=[edx, ev_br],
                    rd=[f"dx{par}", "brep"], wr=["dbx"], tag=f"{tg}-db.{ch}.{nb}")
                last_db = ev_db
                ep1 = E.op("v", lambda v, ch=ch, nb=nb, d8=d8,
                           par=par: v.tensor_mul(
                    out=t8_t[:, par * 8:par * 8 + 8],
                    in0=sap(da_t, d8, [[T, 8]]),
                    in1=hst_t[:, ch * 16 + nb:ch * 16 + nb + 8]),
                    wait=[ev_da], rd=[f"da{par}", "hst"], wr=[f"t8{par}"],
                    tag=f"{tg}-p1.{ch}.{nb}")
                ep2 = E.op("v", lambda v, par=par: v.tensor_add(
                    out=sap(dbx_t, 0, [[T, 8]]),
                    in0=sap(dbx_t, 0, [[T, 8]]),
                    in1=t8_t[:, par * 8:par * 8 + 8]),
                    wait=[ep1], rd=["dbx", f"t8{par}"], wr=["dbx"],
                    tag=f"{tg}-p2.{ch}.{nb}")
                ep3 = E.op("v", lambda v, d8=d8: v.tensor_scalar(
                    out=sap(da_t, d8, [[T, 8]]), in0=sap(da_t, d8, [[T, 8]]),
                    scalar1=0.0, scalar2=None, op0=ALU.mult),
                    wait=[ep1], rd=[f"da{par}"], wr=[f"da{par}"],
                    tag=f"{tg}-p3.{ch}.{nb}")
                ev_sc = E.op("v", lambda v, ch=ch, nb=nb,
                             d8=d8: v.tensor_tensor_scan(
                    out=h_t[:, 0:8 * T],
                    data0=da_t[:, d8:d8 + 8 * T],
                    data1=dbx_t[:, 0:8 * T],
                    initial=hst_t[:, ch * 16 + nb:ch * 16 + nb + 1],
                    op0=ALU.mult, op1=ALU.add),
                    wait=[ep2, ep3],
                    rd=[f"da{par}", "dbx", "hst"], wr=["h"],
                    tag=f"{tg}-sc.{ch}.{nb}")
                E.op("v", lambda v, ch=ch, nb=nb: v.tensor_copy(
                    out=hst_t[:, ch * 16 + nb:ch * 16 + nb + 8],
                    in_=sap(h_t, T - 1, [[T, 8]])), wait=[ev_sc],
                    rd=["h"], wr=["hst"], tag=f"{tg}-hc.{ch}.{nb}")
                ev_q = E.op("v", lambda v, nb=nb, d8=d8: v.tensor_mul(
                    out=da_t[:, d8:d8 + 8 * T], in0=h_t[:, 0:8 * T],
                    in1=crep_t[:, nb * T:(nb + 8) * T]),
                    wait=[ev_cr],
                    rd=["h", "crep"], wr=[f"da{par}"], tag=f"{tg}-q.{ch}.{nb}")
                last_q = ev_q
                ev_y = E.op("v", lambda v, nh=nh, d8=d8: v.tensor_reduce(
                    out=y_t[:, nh * T:(nh + 1) * T],
                    in_=sap(da_t, d8, [[1, T], [T, 8]]),
                    axis=AXX, op=ALU.add), wait=[ev_q],
                    rd=[f"da{par}"], wr=["y"], tag=f"{tg}-yr.{ch}.{nb}")
                prev[("da_used", par)] = ev_y
            prev[("delta_used", par)] = ev_da
            ey = E.op("v", lambda v: v.tensor_add(
                out=y_t[:, 0:T], in0=y_t[:, 0:T], in1=y_t[:, T:2 * T]),
                wait=[ev_y], rd=["y"], wr=["y"], tag=f"{tg}-ya.{ch}")
            ey2 = E.op("v", lambda v, ch=ch: v.scalar_tensor_tensor(
                out=y_t[:, T:2 * T], in0=xc_t[:, ch * T:(ch + 1) * T],
                scalar=dres_t[:, ch:ch + 1], in1=y_t[:, 0:T],
                op0=ALU.mult, op1=ALU.add), wait=[ey, evS],
                rd=[f"xc{ch}", "dres", "y"], wr=["y"], tag=f"{tg}-y2.{ch}")
            prev[("xcu", ch)] = ey2
            ey3 = E.op("v", lambda v, ch=ch, par=par: v.tensor_mul(
                out=y2bf_t[:, par * T:(par + 1) * T], in0=y_t[:, T:2 * T],
                in1=silz_t[:, ch * T:(ch + 1) * T]),
                wait=[ey2, prev.get(("y2_used", par))],
                rd=["y", f"silz{ch}"], wr=[f"y2{par}"], tag=f"{tg}-y3.{ch}")
            prev[("szu", ch)] = ey3
            ev_wl = E.dma("p", wout_db[:, (ch % 2) * DM:(ch % 2) * DM + DM],
                          w_out[ch * 128:(ch + 1) * 128, :],
                          wait=[prev.get(("wou", ch % 2))],
                          wr=[f"wod{ch % 2}"], tag=f"{tg}-wol.{ch}", q=f"pdw{ch % 2}")
            ev_mo = None
            for m in range(8):
                bk = P[WOB[m]]
                col = (m % 2) * 256
                w0 = []
                if m == 0:
                    w0 = [ev_wl, ey3]
                    if ch == 0:
                        w0 += [prev.get("wo_evac"), prev.get(("xzf", 0)),
                               prev.get(("xzf", 1))]
                ev_mo = E.op("t", lambda t, ch=ch, m=m, bk=bk, col=col: t.matmul(
                    out=bk[:, col:col + T],
                    lhsT=wout_db[:, (ch % 2) * DM + m * 128:(ch % 2) * DM + m * 128 + 128],
                    rhs=y2bf_t[:, (ch % 2) * T:(ch % 2) * T + T],
                    start=(ch == 0 and m % 2 == 0),
                    stop=(ch == 15 and m % 2 == 1)),
                    wait=w0, inc=["ts"] if m == 7 else [],
                    rd=[f"wod{ch % 2}", f"y2{ch % 2}"], wr=[f"P{WOB[m]}"],
                    tag=f"{tg}-wo.{ch}.{m}")
            prev[("wou", ch % 2)] = ev_mo
            prev[("y2_used", ch % 2)] = ev_mo
        if mode == "dbg" and it == 0:
            E.dma("s", dbgb[0][:, 0:2048], xnbf_t[:], wait=[ev_xn], rd=["xnbf"], tag="dbg0", q="sdd")
            E.dma("s", dbg[0], xc_t[:], wait=[E.ev("as")], rd=[], tag="dbg1", q="sdd")
            E.dma("s", dbg[1], silz_t[:], wait=[E.ev("as")], rd=[], tag="dbg2", q="sdd")
            E.dma("s", dbg[2][0:33, 0:TA], ssm_t[:], wait=[ev_se], rd=["ssm"], tag="dbg3", q="sdd")
            E.dma("s", dbg[3], brep_t[:], wait=[ev_br], rd=["brep"], tag="dbg4", q="sdd")
            E.dma("s", dbg[4], crep_t[:], wait=[ev_cr], rd=["crep"], tag="dbg5", q="sdd")
            E.dma("s", dbg[5][:, 0:TA], delta_t[:], wait=[E.ev("vs")], rd=[], tag="dbg6", q="sdd")
            E.dma("s", dbg[6][:, 0:2048], da_t[:], wait=[E.ev("vs")], rd=[], tag="dbg7", q="sdd")
            E.dma("s", dbg[7][:, 0:2048], dbx_t[:], wait=[E.ev("vs")], rd=[], tag="dbg8", q="sdd")
            E.dma("s", dbg[8][:, 0:2048], h_t[:], wait=[E.ev("vs")], rd=[], tag="dbg9", q="sdd")
            E.dma("s", dbg[9][:, 0:512], y_t[:], wait=[E.ev("vs")], rd=[], tag="dbg10", q="sdd")
            E.dma("s", dbg[11][:, 0:256], hst_t[:], wait=[E.ev("vs")], rd=[], tag="dbg11", q="sdd")
        ev_ev = None
        for m in range(8):
            bk = P[WOB[m]]
            col = (m % 2) * 256
            ev_ev = E.op("v", lambda v, m=m, bk=bk, col=col: v.tensor_tensor(
                out=x_t[:, m * T:(m + 1) * T], in0=bk[:, col:col + T],
                in1=x_t[:, m * T:(m + 1) * T], op=ALU.add), wait=[ev_mo],
                rd=[f"P{WOB[m]}", "x"], wr=["x"], tag=f"{tg}-oev{m}")
        prev["wo_evac"] = ev_ev
        prev["b0_free"] = ev_ev
        prev["b1_free"] = ev_ev
        prev[("xzf", 0)] = ev_ev
        prev[("xzf", 1)] = ev_ev
        xp_dest = yout if mode == "xpart" else xpart_s
        ev_out = E.dma("s", fmaj(xp_dest[it], 8, TA), x_t[:], wait=[ev_ev],
                       rd=["x"], wr=[f"d:xp:{it}"], tag=f"{tg}-stxp", q="sdo")
        prev["x_used"] = ev_out
        prev["ssm_used"] = ev_cc
        prev["ssm_used_pe"] = ev_rb
        prev["sp_free"] = last_ed[0]
        prev["brep_used"] = last_db
        prev["crep_used"] = last_q
        prev["b4_free"] = E.ev("as")

    ev_sdn_tot = E.ev("sdn")
    ev_sdo_tot = E.ev("sdo")
    pa.close()
    E.barrier()

    if mode == "xpart":
        qsems = [k for k in E.cnt if k in E.sem_owner and E.sem_owner[k].endswith("q")]
        E.op("s", lambda e: e.nop(), wait=[E.ev(k) for k in qsems], inc=[])
        E.verify()
        _finish(nc, E, es)
        return nc

    pf = ExitStack()
    w1s_t = sb("w1s_t", [128, 8 * 4096], BF16, stk=pf)
    w2s_t = sb("w2s_t", [128, 32 * 1024], BF16, stk=pf)
    xnbf2_t = sb("xnbf2_t", [128, 8 * TA], BF16, stk=pf)
    xp_t = sb("xp_t", [128, 8 * TA], stk=pf)
    hbf_t = sb("hbf_t", [128, 32 * TA], BF16, stk=pf)
    sp2_t = sb("sp2_t", [128, 2 * TA], stk=pf)

    ev_w1l = None
    for c in range(8):
        ev_w1l = E.dma("s", w1s_t[:, c * 4096:(c + 1) * 4096],
                       w1s_s[c * 128:(c + 1) * 128, :],
                       wait=[ev_w1] if c == 0 else (),
                       rd=["d:w1s"], wr=["w1s"], tag=f"ld-w1s{c}", q="sdw1")
    ev_w2l = None
    for c in range(32):
        ev_w2l = E.dma("s", w2s_t[:, c * 1024:(c + 1) * 1024],
                       w2s_s[c * 128:(c + 1) * 128, :],
                       wait=[ev_w2] if c == 0 else (),
                       rd=["d:w2s"], wr=["w2s"], tag=f"ld-w2s{c}", q="sdw2")

    FOB = {0: 4, 1: 4, 2: 5, 3: 5, 4: 6, 5: 6, 6: 7, 7: 7}
    prev = {}
    for it in range(NTA):
        tg = f"f{it}"
        ev_xn = E.dma("s", xnbf2_t[:], fmaj(xnbf_s[it], 8, TA),
                      wait=[prev.get("xn_used"), ev_sdn_tot],
                      rd=[f"d:xn:{it}"], wr=["xnbf2"], tag=f"{tg}-ldxn", q="sdn2")
        ev_xp = E.dma("s", xp_t[:], fmaj(xpart_s[it], 8, TA),
                      wait=[prev.get("xp_used"), ev_sdo_tot],
                      rd=[f"d:xp:{it}"], wr=["xp"], tag=f"{tg}-ldxp", q="sdp2")
        ev_hl = {}
        for oc in range(32):
            bank = P[oc % 4]
            ev_mm = None
            for kc in range(8):
                ev_mm = E.op("t", lambda t, oc=oc, kc=kc, bank=bank: t.matmul(
                    out=bank[:, 0:TA],
                    lhsT=w1s_t[:, kc * 4096 + oc * 128:kc * 4096 + oc * 128 + 128],
                    rhs=xnbf2_t[:, kc * TA:(kc + 1) * TA],
                    start=(kc == 0), stop=(kc == 7)),
                    wait=[ev_xn, prev.get(("ff", oc % 4)), ev_w1l] if kc == 0 else [],
                    inc=["ts"] if kc == 7 else [],
                    rd=["w1s", "xnbf2"], wr=[f"P{oc % 4}"], tag=f"{tg}-h{oc}.{kc}")
            ev_h = E.op("a", lambda a, oc=oc, bank=bank: a.activation(
                out=hbf_t[:, oc * TA:(oc + 1) * TA], in_=bank[:, 0:TA],
                func=AF.Gelu, scale=a1_t[:, oc:oc + 1], bias=b1_t[:, oc:oc + 1]),
                wait=[ev_mm, evS], rd=[f"P{oc % 4}", "a1", "b1"], wr=[f"hbf{oc}"],
                tag=f"{tg}-he{oc}")
            prev[("ff", oc % 4)] = ev_h
            ev_hl[oc] = ev_h
        ev_o = None
        for kc in range(32):
            for m in range(8):
                bk = P[FOB[m]]
                col = (m % 2) * 256
                w0 = []
                if m == 0:
                    w0 = [ev_hl[kc], ev_w2l]
                    if kc == 0:
                        w0.append(prev.get("fo_evac"))
                ev_o = E.op("t", lambda t, kc=kc, m=m, bk=bk, col=col: t.matmul(
                    out=bk[:, col:col + TA],
                    lhsT=w2s_t[:, kc * 1024 + m * 128:kc * 1024 + m * 128 + 128],
                    rhs=hbf_t[:, kc * TA:(kc + 1) * TA],
                    start=(kc == 0 and m % 2 == 0), stop=(kc == 31 and m % 2 == 1)),
                    wait=w0, inc=["ts"] if m == 7 else [],
                    rd=["w2s", f"hbf{kc}"], wr=[f"P{FOB[m]}"], tag=f"{tg}-o{kc}.{m}")
        ev_ev = None
        for m in range(8):
            bk = P[FOB[m]]
            col = (m % 2) * 256
            ev_s = E.op("a", lambda a, m=m, bk=bk, col=col: a.activation(
                out=sp2_t[:, (m % 2) * TA:(m % 2) * TA + TA],
                in_=bk[:, col:col + TA], func=AF.Identity,
                scale=a2_t[:, m:m + 1], bias=b2_t[:, m:m + 1]),
                wait=[ev_o, prev.get(("sp2", m % 2)), evS],
                rd=[f"P{FOB[m]}", "a2", "b2"], wr=[f"sp2{m % 2}"],
                tag=f"{tg}-fe{m}")
            ev_ev = E.op("v", lambda v, m=m: v.tensor_tensor(
                out=xp_t[:, m * TA:(m + 1) * TA],
                in0=sp2_t[:, (m % 2) * TA:(m % 2) * TA + TA],
                in1=xp_t[:, m * TA:(m + 1) * TA], op=ALU.add),
                wait=[ev_s, ev_xp], rd=[f"sp2{m % 2}", "xp"], wr=["xp"],
                tag=f"{tg}-fa{m}")
            prev[("sp2", m % 2)] = ev_ev
        prev["fo_evac"] = ev_ev
        dest = yout if mode == "x1" else x1_s
        ev_out = E.dma("s", fmaj(dest[it], 8, TA), xp_t[:], wait=[ev_ev],
                       rd=["xp"], wr=[f"d:y:{it}"], tag=f"{tg}-sty", q="sdy")
        prev["xn_used"] = ev_xn
        prev["xp_used"] = ev_out
    pf.close()
    E.barrier()

    def _tail():
        qsems = [k for k in E.cnt if k in E.sem_owner
                 and E.sem_owner[k].endswith("q")]
        E.op("s", lambda e: e.nop(), wait=[E.ev(k) for k in qsems], inc=[])

    if mode == "x1":
        _tail()
        E.verify()
        _finish(nc, E, es)
        return nc

    ev_sdy_tot = E.ev("sdy")
    pb0 = ExitStack()
    qw_t = sb("qw_t", [128, 8 * DM], BF16, stk=pb0)
    kw_t = sb("kw_t", [128, 8 * DM], BF16, stk=pb0)
    vw_t = sb("vw_t", [128, 8 * DM], BF16, stk=pb0)
    bw_t = sb("bw_t", [128, 8 * 4], BF16, stk=pb0)
    gm1_t = sb("gm1_t", [128, 8 * 2 * DM], BF16, stk=pb0)
    gm2_t = sb("gm2_t", [128, 16 * 16], BF16, stk=pb0)
    gmb1_t = sb("gmb1_t", [128, 16], F32, stk=pb0)
    gmb2_t = sb("gmb2_t", [16, 1], F32, stk=pb0)
    qkvcd_t = sb("qkvcd_t", [128, 96 * 128], BF16, stk=pb0)
    bsel_t = sb("bsel_t", [4, 512], F32, stk=pb0)
    x1_t = sb("x1_t", [128, 8 * TA], stk=pb0)
    xnb_t = sb("xnb_t", [128, 8 * TA], BF16, stk=pb0)
    sp_t = sb("spb_t", [128, 2 * TA], stk=pb0)
    row_t = sb("rowb_t", [1, 4 * TA], stk=pb0)
    pst_t = sb("pst_t", [128, 2 * (TA + 3)], BF16, stk=pb0)
    ptl_t = sb("ptl_t", [128, 24 * 3], BF16, stk=pb0)
    qkvf_t = sb("qkvf_t", [128, 24 * TA], stk=pb0)
    vvb_t = sb("vvb_t", [128, 8 * TA], BF16, stk=pb0)
    qhb_t = sb("qhb_t", [128, 16 * TA], BF16, stk=pb0)
    kbb_t = sb("kbb_t", [128, 8 * TA], BF16, stk=pb0)
    vbb_t = sb("vbb_t", [128, 8 * TA], BF16, stk=pb0)
    nrm_t = sb("nrm_t", [1, 8 * TA], stk=pb0)
    beta_t = sb("beta_t", [4, TA], stk=pb0)
    gh_t = sb("gh_t", [128, 16 * TA], BF16, stk=pb0)
    g16_t = sb("g16_t", [16, TA], stk=pb0)
    tm_t = sb("tm_t", [128, DM], stk=pb0)
    tmb_t = sb("tmb_t", [128, 3 * DM], BF16, stk=pb0)
    g16tm_t = sb("g16tm_t", [128, 2 * 16], stk=pb0)
    mx1_t = sb("mx1_t", [128, 2], stk=pb0)

    ev_m0 = E.op("v", lambda v: v.memset(mx1_t[:], 0.0), wr=["mx1"], tag="mx1-init")
    E.op("v", lambda v: v.memset(ptl_t[:], 0.0), wr=["ptl"], tag="ptl-init")

    for (d, sr, nm) in [(qw_t, qwT, "qw"), (kw_t, kwT, "kw"), (vw_t, vwT, "vw"),
                        (bw_t, bwT, "bw"), (gm1_t, gm1T, "gm1"), (gm2_t, gm2T, "gm2")]:
        E.dma("p", d[:], sr, wr=[nm], tag=f"ld-{nm}", q="pdB")
    evWB = E.dma("p", qkvcd_t[:],
                 bass.AP(qkvcd.tensor, 0, [[128, 128], [128 * 128, 96], [1, 128]]),
                 wr=["qkvcd"], tag="ld-qkvcd", q="pdB")
    E.dma("s", gmb1_t[:], gmb1, wr=["gmb1"], tag="ld-gmb1", q="sdS2")
    E.dma("s", gmb2_t[:], gmb2, wr=["gmb2"], tag="ld-gmb2", q="sdS2")
    E.dma("s", id_bf_t[:], id_bf, wr=["idb"], tag="ld-idb", q="sdS2")
    E.dma("s", idf_t[:], id_f32, wr=["idf"], tag="ld-idf", q="sdS2")
    E.dma("s", bsel_t[:], bsel, wr=["bsel"], tag="ld-bsel", q="sdS2")
    evS2 = E.ev("sdS2")
    ev_gb1 = evS2
    ev_idf = evS2
    ev_ib = evS2

    T = TA
    prev = {}
    zpad_t = sb("zpad_t", [128, 31], BF16, stk=pb0)
    ev_z0 = E.op("v", lambda v: v.memset(zpad_t[:], 0.0),
                 wr=["zpad"], tag="vzp")
    ev_zd = E.dma("s",
                  bass.AP(vF.tensor, 0, [[31 + L, 128], [128 * (31 + L), 8], [1, 31]]),
                  bass.AP(zpad_t[:].tensor, zpad_t[:].offset,
                          [[31, 128], [0, 8], [1, 31]]),
                  wait=[ev_z0], rd=["zpad"], wr=["d:vFz"], tag="stvz", q="sdf4")
    for it in range(NTA):
        tg = f"b{it}"
        ev_x = E.dma("s", x1_t[:], fmaj(x1_s[it], 8, TA),
                     wait=[prev.get("x_used"), ev_sdy_tot,
                           prev.get("dbg_hold")],
                     rd=[f"d:y:{it}"], wr=["x"], tag=f"{tg}-ldx", q="sdx")
        b0 = P[0]
        mm2ev = {}
        ev_sums = None
        for c in range(8):
            ev_sq = E.op("a", lambda a, c=c, sp_t=sp_t: a.activation(
                out=sp_t[:, (c % 2) * T:(c % 2) * T + T],
                in_=x1_t[:, c * T:(c + 1) * T], func=AF.Square),
                wait=[ev_x, mm2ev.get(c % 2), prev.get("sp_free")],
                rd=["x"], wr=[f"sp{c % 2}"], tag=f"{tg}-sq{c}")
            E.op("t", lambda t, c=c: t.matmul(
                out=b0[0:1, 0:T], lhsT=ones_t[:, 0:1],
                rhs=x1_t[:, c * T:(c + 1) * T], start=(c == 0), stop=False),
                wait=[ev_x, prev.get("b0_free")] if c == 0 else [],
                rd=["x", "ones"], wr=["P0"], tag=f"{tg}-mmx{c}")
            ev_sums = E.op("t", lambda t, c=c, sp_t=sp_t: t.matmul(
                out=b0[0:1, T:2 * T], lhsT=ones_t[:, 0:1],
                rhs=sp_t[:, (c % 2) * T:(c % 2) * T + T],
                start=False, stop=(c == 7)),
                wait=[ev_sq], inc=["ts"], rd=[f"sp{c % 2}", "ones"], wr=["P0"],
                tag=f"{tg}-mmsq{c}")
            mm2ev[c % 2] = ev_sums
        e1 = E.op("v", lambda v, row_t=row_t: v.tensor_scalar(
            out=row_t[0:1, 0:2 * T], in0=b0[0:1, 0:2 * T], scalar1=1.0 / DM,
            scalar2=None, op0=ALU.mult), wait=[ev_sums],
            rd=["P0"], wr=["row"], tag=f"{tg}-ln1")
        prev["b0_free"] = e1
        e2 = E.op("v", lambda v, row_t=row_t: v.tensor_mul(
            out=row_t[0:1, 2 * T:3 * T], in0=row_t[0:1, 0:T],
            in1=row_t[0:1, 0:T]), wait=[e1], rd=["row"], wr=["row"],
            tag=f"{tg}-ln2")
        e3 = E.op("v", lambda v, row_t=row_t: v.tensor_sub(
            out=row_t[0:1, 2 * T:3 * T], in0=row_t[0:1, T:2 * T],
            in1=row_t[0:1, 2 * T:3 * T]), wait=[e2], rd=["row"], wr=["row"],
            tag=f"{tg}-ln3")
        e4a = E.op("a", lambda a, row_t=row_t: a.activation(
            out=row_t[0:1, 3 * T:4 * T], in_=row_t[0:1, 2 * T:3 * T],
            func=AF.Sqrt, bias=eps_t[0:1, 0:1]), wait=[e3, ev_eps],
            rd=["row", "eps"], wr=["row"], tag=f"{tg}-ln4")
        e4 = E.op("v", lambda v, row_t=row_t: v.reciprocal(
            out=row_t[0:1, 3 * T:4 * T], in_=row_t[0:1, 3 * T:4 * T]),
            wait=[e4a], rd=["row"], wr=["row"], tag=f"{tg}-ln5")
        e5 = E.op("v", lambda v, row_t=row_t: v.tensor_mul(
            out=row_t[0:1, 2 * T:3 * T], in0=row_t[0:1, 0:T],
            in1=row_t[0:1, 3 * T:4 * T]), wait=[e4], rd=["row"], wr=["row"],
            tag=f"{tg}-ln6")
        e6 = E.op("v", lambda v, row_t=row_t: v.tensor_copy(
            out=row_t[0:1, 0:T], in_=row_t[0:1, 3 * T:4 * T]), wait=[e5],
            rd=["row"], wr=["row"], tag=f"{tg}-ln7")
        e7 = E.op("v", lambda v, row_t=row_t: v.tensor_scalar(
            out=row_t[0:1, T:2 * T], in0=row_t[0:1, 2 * T:3 * T], scalar1=-1.0,
            scalar2=None, op0=ALU.mult), wait=[e6], rd=["row"], wr=["row"],
            tag=f"{tg}-ln8")
        b1p = P[1]
        ebc = E.op("t", lambda t, row_t=row_t: t.matmul(
            out=b1p[:, 0:2 * T], lhsT=onesr_t[0:1, :], rhs=row_t[0:1, 0:2 * T],
            start=True, stop=True), wait=[e7, prev.get("b1_free")], inc=["ts"],
            rd=["row", "onesr"], wr=["P1"], tag=f"{tg}-lnbc")
        ev_xn = None
        for c in range(8):
            em = E.op("v", lambda v, c=c, sp_t=sp_t: v.tensor_mul(
                out=sp_t[:, (c % 2) * T:(c % 2) * T + T],
                in0=x1_t[:, c * T:(c + 1) * T], in1=b1p[:, 0:T]),
                wait=[ebc, mm2ev.get(c % 2)],
                rd=["x", "P1"], wr=[f"sp{c % 2}"], tag=f"{tg}-xnm{c}")
            ev_xn = E.op("v", lambda v, c=c, sp_t=sp_t: v.tensor_tensor(
                out=xnb_t[:, c * T:(c + 1) * T],
                in0=sp_t[:, (c % 2) * T:(c % 2) * T + T], in1=b1p[:, T:2 * T],
                op=ALU.add), wait=[em], rd=[f"sp{c % 2}", "P1"], wr=["xnb"],
                tag=f"{tg}-xna{c}")
            mm2ev[c % 2] = ev_xn
        prev["b1_free"] = ev_xn
        prev["sp_free"] = ev_xn
        ev_mx = E.op("v", lambda v: v.tensor_reduce(
            out=mx1_t[:, 1:2], in_=xnb_t[:], axis=AXX, op=ALU.max,
            apply_absolute_value=True), wait=[ev_xn],
            rd=["xnb"], wr=["mx1"], tag=f"{tg}-mxr")
        prev["mxw"] = E.op("v", lambda v: v.tensor_tensor(
            out=mx1_t[:, 0:1], in0=mx1_t[:, 0:1], in1=mx1_t[:, 1:2],
            op=ALU.max), wait=[ev_mx, ev_m0], rd=["mx1"], wr=["mx1"],
            tag=f"{tg}-mxa")

        wsel = {0: (qw_t, "qw"), 1: (kw_t, "kw"), 2: (vw_t, "vw")}
        for oc in range(24):
            wt, wky = wsel[oc // 8]
            occ = oc % 8
            bank = P[2 + oc % 2]
            bkk = f"P{2 + oc % 2}"
            ev_mm = None
            for kc in range(8):
                w0 = []
                if kc == 0:
                    w0 = [ev_xn, prev.get(("pjf", oc % 2)), evWB]
                ev_mm = E.op("t", lambda t, wt=wt, occ=occ, kc=kc, bank=bank: t.matmul(
                    out=bank[:, 0:T],
                    lhsT=wt[:, kc * DM + occ * 128:kc * DM + occ * 128 + 128],
                    rhs=xnb_t[:, kc * T:(kc + 1) * T],
                    start=(kc == 0), stop=(kc == 7)),
                    wait=w0, inc=["ts"] if kc == 7 else [],
                    rd=[wky, "xnb"], wr=[bkk], tag=f"{tg}-pj{oc}.{kc}")
            st0 = (oc % 2) * (TA + 3)
            stk_ = f"pst{oc % 2}"
            ev_ev = E.op("a", lambda a, st0=st0, bank=bank: a.activation(
                out=pst_t[:, st0 + 3:st0 + 3 + T], in_=bank[:, 0:T],
                func=AF.Copy), wait=[ev_mm, prev.get(("pstw", oc % 2))],
                rd=[bkk], wr=[stk_], tag=f"{tg}-pje{oc}")
            prev[("pjf", oc % 2)] = ev_ev
            ev_h = E.op("v", lambda v, st0=st0, oc=oc: v.tensor_copy(
                out=pst_t[:, st0:st0 + 3], in_=ptl_t[:, oc * 3:oc * 3 + 3]),
                wait=[ev_ev], rd=["ptl"], wr=[stk_], tag=f"{tg}-pjh{oc}")
            ev_tl = E.op("v", lambda v, st0=st0, oc=oc: v.tensor_copy(
                out=ptl_t[:, oc * 3:oc * 3 + 3],
                in_=pst_t[:, st0 + T:st0 + T + 3]), wait=[ev_h],
                rd=[stk_], wr=["ptl"], tag=f"{tg}-pjt{oc}")
            cbk = P[4 + oc % 2]
            cbkk = f"P{4 + oc % 2}"
            ev_cv = None
            for k in range(4):
                ev_cv = E.op("t", lambda t, oc=oc, k=k, st0=st0, cbk=cbk: t.matmul(
                    out=cbk[:, 0:T],
                    lhsT=qkvcd_t[:, (oc * 4 + k) * 128:(oc * 4 + k) * 128 + 128],
                    rhs=pst_t[:, st0 + k:st0 + k + T],
                    start=(k == 0), stop=(k == 3)),
                    wait=[ev_tl, prev.get(("cvf2", oc % 2))] if k == 0 else [],
                    inc=["ts"] if k == 3 else [],
                    rd=["qkvcd", stk_], wr=[cbkk], tag=f"{tg}-cv{oc}.{k}")
            prev[("pstw", oc % 2)] = ev_cv
            ev_sl = E.op("a", lambda a, oc=oc, cbk=cbk: a.activation(
                out=qkvf_t[:, oc * T:(oc + 1) * T], in_=cbk[:, 0:T],
                func=AF.Silu), wait=[ev_cv, prev.get(("qkvfw", oc)),
                                     prev.get(("qkvfwT", oc))],
                rd=[cbkk], wr=[f"qkvf{oc}"], tag=f"{tg}-sl{oc}")
            prev[("cvf2", oc % 2)] = ev_sl
            if oc >= 16:
                E.op("v", lambda v, oc=oc: v.tensor_copy(
                    out=vvb_t[:, (oc - 16) * T:(oc - 15) * T],
                    in_=qkvf_t[:, oc * T:(oc + 1) * T]),
                    wait=[ev_sl, prev.get(("vvbw", oc - 16))],
                    rd=[f"qkvf{oc}"], wr=[f"vvb{oc - 16}"], tag=f"{tg}-cb{oc}")

        if it == 0:
            E.dma("s", dbg_xn, xnb_t[:], wait=[ev_xn],
                  rd=["xnb"], wr=["d:dbgxn"], tag="dbg-xn", q="sdbg")
            prev["dbg_hold"] = E.dma(
                "s", dbg_qf, qkvf_t[:], wait=[ev_sl],
                rd=[f"qkvf{o}" for o in range(24)], wr=["d:dbgqf"],
                tag="dbg-qf", q="sdbg")

        ev_bm = None
        for kc in range(8):
            ev_bm = E.op("t", lambda t, kc=kc: t.matmul(
                out=P[6][0:4, 0:T], lhsT=bw_t[:, kc * 4:(kc + 1) * 4],
                rhs=xnb_t[:, kc * T:(kc + 1) * T],
                start=(kc == 0), stop=(kc == 7)),
                wait=[ev_xn, prev.get("b6_free")] if kc == 0 else [],
                inc=["ts"] if kc == 7 else [],
                rd=["bw", "xnb"], wr=["P6"], tag=f"{tg}-bm{kc}")
        ev_bs = E.op("a", lambda a: a.activation(
            out=beta_t[:], in_=P[6][0:4, 0:T], func=AF.Sigmoid),
            wait=[ev_bm, prev.get("beta_used")],
            rd=["P6"], wr=["beta"], tag=f"{tg}-bsig")

        ghl = {}
        for oc in range(16):
            bank = P[2 + oc % 2]
            bkk = f"P{2 + oc % 2}"
            ev_mm = None
            for kc in range(8):
                ev_mm = E.op("t", lambda t, oc=oc, kc=kc, bank=bank: t.matmul(
                    out=bank[:, 0:T],
                    lhsT=gm1_t[:, kc * 2 * DM + oc * 128:kc * 2 * DM + oc * 128 + 128],
                    rhs=xnb_t[:, kc * T:(kc + 1) * T],
                    start=(kc == 0), stop=(kc == 7)),
                    wait=[ev_xn, prev.get(("pjf", oc % 2))] if kc == 0 else [],
                    inc=["ts"] if kc == 7 else [],
                    rd=["gm1", "xnb"], wr=[bkk], tag=f"{tg}-g1.{oc}.{kc}")
            ev_gh = E.op("a", lambda a, oc=oc, bank=bank: a.activation(
                out=gh_t[:, oc * T:(oc + 1) * T], in_=bank[:, 0:T],
                func=AF.Gelu, bias=gmb1_t[:, oc:oc + 1]),
                wait=[ev_mm, ev_gb1, prev.get(("ghw", oc))],
                rd=[bkk, "gmb1"], wr=[f"gh{oc}"], tag=f"{tg}-g1e{oc}")
            prev[("pjf", oc % 2)] = ev_gh
            ghl[oc] = ev_gh
        ev_g2 = None
        for kc in range(16):
            ev_g2 = E.op("t", lambda t, kc=kc: t.matmul(
                out=P[7][0:16, 0:T], lhsT=gm2_t[:, kc * 16:(kc + 1) * 16],
                rhs=gh_t[:, kc * T:(kc + 1) * T],
                start=(kc == 0), stop=(kc == 15)),
                wait=[ghl[kc]] + ([prev.get("b7_free")] if kc == 0 else []),
                inc=["ts"] if kc == 15 else [],
                rd=["gm2", f"gh{kc}"], wr=["P7"], tag=f"{tg}-g2.{kc}")
        ev_g16 = E.op("v", lambda v: v.tensor_scalar(
            out=g16_t[:], in0=P[7][0:16, 0:T], scalar1=gmb2_t[:, 0:1],
            scalar2=None, op0=ALU.add),
            wait=[ev_g2, evS2, prev.get("g16_used")],
            rd=["P7", "gmb2"], wr=["g16"], tag=f"{tg}-g2e")
        ev_gt = None
        for hf in range(2):
            ev_gt = E.op("t", lambda t, hf=hf: t.matmul(
                out=P[6][0:128, 256 + hf * 16:256 + hf * 16 + 16],
                lhsT=g16_t[0:16, hf * 128:hf * 128 + 128],
                rhs=idf_t[0:16, 0:16], start=(hf == 0), stop=(hf == 1)),
                wait=[ev_g16, ev_idf, ev_bs] if hf == 0 else [],
                inc=["ts"] if hf == 1 else [],
                rd=["g16", "idf"], wr=["P6"], tag=f"{tg}-gt{hf}")
        ev_gte = None
        for hf in range(2):
            ev_gte = E.op("v", lambda v, hf=hf: v.tensor_copy(
                out=g16tm_t[:, hf * 16:(hf + 1) * 16],
                in_=P[6][0:128, 256 + hf * 16:256 + hf * 16 + 16]),
                wait=[ev_gt, prev.get(("g16tm_used", hf))],
                rd=["P6"], wr=[f"g16tm{hf}"], tag=f"{tg}-gte{hf}")
            E.dma("s", gateT[2 * it + hf], g16tm_t[:, hf * 16:(hf + 1) * 16],
                  wait=[ev_gte], rd=[f"g16tm{hf}"], wr=[f"d:gT:{2 * it + hf}"],
                  tag=f"{tg}-stg{hf}", q=f"sdg{hf}")
            prev[("g16tm_used", hf)] = E.ev(f"sdg{hf}")
        prev["g16_used"] = E.ev("ts")

        REG = [(P[6], 0, "P6"), (P[6], 256, "P6"), (P[7], 0, "P7"),
               (P[7], 256, "P7")]
        ev_nv = None
        for idx in range(8):
            bnk, coff, bky = REG[idx % 4]
            w_extra = [] if ev_nv is None else [ev_nv]
            if idx == 0:
                w_extra += [ev_gte, prev.get("nrm_used")]
            elif idx == 1:
                w_extra += [ev_gte]
            elif idx in (2, 3):
                w_extra += [ev_g16]
            ev_nm = None
            for j in range(2):
                c = idx * 2 + j
                ev_sq = E.op("a", lambda a, c=c, sp_t=sp_t: a.activation(
                    out=sp_t[:, (c % 2) * T:(c % 2) * T + T],
                    in_=qkvf_t[:, c * T:(c + 1) * T], func=AF.Square),
                    wait=[mm2ev.get(c % 2)],
                    rd=[f"qkvf{c}"], wr=[f"sp{c % 2}"], tag=f"{tg}-nsq{c}")
                ev_nm = E.op("t", lambda t, bnk=bnk, coff=coff, j=j, c=c,
                             sp_t=sp_t: t.matmul(
                    out=bnk[0:1, coff:coff + T], lhsT=ones_t[:, 0:1],
                    rhs=sp_t[:, (c % 2) * T:(c % 2) * T + T],
                    start=(j == 0), stop=(j == 1)),
                    wait=[ev_sq] + (w_extra if j == 0 else []), inc=["ts"],
                    rd=[f"sp{c % 2}", "ones"], wr=[bky], tag=f"{tg}-nm{c}")
                mm2ev[c % 2] = ev_nm
            ev_nv = E.op("v", lambda v, idx=idx, bnk=bnk, coff=coff: v.tensor_copy(
                out=nrm_t[0:1, idx * T:(idx + 1) * T],
                in_=bnk[0:1, coff:coff + T]), wait=[ev_nm],
                rd=[bky], wr=["nrm"], tag=f"{tg}-ne{idx}")
        prev["b6_free"] = ev_nv
        prev["b7_free"] = ev_nv
        ev_ns = E.op("a", lambda a: a.activation(
            out=nrm_t[:], in_=nrm_t[:], func=AF.Sqrt), wait=[ev_nv],
            rd=["nrm"], wr=["nrm"], tag=f"{tg}-nsqr")
        ev_nr = E.op("v", lambda v: v.reciprocal(out=nrm_t[:], in_=nrm_t[:]),
                     wait=[ev_ns], rd=["nrm"], wr=["nrm"], tag=f"{tg}-nrc")
        prev["nrm_used"] = ev_nr

        bci = 0
        ev_bcd = {}
        evVS0 = E.ev("vs")
        for idx in range(8):
            bnk = P[2 + bci % 4]
            bky = f"P{2 + bci % 4}"
            ev_b = E.op("t", lambda t, bnk=bnk, idx=idx: t.matmul(
                out=bnk[:, 0:T], lhsT=onesr_t[0:1, :],
                rhs=nrm_t[0:1, idx * T:(idx + 1) * T],
                start=True, stop=True),
                wait=[ev_nr, ev_bcd.get(bci % 4), evVS0 if bci < 4 else None],
                inc=["ts"], rd=["nrm", "onesr"], wr=[bky], tag=f"{tg}-nb{idx}")
            ev_qh = None
            for j in range(2):
                c = idx * 2 + j
                ev_qh = E.op("v", lambda v, c=c, bnk=bnk: v.tensor_mul(
                    out=qhb_t[:, c * T:(c + 1) * T],
                    in0=qkvf_t[:, c * T:(c + 1) * T], in1=bnk[:, 0:T]),
                    wait=[ev_b, prev.get(("qhbw", c)), prev.get(("qhbwT", c))],
                    rd=[f"qkvf{c}", bky], wr=[f"qhb{c}"], tag=f"{tg}-qh{c}")
            ev_bcd[bci % 4] = ev_qh
            bci += 1
        for h in range(4):
            bnk = P[2 + bci % 4]
            bky = f"P{2 + bci % 4}"
            ev_b = E.op("t", lambda t, bnk=bnk, h=h: t.matmul(
                out=bnk[:, 0:T], lhsT=bsel_t[:, h * 128:(h + 1) * 128],
                rhs=beta_t[0:4, 0:T], start=True, stop=True),
                wait=[ev_bs, ev_bcd.get(bci % 4), evS2],
                inc=["ts"], rd=["beta", "bsel"], wr=[bky], tag=f"{tg}-bb{h}")
            ev_vb = None
            for j in range(2):
                cc = 2 * h + j
                E.op("v", lambda v, cc=cc, bnk=bnk: v.tensor_mul(
                    out=kbb_t[:, cc * T:(cc + 1) * T],
                    in0=qhb_t[:, (8 + cc) * T:(9 + cc) * T], in1=bnk[:, 0:T]),
                    wait=[ev_b, ev_bcd.get((8 + cc - 8))] if False else
                         [ev_b, prev.get(("kbbw", cc)), prev.get(("kbbwT", cc))],
                    rd=[f"qhb{8 + cc}", bky], wr=[f"kbb{cc}"], tag=f"{tg}-kb{cc}")
                ev_vb = E.op("v", lambda v, cc=cc, bnk=bnk: v.tensor_mul(
                    out=vbb_t[:, cc * T:(cc + 1) * T],
                    in0=qkvf_t[:, (16 + cc) * T:(17 + cc) * T], in1=bnk[:, 0:T]),
                    wait=[ev_b, prev.get(("vbbw", cc)), prev.get(("vbbwT", cc))],
                    rd=[f"qkvf{16 + cc}", bky], wr=[f"vbb{cc}"], tag=f"{tg}-vb{cc}")
            ev_bcd[bci % 4] = ev_vb
            bci += 1
        prev["beta_used"] = E.ev("ts")

        def fmsl(dr, c0, w, rowlen=L):
            return bass.AP(dr.tensor, c0, [[rowlen, 128], [128 * rowlen, 8], [1, w]])

        evVS1 = E.ev("vs")
        ev1 = E.dma("s", fmsl(qhatF, it * TA, TA), qhb_t[:, 0:8 * T],
                    wait=[evVS1], rd=[f"qhb{c}" for c in range(8)],
                    wr=[f"d:qF:{it}"], tag=f"{tg}-stqF", q="sdf1")
        ev2 = E.dma("s", fmsl(khatF, it * TA, TA), qhb_t[:, 8 * T:16 * T],
                    wait=[evVS1], rd=[f"qhb{c}" for c in range(8, 16)],
                    wr=[f"d:kF:{it}"], tag=f"{tg}-stkF", q="sdf2")
        ev3 = E.dma("s", fmsl(kbF, it * TA, TA), kbb_t[:],
                    wait=[evVS1], rd=[f"kbb{c}" for c in range(8)],
                    wr=[f"d:kbF:{it}"], tag=f"{tg}-stkbF", q="sdf3")
        ev4 = E.dma("s", fmsl(vF, 31 + it * TA, TA, rowlen=31 + L), vvb_t[:],
                    wait=[E.ev("vs")], rd=[f"vvb{c}" for c in range(8)],
                    wr=[f"d:vF:{it}"], tag=f"{tg}-stvF", q="sdf4")
        for c in range(8):
            prev[("qhbw", c)] = ev1
            prev[("qhbw", c + 8)] = ev2
            prev[("kbbw", c)] = ev3
            prev[("vvbw", c)] = ev4

        evVS2 = E.ev("vs")
        for hf in range(2):
            tb = 2 * it + hf

            def tquad(src_ap_fn, nchunks, banks, bkeys, rdkeys, idt, idk, tagp,
                      extra0=(), bf=False):
                done = None
                for c in range(nchunks):
                    bnk = banks[c // 4]
                    q0 = (c % 4) * 128
                    w0 = []
                    if c % 4 == 0:
                        w0 = [prev.get(("tpf", bkeys[c // 4]))] + list(extra0)
                    if bf:
                        ov = bnk[:].bitcast(BF16)[:, q0:q0 + 128]
                    else:
                        ov = bnk[:, q0:q0 + 128]
                    done = E.op("t", lambda t, c=c, ov=ov: t.matmul(
                        out=ov, lhsT=src_ap_fn(c),
                        rhs=idt, start=(c % 4 == 0), stop=(c % 4 == 3),
                        is_transpose=True),
                        wait=w0, inc=["ts"] if c % 4 == 3 else [],
                        rd=[rdkeys(c), idk], wr=[bkeys[c // 4]],
                        tag=f"{tagp}{c}")
                return done

            evt = tquad(lambda c, hf=hf: qhb_t[:, (8 + c) * T + hf * 128:
                                                 (8 + c) * T + hf * 128 + 128],
                        8, [P[2], P[3]], ["P2", "P3"],
                        lambda c: f"qhb{8 + c}", id_bf_t[:], "idb",
                        f"{tg}-tpk{hf}.", extra0=[evVS2, ev_ib], bf=True)
            ee1 = E.op("v", lambda v: v.tensor_copy(
                out=tmb_t[:, 0:512], in_=P[2][:].bitcast(BF16)[:, 0:512]),
                wait=[evt, prev.get("tmb_used0")], rd=["P2"], wr=["tmb0"],
                tag=f"{tg}-tke{hf}a")
            ee2 = E.op("v", lambda v: v.tensor_copy(
                out=tmb_t[:, 512:1024], in_=P[3][:].bitcast(BF16)[:, 0:512]),
                wait=[evt, prev.get("tmb_used0")],
                rd=["P3"], wr=["tmb0"], tag=f"{tg}-tke{hf}b")
            prev[("tpf", "P2")] = ee1
            prev[("tpf", "P3")] = ee2
            evst = E.dma("s", khT[tb], tmb_t[:, 0:1024], wait=[ee1, ee2],
                  rd=["tmb0"], wr=[f"d:khT:{tb}"], tag=f"{tg}-stkT{hf}", q="sdt1")
            prev["tmb_used0"] = evst

            evt = tquad(lambda c, hf=hf: kbb_t[:, c * T + hf * 128:
                                               c * T + hf * 128 + 128],
                        8, [P[2], P[3]], ["P2", "P3"],
                        lambda c: f"kbb{c}", id_bf_t[:], "idb",
                        f"{tg}-tpb{hf}.", bf=True)
            ee1 = E.op("v", lambda v: v.tensor_copy(
                out=tmb_t[:, 1024:1536], in_=P[2][:].bitcast(BF16)[:, 0:512]),
                wait=[evt, prev.get("tmb_used1")],
                rd=["P2"], wr=["tmb1"], tag=f"{tg}-tbe{hf}a")
            ee2 = E.op("v", lambda v: v.tensor_copy(
                out=tmb_t[:, 1536:2048], in_=P[3][:].bitcast(BF16)[:, 0:512]),
                wait=[evt, prev.get("tmb_used1")],
                rd=["P3"], wr=["tmb1"], tag=f"{tg}-tbe{hf}b")
            prev[("tpf", "P2")] = ee1
            prev[("tpf", "P3")] = ee2
            evst = E.dma("s", kbT[tb], tmb_t[:, 1024:2048], wait=[ee1, ee2],
                  rd=["tmb1"], wr=[f"d:kbT:{tb}"], tag=f"{tg}-stbT{hf}", q="sdt2")
            prev["tmb_used1"] = evst

            evt = tquad(lambda c, hf=hf: vbb_t[:, c * T + hf * 128:
                                               c * T + hf * 128 + 128],
                        8, [P[2], P[3]], ["P2", "P3"],
                        lambda c: f"vbb{c}", id_bf_t[:], "idb",
                        f"{tg}-tpv{hf}.", bf=True)
            ee1 = E.op("v", lambda v: v.tensor_copy(
                out=tmb_t[:, 2048:2560], in_=P[2][:].bitcast(BF16)[:, 0:512]),
                wait=[evt, prev.get("tmb_used2")],
                rd=["P2"], wr=["tmb2"], tag=f"{tg}-tve{hf}a")
            ee2 = E.op("v", lambda v: v.tensor_copy(
                out=tmb_t[:, 2560:3072], in_=P[3][:].bitcast(BF16)[:, 0:512]),
                wait=[evt, prev.get("tmb_used2")],
                rd=["P3"], wr=["tmb2"], tag=f"{tg}-tve{hf}b")
            prev[("tpf", "P2")] = ee1
            prev[("tpf", "P3")] = ee2
            evst = E.dma("s", vbT[tb], tmb_t[:, 2048:3072], wait=[ee1, ee2],
                  rd=["tmb2"], wr=[f"d:vbT:{tb}"], tag=f"{tg}-stvT{hf}", q="sdt3")
            prev["tmb_used2"] = evst

            evt = tquad(lambda c, hf=hf: qkvf_t[:, (16 + c) * T + hf * 128:
                                                (16 + c) * T + hf * 128 + 128],
                        8, [P[4], P[5]], ["P4", "P5"],
                        lambda c: f"qkvf{16 + c}", idf_t[:], "idf",
                        f"{tg}-tpw{hf}.", extra0=[evVS2, ev_idf])
            ee1 = E.op("v", lambda v: v.tensor_copy(
                out=tm_t[:, 0:512], in_=P[4][:, 0:512]),
                wait=[evt, prev.get("tm_used")], rd=["P4"], wr=["tm"],
                tag=f"{tg}-twe{hf}a")
            ee2 = E.op("v", lambda v: v.tensor_copy(
                out=tm_t[:, 512:1024], in_=P[5][:, 0:512]),
                wait=[evt, prev.get("tm_used")],
                rd=["P5"], wr=["tm"], tag=f"{tg}-twe{hf}b")
            prev[("tpf", "P4")] = ee1
            prev[("tpf", "P5")] = ee2
            evst = E.dma("s", vT[tb], tm_t[:, 0:1024], wait=[ee1, ee2],
                  rd=["tm"], wr=[f"d:vT:{tb}"], tag=f"{tg}-stwT{hf}", q="sdt4")
            prev["tm_used"] = evst
            
        evTS = E.ev("ts")
        evVS3 = E.ev("vs")
        for oc in range(24):
            prev[("qkvfw", oc)] = evVS3
            prev[("qkvfwT", oc)] = evTS
        for c in range(16):
            prev[("qhbwT", c)] = evTS
        for c in range(8):
            prev[("kbbwT", c)] = evTS
            prev[("vbbwT", c)] = evTS
            prev[("vbbw", c)] = evTS
        for oc in range(16):
            prev[("ghw", oc)] = evTS
        prev["x_used"] = ev_x
        prev[("pjf", 0)] = evVS3
        prev[("pjf", 1)] = evVS3
        prev[("cvf2", 0)] = evVS3
        prev[("cvf2", 1)] = evVS3
        prev[("pstw", 0)] = evTS
        prev[("pstw", 1)] = evTS

    ev_d1 = E.dma("s", bass.AP(cc_in[0].tensor, 0, [[1, 128]]), mx1_t[:, 0:1],
                  wait=[prev.get("mxw")], rd=["mx1"], wr=["d:cc0i"],
                  tag="cc0-st", q="sdcc")
    ev_cc0 = E.op("p", lambda p: p.collective_compute(
        "AllReduce", ALU.max, replica_groups=[list(range(NCORES))],
        ins=[cc_in[0][0:128]], outs=[cc_out[0][0:128]]),
        wait=[ev_d1], inc=["ps"], rd=["d:cc0i"], wr=["d:cc0o"], tag="cc0")
    ev_l1 = E.dma("s", nrm_t[0:1, 0:128],
                  bass.AP(cc_out[0].tensor, 0, [[128, 1], [1, 128]]),
                  wait=[ev_cc0, E.ev("vs")], rd=["d:cc0o", "nrm"], wr=["nrm"],
                  tag="cc0-ld", q="sdcc")
    ev_r1 = E.op("v", lambda v: v.tensor_reduce(
        out=nrm_t[0:1, 128:129], in_=nrm_t[0:1, 0:128], axis=AXX, op=ALU.max),
        wait=[ev_l1], rd=["nrm"], wr=["nrm"], tag="cc0-red")
    ev_r2 = E.op("v", lambda v: v.tensor_scalar(
        out=nrm_t[0:1, 129:130], in0=nrm_t[0:1, 128:129],
        scalar1=1.0 / 32767.0, scalar2=1e-12, op0=ALU.mult, op1=ALU.max),
        wait=[ev_r1], rd=["nrm"], wr=["nrm"], tag="cc0-s")
    ev_r3 = E.op("v", lambda v: v.reciprocal(
        out=nrm_t[0:1, 130:131], in_=nrm_t[0:1, 129:130]),
        wait=[ev_r2], rd=["nrm"], wr=["nrm"], tag="cc0-rs")
    ev_r4 = E.op("v", lambda v: v.tensor_scalar(
        out=nrm_t[0:1, 131:132], in0=nrm_t[0:1, 129:130], scalar1=-1.0,
        scalar2=None, op0=ALU.mult), wait=[ev_r3],
        rd=["nrm"], wr=["nrm"], tag="cc0-ns")
    ev_bc1 = E.op("t", lambda t: t.matmul(
        out=P[0][:, 0:3], lhsT=onesr_t[0:1, :], rhs=nrm_t[0:1, 129:132],
        start=True, stop=True), wait=[ev_r4, prev.get("b0_free")], inc=["ts"],
        rd=["nrm", "onesr"], wr=["P0"], tag="cc0-bc")
    ev_sc1 = E.op("v", lambda v: v.tensor_copy(out=sc_t[:, 0:3], in_=P[0][:, 0:3]),
                  wait=[ev_bc1], rd=["P0"], wr=["sc"], tag="cc0-ev")

    pb0.close()
    E.barrier()

    pba = ExitStack()
    qhF_t = sb("qhF_t", [128, 8 * TB], BF16, stk=pba)
    khF_t = sb("khF_t", [128, 8 * TB], BF16, stk=pba)
    kbF_t = sb("kbF_t", [128, 8 * TB], BF16, stk=pba)
    khT_t = sb("khT_t", [128, DM], BF16, stk=pba)
    kbT_t = sb("kbT_t", [128, DM], BF16, stk=pba)
    vbT_t = sb("vbT_t", [128, DM], BF16, stk=pba)
    ms_t = sb("ms_t", [128, 512], F32, stk=pba)
    mo_t = sb("mo_t", [128, 512], F32, stk=pba)
    ma_t = sb("ma_t", [128, 512], F32, stk=pba)
    idr_t = sb("idr_t", [128, 512], BF16, stk=pba)
    bstd_t = sb("bstd_t", [128, 512], BF16, stk=pba)
    bor_t = sb("bor_t", [128, 512], BF16, stk=pba)
    cstd_t = sb("cstd_t", [128, 512], BF16, stk=pba)
    cor_t = sb("cor_t", [128, 512], BF16, stk=pba)
    tor_t = sb("tor_t", [128, 512], BF16, stk=pba)
    alor_t = sb("alor_t", [128, 512], BF16, stk=pba)
    unew_t = sb("unew_t", [128, DM], F32, stk=pba)
    unb_t = sb("unb_t", [128, DM], BF16, stk=pba)
    wfm_t = sb("wfm_t", [128, 8 * TB], BF16, stk=pba)
    dout_t = sb("dout_t", [128, DM], F32, stk=pba)
    s_t = sb("s_t", [128, 4 * 512], F32, stk=pba)
    sb_t = sb("sbf_t", [128, 4 * 512], BF16, stk=pba)
    sn_t = sb("snb_t", [128, 4 * 512], BF16, stk=pba)

    ev_ms = E.dma("s", ms_t[:], maskstd, wr=["ms"], tag="ld-ms", q="sdM")
    E.dma("s", mo_t[:], maskor, wr=["mo"], tag="ld-mo", q="sdM")
    E.dma("s", ma_t[:], maskal, wr=["ma"], tag="ld-ma", q="sdM")
    E.dma("s", idr_t[:], idrep4, wr=["idr"], tag="ld-idr", q="sdM")
    evM = E.ev("sdM")
    ev_s0 = E.op("v", lambda v: v.memset(s_t[:], 0.0), wr=["S"], tag="S-init")
    ev_s1 = E.op("v", lambda v: v.memset(sb_t[:], 0.0), wr=["Sb"], tag="Sb-init")
    ev_s2 = E.op("v", lambda v: v.memset(sn_t[:], 0.0), wr=["Sn"], tag="Sn-init")

    prev = {}
    for tb in range(NTB):
        tg = f"d{tb}"
        it = tb // 2

        def fmsl2(dr, c0, w, rowlen=L):
            return bass.AP(dr.tensor, c0, [[rowlen, 128], [128 * rowlen, 8], [1, w]])

        lq = E.dma("s", qhF_t[:], fmsl2(qhatF, tb * TB, TB),
                   wait=[prev.get("qhF_used"), E.ev("sdf1")],
                   rd=[f"d:qF:{it}"], wr=["qhF"], tag=f"{tg}-lq", q="sdl1")
        lk = E.dma("s", khF_t[:], fmsl2(khatF, tb * TB, TB),
                   wait=[prev.get("khF_used"), E.ev("sdf2")],
                   rd=[f"d:kF:{it}"], wr=["khF"], tag=f"{tg}-lk", q="sdl2")
        lb = E.dma("s", kbF_t[:], fmsl2(kbF, tb * TB, TB),
                   wait=[prev.get("kbF_used"), E.ev("sdf3")],
                   rd=[f"d:kbF:{it}"], wr=["kbF"], tag=f"{tg}-lb", q="sdl3")
        lkt = E.dma("s", khT_t[:], khT[tb],
                    wait=[prev.get("khT_used"), E.ev("sdt1")],
                    rd=[f"d:khT:{tb}"], wr=["khT"], tag=f"{tg}-lkt", q="sdl4")
        lbt = E.dma("s", kbT_t[:], kbT[tb],
                    wait=[prev.get("kbT_used"), E.ev("sdt2")],
                    rd=[f"d:kbT:{tb}"], wr=["kbT"], tag=f"{tg}-lbt", q="sdl5")
        lvt = E.dma("s", vbT_t[:], vbT[tb],
                    wait=[prev.get("vbT_used"), E.ev("sdt3")],
                    rd=[f"d:vbT:{tb}"], wr=["vbT"], tag=f"{tg}-lvt", q="sdl6")

        evg = {}
        for h in range(4):
            for kc in range(2):
                c = 2 * h + kc
                E.op("t", lambda t, h=h, c=c, kc=kc: t.matmul(
                    out=P[0][:, h * 128:h * 128 + 128],
                    lhsT=kbF_t[:, c * TB:(c + 1) * TB],
                    rhs=khF_t[:, c * TB:(c + 1) * TB],
                    start=(h == 0 and kc == 0), stop=(h == 3 and kc == 1)),
                    wait=[lb, lk, prev.get("P0f")] if (h == 0 and kc == 0) else [],
                    inc=["ts"] if (h == 3 and kc == 1) else [],
                    rd=["kbF", "khF"], wr=["P0"], tag=f"{tg}-g0.{h}.{kc}")
                evg[0] = E.op("t", lambda t, h=h, c=c, kc=kc: t.matmul(
                    out=P[1][:, h * 128:h * 128 + 128],
                    lhsT=khF_t[:, c * TB:(c + 1) * TB],
                    rhs=kbF_t[:, c * TB:(c + 1) * TB],
                    start=(h == 0 and kc == 0), stop=(h == 3 and kc == 1)),
                    wait=[lb, lk, prev.get("P1f")] if (h == 0 and kc == 0) else [],
                    inc=["ts"] if (h == 3 and kc == 1) else [],
                    rd=["kbF", "khF"], wr=["P1"], tag=f"{tg}-g1.{h}.{kc}")
        evgram = evg[0]
        eb1 = E.op("v", lambda v: v.tensor_mul(
            out=bstd_t[:], in0=P[0][:, 0:512], in1=ms_t[:]),
            wait=[evgram, evM, prev.get("bstd_used")],
            rd=["P0", "ms"], wr=["bstd"], tag=f"{tg}-bstd")
        prev["P0f"] = eb1
        eb2 = E.op("v", lambda v: v.tensor_mul(
            out=bor_t[:], in0=P[1][:, 0:512], in1=mo_t[:]),
            wait=[evgram, prev.get("bor_used")],
            rd=["P1", "mo"], wr=["bor"], tag=f"{tg}-bor")
        prev["P1f"] = eb2
        et0 = E.op("v", lambda v: v.tensor_tensor(
            out=tor_t[:], in0=bor_t[:], in1=idr_t[:], op=ALU.add),
            wait=[eb2, prev.get("tor_used")],
            rd=["bor", "idr"], wr=["tor"], tag=f"{tg}-t0")
        ecs = E.op("v", lambda v: v.tensor_copy(out=cstd_t[:], in_=bstd_t[:]),
                   wait=[eb1, prev.get("cstd_used")],
                   rd=["bstd"], wr=["cstd"], tag=f"{tg}-cs0")
        eco = E.op("v", lambda v: v.tensor_copy(out=cor_t[:], in_=bor_t[:]),
                   wait=[eb2, prev.get("cor_used")],
                   rd=["bor"], wr=["cor"], tag=f"{tg}-co0")
        for j in range(6):
            evs = evo = evt2 = None
            for h in range(4):
                evs = E.op("t", lambda t, h=h: t.matmul(
                    out=P[2][:, h * 128:h * 128 + 128],
                    lhsT=cor_t[:, h * 128:h * 128 + 128],
                    rhs=cstd_t[:, h * 128:h * 128 + 128],
                    start=(h == 0), stop=(h == 3)),
                    wait=[ecs, eco, prev.get("P2f")] if h == 0 else [],
                    inc=["ts"] if h == 3 else [],
                    rd=["cor", "cstd"], wr=["P2"], tag=f"{tg}-ts{j}.{h}")
                evo = E.op("t", lambda t, h=h: t.matmul(
                    out=P[3][:, h * 128:h * 128 + 128],
                    lhsT=cstd_t[:, h * 128:h * 128 + 128],
                    rhs=cor_t[:, h * 128:h * 128 + 128],
                    start=(h == 0), stop=(h == 3)),
                    wait=[prev.get("P3f")] if h == 0 else [],
                    inc=["ts"] if h == 3 else [],
                    rd=["cstd", "cor"], wr=["P3"], tag=f"{tg}-to{j}.{h}")
            ecs = E.op("v", lambda v: v.tensor_copy(out=cstd_t[:], in_=P[2][:, 0:512]),
                       wait=[evs, evo], rd=["P2"], wr=["cstd"], tag=f"{tg}-cse{j}")
            prev["P2f"] = ecs
            eco = E.op("v", lambda v: v.tensor_copy(out=cor_t[:], in_=P[3][:, 0:512]),
                       wait=[evo], rd=["P3"], wr=["cor"], tag=f"{tg}-coe{j}")
            prev["P3f"] = eco
            for h in range(4):
                evt2 = E.op("t", lambda t, h=h: t.matmul(
                    out=P[4][:, h * 128:h * 128 + 128],
                    lhsT=cstd_t[:, h * 128:h * 128 + 128],
                    rhs=tor_t[:, h * 128:h * 128 + 128],
                    start=(h == 0), stop=(h == 3)),
                    wait=[ecs, et0, prev.get("P4f")] if h == 0 else [],
                    inc=["ts"] if h == 3 else [],
                    rd=["cstd", "tor"], wr=["P4"], tag=f"{tg}-ta{j}.{h}")
            et0 = E.op("v", lambda v: v.tensor_tensor(
                out=tor_t[:], in0=P[4][:, 0:512], in1=tor_t[:], op=ALU.add),
                wait=[evt2], rd=["P4", "tor"], wr=["tor"], tag=f"{tg}-tae{j}")
            prev["P4f"] = et0

        eva = None
        for h in range(4):
            for kc in range(2):
                c = 2 * h + kc
                eva = E.op("t", lambda t, h=h, c=c, kc=kc: t.matmul(
                    out=P[5][:, h * 128:h * 128 + 128],
                    lhsT=khF_t[:, c * TB:(c + 1) * TB],
                    rhs=qhF_t[:, c * TB:(c + 1) * TB],
                    start=(h == 0 and kc == 0), stop=(h == 3 and kc == 1)),
                    wait=[lq, prev.get("P5f")] if (h == 0 and kc == 0) else [],
                    inc=["ts"] if (h == 3 and kc == 1) else [],
                    rd=["khF", "qhF"], wr=["P5"], tag=f"{tg}-al.{h}.{kc}")
        eal = E.op("v", lambda v: v.tensor_mul(
            out=alor_t[:], in0=P[5][:, 0:512], in1=ma_t[:]),
            wait=[eva, prev.get("alor_used")],
            rd=["P5", "ma"], wr=["alor"], tag=f"{tg}-ale")
        prev["P5f"] = eal

        UB = {0: (6, 0), 1: (6, 256), 2: (7, 0), 3: (7, 256)}
        ev_un = {}
        for h in range(4):
            bi, co = UB[h]
            E.op("t", lambda t, h=h, bi=bi, co=co: t.matmul(
                out=P[bi][:, co:co + 256],
                lhsT=tor_t[:, h * 128:h * 128 + 128],
                rhs=vbT_t[:, h * 256:(h + 1) * 256],
                start=(h % 2 == 0), stop=False),
                wait=[et0, lvt] + ([prev.get(f"P{bi}f")] if co == 0 else []),
                rd=["tor", "vbT"], wr=[f"P{bi}"], tag=f"{tg}-u{h}")
            eww = None
            for kc in range(2):
                bq = P[2 + kc]
                eww = E.op("t", lambda t, h=h, kc=kc, bq=bq: t.matmul(
                    out=bq[:, h * 128:h * 128 + 128],
                    lhsT=kbT_t[:, h * 256 + kc * 128:h * 256 + kc * 128 + 128],
                    rhs=tor_t[:, h * 128:h * 128 + 128],
                    start=True, stop=True),
                    wait=[lbt, et0, prev.get(f"wP{2 + kc}")],
                    inc=["ts"], rd=["kbT", "tor"], wr=[f"P{2 + kc}"],
                    tag=f"{tg}-w{h}.{kc}")
        eww1 = E.op("v", lambda v: v.tensor_copy(
            out=wfm_t[:, 0:4 * TB], in_=bass.AP(
                P[2][:].tensor, P[2][:].offset, [[512, 128], [1, 512]])),
            wait=[E.ev("ts")], rd=["P2"], wr=["wfm0"], tag=f"{tg}-we1")
        eww2 = E.op("v", lambda v: v.tensor_copy(
            out=wfm_t[:, 4 * TB:8 * TB], in_=bass.AP(
                P[3][:].tensor, P[3][:].offset, [[512, 128], [1, 512]])),
            wait=[E.ev("ts")], rd=["P3"], wr=["wfm1"], tag=f"{tg}-we2")
        prev["wP2"] = eww1
        prev["wP3"] = eww2
        ev_uc = None
        for h in range(4):
            bi, co = UB[h]
            for kc in range(2):
                ev_uc = E.op("t", lambda t, h=h, kc=kc, bi=bi, co=co: t.matmul(
                    out=P[bi][:, co:co + 256],
                    lhsT=wfm_t[:, kc * 4 * TB + h * TB:kc * 4 * TB + (h + 1) * TB],
                    rhs=sn_t[:, h * 512 + kc * 256:h * 512 + kc * 256 + 256],
                    start=False, stop=(kc == 1 and h % 2 == 1)),
                    wait=[eww1 if kc == 0 else eww2, ev_s2,
                          prev.get("Sn_rd_guard")],
                    inc=["ts"] if kc == 1 else [],
                    rd=[f"wfm{kc}", "Sn"], wr=[f"P{bi}"], tag=f"{tg}-uc{h}.{kc}")
            ev_un[h] = ev_uc
        eu1 = E.op("v", lambda v: v.tensor_copy(out=unew_t[:, 0:512],
                                                in_=P[6][:, 0:512]),
                   wait=[ev_un[1], prev.get("unew_used")],
                   rd=["P6"], wr=["unew"], tag=f"{tg}-ue1")
        eu2 = E.op("v", lambda v: v.tensor_copy(out=unew_t[:, 512:1024],
                                                in_=P[7][:, 0:512]),
                   wait=[ev_un[3]], rd=["P7"], wr=["unew"], tag=f"{tg}-ue2")
        prev["P6f"] = eu1
        prev["P7f"] = eu2
        eub = E.op("v", lambda v: v.tensor_copy(out=unb_t[:], in_=unew_t[:]),
                   wait=[eu2, prev.get("unb_used")],
                   rd=["unew"], wr=["unb"], tag=f"{tg}-ub")

        OB = {0: (0, 0), 1: (0, 256), 2: (1, 0), 3: (1, 256)}
        ev_o = {}
        for h in range(4):
            bi, co = OB[h]
            for kc in range(2):
                c = 2 * h + kc
                E.op("t", lambda t, h=h, c=c, kc=kc, bi=bi, co=co: t.matmul(
                    out=P[bi][:, co:co + 256],
                    lhsT=qhF_t[:, c * TB:(c + 1) * TB],
                    rhs=sb_t[:, h * 512 + kc * 256:h * 512 + kc * 256 + 256],
                    start=(kc == 0 and h % 2 == 0), stop=False),
                    wait=[lq, ev_s1, eb1 if bi == 0 else eb2,
                          prev.get("S_rd_guard")] if kc == 0 else [],
                    rd=["qhF", "Sb"], wr=[f"P{bi}"], tag=f"{tg}-oq{h}.{kc}")
            ev_o[h] = E.op("t", lambda t, h=h, bi=bi, co=co: t.matmul(
                out=P[bi][:, co:co + 256],
                lhsT=alor_t[:, h * 128:h * 128 + 128],
                rhs=unb_t[:, h * 256:(h + 1) * 256],
                start=False, stop=(h % 2 == 1)),
                wait=[eal, eub], inc=["ts"],
                rd=["alor", "unb"], wr=[f"P{bi}"], tag=f"{tg}-oa{h}")
        eo1 = E.op("v", lambda v: v.tensor_copy(out=dout_t[:, 0:512],
                                                in_=P[0][:, 0:512]),
                   wait=[ev_o[1], prev.get("dout_used")],
                   rd=["P0"], wr=["dout"], tag=f"{tg}-oe1")
        eo2 = E.op("v", lambda v: v.tensor_copy(out=dout_t[:, 512:1024],
                                                in_=P[1][:, 0:512]),
                   wait=[ev_o[3]], rd=["P1"], wr=["dout"], tag=f"{tg}-oe2")
        prev["P0f"] = eo1
        prev["P1f"] = eo2
        ev_do = E.dma("s", deltaT[tb], dout_t[:], wait=[eo2],
                      rd=["dout"], wr=[f"d:dT:{tb}"], tag=f"{tg}-std", q="sdd2")
        prev["dout_used"] = ev_do

        ev_ds = {}
        for h in range(4):
            for kc in range(2):
                bq = P[2 + h]
                ev_ds[(h, kc)] = E.op("t", lambda t, h=h, kc=kc, bq=bq: t.matmul(
                    out=bq[:, kc * 256:kc * 256 + 256],
                    lhsT=khT_t[:, h * 256 + kc * 128:h * 256 + kc * 128 + 128],
                    rhs=unb_t[:, h * 256:(h + 1) * 256],
                    start=(kc == 0), stop=(kc == 1)),
                    wait=[lkt, eub, eww1 if h <= 1 else eww2,
                          prev.get(f"dsP{2 + h}")] if kc == 0 else [],
                    inc=["ts"] if kc == 1 else [],
                    rd=["khT", "unb"], wr=[f"P{2 + h}"], tag=f"{tg}-ds{h}.{kc}")
        ev_sa = None
        for h in range(4):
            ev_sa = E.op("v", lambda v, h=h: v.tensor_tensor(
                out=s_t[:, h * 512:(h + 1) * 512],
                in0=bass.AP(P[2 + h][:].tensor, P[2 + h][:].offset,
                            [[512, 128], [1, 512]]),
                in1=s_t[:, h * 512:(h + 1) * 512], op=ALU.add),
                wait=[ev_ds[(h, 1)], ev_s0, E.ev("ts")],
                rd=[f"P{2 + h}", "S"], wr=["S"], tag=f"{tg}-sa{h}")
            prev[f"dsP{2 + h}"] = ev_sa
        esb = E.op("v", lambda v: v.tensor_copy(out=sb_t[:], in_=s_t[:]),
                   wait=[ev_sa, E.ev("ts")], rd=["S"], wr=["Sb"],
                   tag=f"{tg}-sbc")
        esn = E.op("v", lambda v: v.tensor_scalar(
            out=sn_t[:], in0=s_t[:], scalar1=-1.0, scalar2=None, op0=ALU.mult),
            wait=[ev_sa, E.ev("ts")], rd=["S"], wr=["Sn"], tag=f"{tg}-snc")
        prev["S_rd_guard"] = esb
        prev["Sn_rd_guard"] = esn
        prev["Sb_w_guard"] = E.ev("ts")
        prev["unew_used"] = E.ev("ts")
        prev["unb_used"] = E.ev("ts")
        prev["qhF_used"] = E.ev("ts")
        prev["khF_used"] = E.ev("ts")
        prev["kbF_used"] = E.ev("ts")
        prev["khT_used"] = E.ev("ts")
        prev["kbT_used"] = E.ev("ts")
        prev["vbT_used"] = E.ev("ts")
        prev["bstd_used"] = E.ev("ts")
        prev["bor_used"] = E.ev("ts")
        prev["tor_used"] = E.ev("ts")
        prev["cstd_used"] = E.ev("ts")
        prev["cor_used"] = E.ev("ts")
        prev["alor_used"] = E.ev("ts")
        prev["wP2"] = ev_sa
        prev["wP3"] = ev_sa

    pba.close()
    E.barrier()

    pbb = ExitStack()
    fircd_t = sb("fircd_t", [128, 304 * 128], BF16, stk=pbb)
    opw_t = sb("opw_t", [128, 8 * DM], BF16, stk=pbb)
    onw_t = sb("onw_t", [128, DM], F32, stk=pbb)
    alr_t = sb("alr_t", [128, 16], F32, stk=pbb)
    vfb_t = sb("vfb_t", [128, 8 * (TA + 31)], BF16, stk=pbb)
    firf_t = sb("firf_t", [128, 16 * TA], F32, stk=pbb)
    vT_t = sb("vT_t", [128, DM], F32, stk=pbb)
    dT_t = sb("dT_t", [128, DM], F32, stk=pbb)
    gT_t = sb("gT_t", [128, 16], F32, stk=pbb)
    fsT_t = sb("fsT_t", [128, DM], F32, stk=pbb)
    flT_t = sb("flT_t", [128, DM], F32, stk=pbb)
    sqs_t = sb("sqs_t", [128, TA], F32, stk=pbb)
    s1c_t = sb("s1c_t", [128, 16], F32, stk=pbb)
    s2c_t = sb("s2c_t", [128, 16], F32, stk=pbb)
    mxc_t = sb("mxc_t", [128, 16], F32, stk=pbb)
    st1_t = sb("st1_t", [128, 16], F32, stk=pbb)
    st2_t = sb("st2_t", [128, 16], F32, stk=pbb)
    mx4_t = sb("mx4_t", [128, 4], F32, stk=pbb)
    wts_t = sb("wts_t", [128, 16], F32, stk=pbb)
    oacc_t = sb("oacc_t", [128, DM], F32, stk=pbb)
    rr_t = sb("rr_t", [128, 4], F32, stk=pbb)
    obf_t = sb("obf_t", [128, DM], BF16, stk=pbb)
    ofm_t = sb("ofm_t", [128, 8 * TB], BF16, stk=pbb)
    a1o_t = sb("a1o_t", [128, 8 * TA], F32, stk=pbb)

    for fci in range(4):
        E.dma("p", fircd_t[:, fci * 76 * 128:(fci + 1) * 76 * 128],
              bass.AP(fircd.tensor, fci * 76 * 128 * 128,
                      [[128, 128], [128 * 128, 76], [1, 128]]),
              wr=["fircd"], tag=f"ld-fircd{fci}", q="pdF")
    evOP = E.dma("p", opw_t[:], oprojT, wr=["opw"], tag="ld-opw", q="pdF")
    evF = evOP = E.ev("pdF")
    E.dma("s", onw_t[:], onw_rep, wr=["onw"], tag="ld-onw", q="sdM2")
    E.dma("s", alr_t[:], alpha_rep, wr=["alr"], tag="ld-alr", q="sdM2")
    evM2 = E.ev("sdM2")

    def fmsl3(dr, c0, w, rowlen=L):
        return bass.AP(dr.tensor, c0, [[rowlen, 128], [128 * rowlen, 8], [1, w]])

    prev = {}
    adst = yout if mode == "attn1" else attn1_s
    for it in range(NTA):
        tg = f"e{it}"
        lv = E.dma("s", vfb_t[:], fmsl3(vF, it * TA, TA + 31, rowlen=31 + L),
                   wait=[prev.get("vfb_used"), E.ev("sdf4")],
                   rd=[f"d:vF:{it}", f"d:vF:{max(0, it - 1)}", "d:vFz"],
                   wr=["vfb"], tag=f"{tg}-lv", q="sdv")
        ev_fs = {}
        ev_fl = {}
        for ch in range(8):
            bs_ = P[ch % 2]
            bl_ = P[2 + ch % 2]
            evk = None
            for j in range(7):
                off = 25 + j
                evk = E.op("t", lambda t, ch=ch, j=j, off=off, bs_=bs_: t.matmul(
                    out=bs_[:, 0:TA],
                    lhsT=fircd_t[:, (ch * 38 + j) * 128:(ch * 38 + j) * 128 + 128],
                    rhs=vfb_t[:, ch * (TA + 31) + off:ch * (TA + 31) + off + TA],
                    start=(j == 0), stop=(j == 6)),
                    wait=[lv, evF, prev.get(("fsf", ch % 2))] if j == 0 else [],
                    inc=["ts"] if j == 6 else [],
                    rd=["fircd", "vfb"], wr=[f"P{ch % 2}"], tag=f"{tg}-fs{ch}.{j}")
            ev_fs[ch] = evk
            evk = None
            for j in range(31):
                off = 1 + j
                evk = E.op("t", lambda t, ch=ch, j=j, off=off, bl_=bl_: t.matmul(
                    out=bl_[:, 0:TA],
                    lhsT=fircd_t[:, (ch * 38 + 7 + j) * 128:(ch * 38 + 7 + j) * 128 + 128],
                    rhs=vfb_t[:, ch * (TA + 31) + off:ch * (TA + 31) + off + TA],
                    start=(j == 0), stop=(j == 30)),
                    wait=[lv, prev.get(("flf", ch % 2))] if j == 0 else [],
                    inc=["ts"] if j == 30 else [],
                    rd=["fircd", "vfb"], wr=[f"P{2 + ch % 2}"], tag=f"{tg}-fl{ch}.{j}")
            ev_fl[ch] = evk
            es_ = E.op("a", lambda a, ch=ch, bs_=bs_: a.activation(
                out=firf_t[:, ch * TA:(ch + 1) * TA], in_=bs_[:, 0:TA],
                func=AF.Copy), wait=[ev_fs[ch], prev.get(("firfw", ch))],
                rd=[f"P{ch % 2}"], wr=[f"firf{ch}"], tag=f"{tg}-fse{ch}")
            prev[("fsf", ch % 2)] = es_
            el_ = E.op("a", lambda a, ch=ch, bl_=bl_: a.activation(
                out=firf_t[:, (8 + ch) * TA:(9 + ch) * TA], in_=bl_[:, 0:TA],
                func=AF.Copy), wait=[ev_fl[ch], prev.get(("firfw", 8 + ch))],
                rd=[f"P{2 + ch % 2}"], wr=[f"firf{8 + ch}"], tag=f"{tg}-fle{ch}")
            prev[("flf", ch % 2)] = el_

        for hf in range(2):
            tb = 2 * it + hf
            tgh = f"{tg}.{hf}"
            lvt = E.dma("s", vT_t[:], vT[tb],
                        wait=[prev.get("vT_used"), E.ev("sdt4")],
                        rd=[f"d:vT:{tb}"], wr=["vTl"], tag=f"{tgh}-lvt", q="sdl1")
            ldt = E.dma("s", dT_t[:], deltaT[tb],
                        wait=[prev.get("dT_used"), E.ev("sdd2")],
                        rd=[f"d:dT:{tb}"], wr=["dTl"], tag=f"{tgh}-ldt", q="sdl2")
            lgt = E.dma("s", gT_t[:], gateT[tb],
                        wait=[prev.get("gT_used"), E.ev("sdg0"), E.ev("sdg1")],
                        rd=[f"d:gT:{tb}"], wr=["gTl"], tag=f"{tgh}-lgt", q="sdl3")
            for (src_base, dstb, dk) in ((0, fsT_t, "fsT"), (8, flT_t, "flT")):
                evt = None
                for c in range(8):
                    bnk = P[4 + c // 4]
                    q0 = (c % 4) * 128
                    w0 = []
                    if c % 4 == 0:
                        w0 = [prev.get(("tpf5", 4 + c // 4)),
                              E.ev("as")]
                    evt = E.op("t", lambda t, c=c, bnk=bnk, q0=q0,
                               src_base=src_base, hf=hf: t.matmul(
                        out=bnk[:, q0:q0 + 128],
                        lhsT=firf_t[:, (src_base + c) * TA + hf * 128:
                                    (src_base + c) * TA + hf * 128 + 128],
                        rhs=idf_t[:], start=(c % 4 == 0), stop=(c % 4 == 3),
                        is_transpose=True),
                        wait=w0, inc=["ts"] if c % 4 == 3 else [],
                        rd=[f"firf{src_base + c}", "idf"], wr=[f"P{4 + c // 4}"],
                        tag=f"{tgh}-tp{dk}{c}")
                e1 = E.op("v", lambda v, dstb=dstb: v.tensor_copy(
                    out=dstb[:, 0:512], in_=P[4][:, 0:512]),
                    wait=[evt, prev.get((dk, "used"))],
                    rd=["P4"], wr=[dk], tag=f"{tgh}-te{dk}a")
                e2 = E.op("v", lambda v, dstb=dstb: v.tensor_copy(
                    out=dstb[:, 512:1024], in_=P[5][:, 0:512]),
                    wait=[evt], rd=["P5"], wr=[dk], tag=f"{tgh}-te{dk}b")
                prev[("tpf5", 4)] = e1
                prev[("tpf5", 5)] = e2
            ev_tm = e2

            BRS = {0: (fsT_t, "fsT", [ev_tm]), 1: (flT_t, "flT", [ev_tm]),
                   2: (dT_t, "dTl", [ldt]), 3: (vT_t, "vTl", [lvt])}
            ev_s1 = ev_s2 = ev_mx = None
            for h in range(4):
                for br in range(4):
                    bt, bk_, wv = BRS[br]
                    col = br * 4 + h
                    ev_s1 = E.op("a", lambda a, bt=bt, h=h, col=col: a.activation(
                        out=sqs_t[:, 0:TA][:, 0:256], in_=bt[:, h * 256:(h + 1) * 256],
                        func=AF.Identity, accum_out=s1c_t[:, col:col + 1]),
                        wait=wv + [prev.get("sqs0")],
                        rd=[bk_], wr=["sqs", "s1c"], tag=f"{tgh}-s1.{col}")
                    prev["sqs0"] = ev_s1
                    ev_s2 = E.op("a", lambda a, bt=bt, h=h, col=col: a.activation(
                        out=sqs_t[:, 0:256], in_=bt[:, h * 256:(h + 1) * 256],
                        func=AF.Square, accum_out=s2c_t[:, col:col + 1]),
                        wait=[ev_s1], rd=[bk_], wr=["sqs", "s2c"],
                        tag=f"{tgh}-s2.{col}")
                    prev["sqs0"] = ev_s2
                    ev_mx = E.op("v", lambda v, bt=bt, h=h, col=col: v.tensor_reduce(
                        out=mxc_t[:, col:col + 1], in_=bt[:, h * 256:(h + 1) * 256],
                        axis=AXX, op=ALU.max),
                        wait=wv + [prev.get("mxc_used")],
                        rd=[bk_], wr=["mxc"], tag=f"{tgh}-mx.{col}")
            t1 = E.op("v", lambda v: v.tensor_scalar(
                out=st1_t[:], in0=s1c_t[:], scalar1=1.0 / 256, scalar2=None,
                op0=ALU.mult), wait=[ev_s2, prev.get("st1_used")],
                rd=["s1c"], wr=["st1"], tag=f"{tgh}-mu")
            t2 = E.op("v", lambda v: v.tensor_mul(
                out=st2_t[:], in0=s1c_t[:], in1=st1_t[:]),
                wait=[t1, prev.get("st2_used")],
                rd=["s1c", "st1"], wr=["st2"], tag=f"{tgh}-s1mu")
            t3 = E.op("v", lambda v: v.tensor_sub(
                out=st2_t[:], in0=s2c_t[:], in1=st2_t[:]), wait=[t2],
                rd=["s2c", "st2"], wr=["st2"], tag=f"{tgh}-var0")
            t4 = E.op("v", lambda v: v.tensor_scalar(
                out=st2_t[:], in0=st2_t[:], scalar1=1.0 / 255, scalar2=0.0,
                op0=ALU.mult, op1=ALU.max), wait=[t3],
                rd=["st2"], wr=["st2"], tag=f"{tgh}-var")
            t5 = E.op("a", lambda a: a.activation(
                out=st2_t[:], in_=st2_t[:], func=AF.Sqrt), wait=[t4],
                rd=["st2"], wr=["st2"], tag=f"{tgh}-std")
            t6 = E.op("v", lambda v: v.tensor_add(
                out=st1_t[:], in0=st1_t[:], in1=st2_t[:]), wait=[t5],
                rd=["st1", "st2"], wr=["st1"], tag=f"{tgh}-ms")
            t7 = E.op("v", lambda v: v.tensor_add(
                out=st1_t[:], in0=st1_t[:], in1=mxc_t[:]), wait=[t6, ev_mx],
                rd=["st1", "mxc"], wr=["st1"], tag=f"{tgh}-msx")
            prev["mxc_used"] = t7
            t8 = E.op("v", lambda v: v.tensor_scalar(
                out=st1_t[:], in0=st1_t[:], scalar1=1.0 / 3, scalar2=None,
                op0=ALU.mult), wait=[t7], rd=["st1"], wr=["st1"],
                tag=f"{tgh}-st3")
            t9 = E.op("v", lambda v: v.tensor_mul(
                out=st1_t[:], in0=st1_t[:], in1=alr_t[:]), wait=[t8, evM2],
                rd=["st1", "alr"], wr=["st1"], tag=f"{tgh}-al")
            t10 = E.op("v", lambda v: v.tensor_add(
                out=st1_t[:], in0=st1_t[:], in1=gT_t[:]), wait=[t9, lgt],
                rd=["st1", "gTl"], wr=["st1"], tag=f"{tgh}-lg")
            t11 = E.op("v", lambda v: v.tensor_reduce(
                out=mx4_t[:], in_=sap(st1_t, 0, [[1, 4], [4, 4]]),
                axis=AXX, op=ALU.max), wait=[t10, prev.get("mx4_used")],
                rd=["st1"], wr=["mx4"], tag=f"{tgh}-sm1")
            t12 = E.op("v", lambda v: v.tensor_tensor(
                out=sap(st1_t, 0, [[4, 4], [1, 4]]),
                in0=sap(st1_t, 0, [[4, 4], [1, 4]]),
                in1=sap(mx4_t, 0, [[0, 4], [1, 4]]), op=ALU.subtract),
                wait=[t11], rd=["st1", "mx4"], wr=["st1"], tag=f"{tgh}-sm2")
            t13 = E.op("a", lambda a: a.activation(
                out=st1_t[:], in_=st1_t[:], func=AF.Exp), wait=[t12],
                rd=["st1"], wr=["st1"], tag=f"{tgh}-sm3")
            t14 = E.op("v", lambda v: v.tensor_reduce(
                out=mx4_t[:], in_=sap(st1_t, 0, [[1, 4], [4, 4]]),
                axis=AXX, op=ALU.add), wait=[t13],
                rd=["st1"], wr=["mx4"], tag=f"{tgh}-sm4")
            t15 = E.op("v", lambda v: v.reciprocal(
                out=mx4_t[:], in_=mx4_t[:]), wait=[t14],
                rd=["mx4"], wr=["mx4"], tag=f"{tgh}-sm5")
            t16 = E.op("v", lambda v: v.tensor_tensor(
                out=sap(wts_t, 0, [[4, 4], [1, 4]]),
                in0=sap(st1_t, 0, [[4, 4], [1, 4]]),
                in1=sap(mx4_t, 0, [[0, 4], [1, 4]]), op=ALU.mult),
                wait=[t15, prev.get("wts_used")],
                rd=["st1", "mx4"], wr=["wts"], tag=f"{tgh}-sm6")
            prev["mx4_used"] = t16
            prev["st1_used"] = t16
            prev["st2_used"] = t5
            ev_cb = None
            for h in range(4):
                for br in range(4):
                    bt, bk_, wv = BRS[br]
                    col = br * 4 + h
                    if br == 0:
                        ev_cb = E.op("v", lambda v, bt=bt, h=h, col=col: v.tensor_scalar(
                            out=oacc_t[:, h * 256:(h + 1) * 256],
                            in0=bt[:, h * 256:(h + 1) * 256],
                            scalar1=wts_t[:, col:col + 1], scalar2=None,
                            op0=ALU.mult),
                            wait=[t16, prev.get("oacc_used")],
                            rd=[bk_, "wts"], wr=["oacc"], tag=f"{tgh}-cb{col}")
                    else:
                        ev_cb = E.op("v", lambda v, bt=bt, h=h, col=col: v.scalar_tensor_tensor(
                            out=oacc_t[:, h * 256:(h + 1) * 256],
                            in0=bt[:, h * 256:(h + 1) * 256],
                            scalar=wts_t[:, col:col + 1],
                            in1=oacc_t[:, h * 256:(h + 1) * 256],
                            op0=ALU.mult, op1=ALU.add),
                            wait=[ev_cb], rd=[bk_, "wts", "oacc"], wr=["oacc"],
                            tag=f"{tgh}-cb{col}")
            prev["fsT_used"] = ev_cb
            prev["flT_used"] = ev_cb
            prev["vT_used"] = ev_cb
            prev["dT_used"] = ev_cb
            prev[("fsT", "used")] = ev_cb
            prev[("flT", "used")] = ev_cb
            ev_rr = None
            for h in range(4):
                ev_q = E.op("a", lambda a, h=h: a.activation(
                    out=sqs_t[:, 0:256], in_=oacc_t[:, h * 256:(h + 1) * 256],
                    func=AF.Square, accum_out=rr_t[:, h:h + 1]),
                    wait=[ev_cb, prev.get("sqs0"), prev.get("rr_used")],
                    rd=["oacc"], wr=["sqs", "rr"], tag=f"{tgh}-os{h}")
                prev["sqs0"] = ev_q
            ev_r1 = E.op("a", lambda a: a.activation(
                out=rr_t[:], in_=rr_t[:], func=AF.Sqrt, scale=1.0 / 256,
                bias=eps_t[:, 0:1]), wait=[ev_q],
                rd=["rr", "eps"], wr=["rr"], tag=f"{tgh}-or")
            ev_r2 = E.op("v", lambda v: v.reciprocal(out=rr_t[:], in_=rr_t[:]),
                         wait=[ev_r1], rd=["rr"], wr=["rr"], tag=f"{tgh}-orc")
            prev["rr_used"] = ev_r2
            ev_on = None
            for h in range(4):
                ev_on = E.op("v", lambda v, h=h: v.tensor_scalar(
                    out=oacc_t[:, h * 256:(h + 1) * 256],
                    in0=oacc_t[:, h * 256:(h + 1) * 256],
                    scalar1=rr_t[:, h:h + 1], scalar2=None, op0=ALU.mult),
                    wait=[ev_r2], rd=["oacc", "rr"], wr=["oacc"],
                    tag=f"{tgh}-on{h}")
            prev["oacc_used"] = ev_on
            ev_ob = E.op("v", lambda v: v.tensor_mul(
                out=obf_t[:], in0=oacc_t[:], in1=onw_t[:]),
                wait=[ev_on, evM2, prev.get("obf_used")],
                rd=["oacc", "onw"], wr=["obf"], tag=f"{tgh}-ob")
            evt = None
            for c in range(8):
                bnk = P[4 + c // 4]
                q0 = (c % 4) * 128
                w0 = []
                if c % 4 == 0:
                    w0 = [ev_ob, prev.get(("tpf5", 4 + c // 4))]
                evt = E.op("t", lambda t, c=c, bnk=bnk, q0=q0: t.matmul(
                    out=bnk[:].bitcast(BF16)[:, q0:q0 + 128],
                    lhsT=obf_t[:, c * 128:(c + 1) * 128],
                    rhs=id_bf_t[:], start=(c % 4 == 0), stop=(c % 4 == 3),
                    is_transpose=True),
                    wait=w0, inc=["ts"] if c % 4 == 3 else [],
                    rd=["obf", "idb"], wr=[f"P{4 + c // 4}"],
                    tag=f"{tgh}-otp{c}")
            e1 = E.op("v", lambda v: v.tensor_copy(
                out=ofm_t[:, 0:512], in_=P[4][:].bitcast(BF16)[:, 0:512]),
                wait=[evt, prev.get("ofm_used")], rd=["P4"], wr=["ofm"],
                tag=f"{tgh}-oea")
            e2 = E.op("v", lambda v: v.tensor_copy(
                out=ofm_t[:, 512:1024], in_=P[5][:].bitcast(BF16)[:, 0:512]),
                wait=[evt], rd=["P5"], wr=["ofm"], tag=f"{tgh}-oeb")
            prev[("tpf5", 4)] = e1
            prev[("tpf5", 5)] = e2
            prev["obf_used"] = evt
            ev_pr = None
            for oc in range(8):
                bnk = P[6 + oc // 4]
                q0 = (oc % 4) * 128
                for kc in range(8):
                    w0 = []
                    if oc == 0 and kc == 0:
                        w0 = [e2, evOP, prev.get("P6p")]
                    elif oc == 4 and kc == 0:
                        w0 = [prev.get("P7p")]
                    ev_pr = E.op("t", lambda t, oc=oc, kc=kc, bnk=bnk, q0=q0: t.matmul(
                        out=bnk[:, q0:q0 + 128],
                        lhsT=opw_t[:, kc * DM + oc * 128:kc * DM + oc * 128 + 128],
                        rhs=ofm_t[:, kc * 128:(kc + 1) * 128],
                        start=(oc % 4 == 0 and kc == 0),
                        stop=(oc % 4 == 3 and kc == 7)),
                        wait=w0, inc=["ts"] if (oc % 4 == 3 and kc == 7) else [],
                        rd=["opw", "ofm"], wr=[f"P{6 + oc // 4}"],
                        tag=f"{tgh}-pr{oc}.{kc}")
            ev_pe1 = E.op("v", lambda v, hf=hf: v.tensor_copy(
                out=sap(a1o_t, hf * 128, [[256, 4], [1, 128]]),
                in_=P[6][:, 0:512]),
                wait=[ev_pr, prev.get("a1o_used")], rd=["P6"], wr=["a1o"],
                tag=f"{tgh}-pe1")
            ev_pe2 = E.op("v", lambda v, hf=hf: v.tensor_copy(
                out=sap(a1o_t, 4 * 256 + hf * 128, [[256, 4], [1, 128]]),
                in_=P[7][:, 0:512]),
                wait=[ev_pr], rd=["P7"], wr=["a1o"], tag=f"{tgh}-pe2")
            prev["P6p"] = ev_pe1
            prev["P7p"] = ev_pe2
            prev["ofm_used"] = ev_pr
        ev_a1 = E.dma("s", fmaj(adst[it], 8, TA), a1o_t[:],
                      wait=[ev_pe2], rd=["a1o"], wr=[f"d:a1:{it}"],
                      tag=f"{tg}-sta1", q="sda1")
        prev["a1o_used"] = ev_a1
        prev["vfb_used"] = E.ev("ts")
        prev["gT_used"] = t10
        for c in range(16):
            prev[("firfw", c)] = E.ev("ts")

    pbb.close()
    E.barrier()

    if mode == "attn1":
        _tail()
        E.verify()
        _finish(nc, E, es)
        return nc

    pcp = ExitStack()
    wnat_t = sb("wnat_t", [128, FFI], F32, stk=pcp)
    wi_t = sb("wi_t", [128, FFI], I32, stk=pcp)
    wfq_t = sb("wfq_t", [128, FFI], F32, stk=pcp)
    wdq_t = sb("wdq_t", [128, FFI], F32, stk=pcp)
    bmq_t = sb("bmq_t", [128, 96], F32, stk=pcp)
    rrq_t = sb("rrq_t", [128, 96], F32, stk=pcp)
    nsq_t = sb("nsq_t", [128, 96], F32, stk=pcp)
    wqb_t = sb("wqb_t", [128, FFI], BF16, stk=pcp)
    stq_t = sb("stq_t", [128, 2 * 512], BF16, stk=pcp)
    prev = {}

    def qbrd(t, nblk):
        a = t[:]
        return bass.AP(a.tensor, a.offset, [a.ap[0], [1, nblk], [0, 32]])

    def qtri(t, D):
        a = t[:]
        return bass.AP(a.tensor, a.offset, [a.ap[0], [32, D // 32], [1, 32]])

    def lpbq_prep(wn, O, D, dst, wkey):
        nch, nkc, nblk = O // 128, D // 128, D // 32
        for c in range(nch):
            tg = f"qp-{wkey}{c}"
            ev_l = E.dma("p", wnat_t[:, 0:D], wn[c * 128:(c + 1) * 128, :],
                         wait=[prev.get("nat_used")], wr=["wnat"],
                         tag=f"{tg}-ld", q="pdq")
            ev_b = E.op("v", lambda v, D=D, nblk=nblk: v.tensor_reduce(
                out=bmq_t[:, 0:nblk], in_=qtri(wnat_t, D), axis=AXX, op=ALU.max,
                apply_absolute_value=True),
                wait=[ev_l, prev.get("bm_used")],
                rd=["wnat"], wr=["bm"], tag=f"{tg}-bm")
            ev_s = E.op("v", lambda v, nblk=nblk: v.tensor_scalar(
                out=bmq_t[:, 0:nblk], in0=bmq_t[:, 0:nblk], scalar1=1.0 / 7.0,
                scalar2=1e-12, op0=ALU.mult, op1=ALU.max), wait=[ev_b],
                rd=["bm"], wr=["bm"], tag=f"{tg}-s")
            ev_n = E.op("v", lambda v, nblk=nblk: v.tensor_scalar(
                out=nsq_t[:, 0:nblk], in0=bmq_t[:, 0:nblk], scalar1=-1.0,
                scalar2=None, op0=ALU.mult), wait=[ev_s, prev.get("ns_used")],
                rd=["bm"], wr=["nsb"], tag=f"{tg}-ns")
            ev_r = E.op("v", lambda v, nblk=nblk: v.reciprocal(
                out=rrq_t[:, 0:nblk], in_=bmq_t[:, 0:nblk]),
                wait=[ev_s, prev.get("rr_used")],
                rd=["bm"], wr=["rr"], tag=f"{tg}-r")
            ev_y = E.op("v", lambda v, D=D, nblk=nblk: v.tensor_tensor(
                out=qtri(wdq_t, D), in0=qtri(wnat_t, D), in1=qbrd(rrq_t, nblk),
                op=ALU.mult), wait=[ev_r, prev.get("wd_used")],
                rd=["wnat", "rr"], wr=["wd"], tag=f"{tg}-y")
            ev_i = E.op("v", lambda v, D=D: v.tensor_copy(
                out=wi_t[:, 0:D], in_=wdq_t[:, 0:D]),
                wait=[ev_y, prev.get("wi_used")],
                rd=["wd"], wr=["wi"], tag=f"{tg}-i")
            ev_f = E.op("v", lambda v, D=D: v.tensor_copy(
                out=wfq_t[:, 0:D], in_=wi_t[:, 0:D]),
                wait=[ev_i, prev.get("wf_used")],
                rd=["wi"], wr=["wf"], tag=f"{tg}-f")
            ev_d = E.op("v", lambda v, D=D: v.tensor_sub(
                out=wdq_t[:, 0:D], in0=wdq_t[:, 0:D], in1=wfq_t[:, 0:D]),
                wait=[ev_f], rd=["wd", "wf"], wr=["wd"], tag=f"{tg}-d")
            ev_f2 = E.op("v", lambda v, D=D: v.scalar_tensor_tensor(
                out=wfq_t[:, 0:D], in0=wdq_t[:, 0:D], scalar=0.5,
                in1=wfq_t[:, 0:D], op0=ALU.is_ge, op1=ALU.add), wait=[ev_d],
                rd=["wd", "wf"], wr=["wf"], tag=f"{tg}-f2")
            ev_h = E.op("v", lambda v, D=D: v.scalar_tensor_tensor(
                out=wdq_t[:, 0:D], in0=wdq_t[:, 0:D], scalar=-0.5,
                in1=wfq_t[:, 0:D], op0=ALU.is_le, op1=ALU.subtract),
                wait=[ev_f2], rd=["wd", "wf"], wr=["wd"], tag=f"{tg}-h")
            ev_q = E.op("v", lambda v, D=D, nblk=nblk: v.tensor_tensor(
                out=qtri(wqb_t, D), in0=qtri(wdq_t, D), in1=qbrd(nsq_t, nblk),
                op=ALU.mult), wait=[ev_h, prev.get("wq_used")],
                rd=["wd", "nsb"], wr=["wq"], tag=f"{tg}-q")
            prev["nat_used"] = ev_y
            prev["bm_used"] = ev_r
            prev["rr_used"] = ev_y
            prev["ns_used"] = ev_q
            prev["wi_used"] = ev_f
            prev["wf_used"] = ev_h
            prev["wd_used"] = ev_q
            for kq in range(nkc // 4):
                bnk = P[kq % 2]
                done = None
                for qq in range(4):
                    kc = kq * 4 + qq
                    ov = bnk[:].bitcast(BF16)[:, qq * 128:qq * 128 + 128]
                    done = E.op("t", lambda t, kc=kc, ov=ov, qq=qq: t.matmul(
                        out=ov, lhsT=wqb_t[:, kc * 128:(kc + 1) * 128],
                        rhs=id_bf_t[:], start=(qq == 0), stop=(qq == 3),
                        is_transpose=True),
                        wait=[ev_q, prev.get(("pqf", kq % 2))] if qq == 0 else [],
                        inc=["ts"] if qq == 3 else [],
                        rd=["wq", "idb"], wr=[f"P{kq % 2}"], tag=f"{tg}-tp{kc}")
                ev_e = E.op("v", lambda v, kq=kq: v.tensor_copy(
                    out=stq_t[:, (kq % 2) * 512:(kq % 2) * 512 + 512],
                    in_=P[kq % 2][:].bitcast(BF16)[:, 0:512]),
                    wait=[done, prev.get(("stq_used", kq % 2))],
                    rd=[f"P{kq % 2}"], wr=[f"stq{kq % 2}"], tag=f"{tg}-te{kq}")
                prev[("pqf", kq % 2)] = ev_e
                ev_st = E.dma(
                    "s",
                    bass.AP(dst.tensor, kq * 4 * 128 * O + c * 128,
                            [[O, 128], [128 * O, 4], [1, 128]]),
                    stq_t[:, (kq % 2) * 512:(kq % 2) * 512 + 512],
                    wait=[ev_e], rd=[f"stq{kq % 2}"],
                    wr=[f"d:{wkey}:{c}.{kq}"], tag=f"{tg}-st{kq}",
                    q=f"sdq{kq % 2}")
                prev[("stq_used", kq % 2)] = ev_st
            prev["wq_used"] = E.ev("ts")

    lpbq_prep(wupN, FFI, DM, wupT_dq, "wu")
    lpbq_prep(wgateN, FFI, DM, wgateT_dq, "wg")
    lpbq_prep(wdownN, DM, FFI, wdownT_dq, "wdn")
    ev_sdq_tot = E.ev("sdq0")
    ev_sdq_tot1 = E.ev("sdq1")

    pcp.close()
    E.barrier()

    pc = ExitStack()
    wupT_t = sb("wupT_t", [128, 8 * FFI], BF16, stk=pc)
    wgT_t = sb("wgT_t", [128, 8 * FFI], BF16, stk=pc)
    xpc_t = sb("xpc_t", [128, 8 * TA], F32, stk=pc)
    spc_t = sb("spc_t", [128, 2 * TA], F32, stk=pc)
    rpc_t = sb("rpc_t", [1, 4 * TA], F32, stk=pc)
    ypc_t = sb("ypc_t", [128, TA], F32, stk=pc)
    ipc_t = sb("ipc_t", [128, TA], I32, stk=pc)
    fpc_t = sb("fpc_t", [128, TA], F32, stk=pc)
    xqb_t = sb("xqb_t", [128, 8 * TA], BF16, stk=pc)
    upe_t = sb("upe_t", [128, 4 * TA], F32, stk=pc)
    mx2_t = sb("mx2_t", [128, 4], F32, stk=pc)

    ev_m2 = E.op("v", lambda v: v.memset(mx2_t[:], 0.0), wr=["mx2"],
                 tag="mx2-init")
    ev_wu = None
    for kc in range(8):
        ev_wu = E.dma("p", wupT_t[:, kc * FFI:(kc + 1) * FFI],
                      wupT_dq[kc * 128:(kc + 1) * 128, :],
                      wait=[ev_sdq_tot, ev_sdq_tot1] if kc == 0 else [],
                      rd=[f"d:wu:{c}.{kc // 4}" for c in range(24)],
                      wr=["wupT"], tag=f"ld-wu{kc}", q="pdq")
    for kc in range(8):
        ev_wu = E.dma("p", wgT_t[:, kc * FFI:(kc + 1) * FFI],
                      wgateT_dq[kc * 128:(kc + 1) * 128, :],
                      rd=[f"d:wg:{c}.{kc // 4}" for c in range(24)],
                      wr=["wgT"], tag=f"ld-wg{kc}", q="pdq")

    ev_sda1_tot = E.ev("sda1")
    for it in range(NTA):
        tg = f"c{it}"
        ev_x = E.dma("s", xpc_t[:], fmaj(x1_s[it], 8, TA),
                     wait=[prev.get("x_used"), ev_sda1_tot],
                     rd=[f"d:y:{it}"], wr=["x"], tag=f"{tg}-ldx", q="sdx")
        b0c = P[4]
        mm2ev = {}
        ev_sums = None
        for c in range(8):
            ev_sq = E.op("a", lambda a, c=c: a.activation(
                out=spc_t[:, (c % 2) * TA:(c % 2) * TA + TA],
                in_=xpc_t[:, c * TA:(c + 1) * TA], func=AF.Square),
                wait=[ev_x, mm2ev.get(c % 2), prev.get("sp_free")],
                rd=["x"], wr=[f"sp{c % 2}"], tag=f"{tg}-sq{c}")
            E.op("t", lambda t, c=c: t.matmul(
                out=b0c[0:1, 0:TA], lhsT=ones_t[:, 0:1],
                rhs=xpc_t[:, c * TA:(c + 1) * TA], start=(c == 0), stop=False),
                wait=[ev_x, prev.get("b4_free")] if c == 0 else [],
                rd=["x", "ones"], wr=["P4"], tag=f"{tg}-mmx{c}")
            ev_sums = E.op("t", lambda t, c=c: t.matmul(
                out=b0c[0:1, TA:2 * TA], lhsT=ones_t[:, 0:1],
                rhs=spc_t[:, (c % 2) * TA:(c % 2) * TA + TA],
                start=False, stop=(c == 7)),
                wait=[ev_sq], inc=["ts"], rd=[f"sp{c % 2}", "ones"], wr=["P4"],
                tag=f"{tg}-mmsq{c}")
            mm2ev[c % 2] = ev_sums
        e1 = E.op("v", lambda v: v.tensor_scalar(
            out=rpc_t[0:1, 0:2 * TA], in0=b0c[0:1, 0:2 * TA], scalar1=1.0 / DM,
            scalar2=None, op0=ALU.mult), wait=[ev_sums],
            rd=["P4"], wr=["row"], tag=f"{tg}-ln1")
        prev["b4_free"] = e1
        e2 = E.op("v", lambda v: v.tensor_mul(
            out=rpc_t[0:1, 2 * TA:3 * TA], in0=rpc_t[0:1, 0:TA],
            in1=rpc_t[0:1, 0:TA]), wait=[e1], rd=["row"], wr=["row"],
            tag=f"{tg}-ln2")
        e3 = E.op("v", lambda v: v.tensor_sub(
            out=rpc_t[0:1, 2 * TA:3 * TA], in0=rpc_t[0:1, TA:2 * TA],
            in1=rpc_t[0:1, 2 * TA:3 * TA]), wait=[e2], rd=["row"], wr=["row"],
            tag=f"{tg}-ln3")
        e4a = E.op("a", lambda a: a.activation(
            out=rpc_t[0:1, 3 * TA:4 * TA], in_=rpc_t[0:1, 2 * TA:3 * TA],
            func=AF.Sqrt, bias=eps_t[0:1, 0:1]), wait=[e3],
            rd=["row", "eps"], wr=["row"], tag=f"{tg}-ln4")
        e4 = E.op("v", lambda v: v.reciprocal(
            out=rpc_t[0:1, 3 * TA:4 * TA], in_=rpc_t[0:1, 3 * TA:4 * TA]),
            wait=[e4a], rd=["row"], wr=["row"], tag=f"{tg}-ln5")
        e5 = E.op("v", lambda v: v.tensor_mul(
            out=rpc_t[0:1, 2 * TA:3 * TA], in0=rpc_t[0:1, 0:TA],
            in1=rpc_t[0:1, 3 * TA:4 * TA]), wait=[e4], rd=["row"], wr=["row"],
            tag=f"{tg}-ln6")
        e6 = E.op("v", lambda v: v.tensor_copy(
            out=rpc_t[0:1, 0:TA], in_=rpc_t[0:1, 3 * TA:4 * TA]), wait=[e5],
            rd=["row"], wr=["row"], tag=f"{tg}-ln7")
        e7 = E.op("v", lambda v: v.tensor_scalar(
            out=rpc_t[0:1, TA:2 * TA], in0=rpc_t[0:1, 2 * TA:3 * TA],
            scalar1=-1.0, scalar2=None, op0=ALU.mult), wait=[e6],
            rd=["row"], wr=["row"], tag=f"{tg}-ln8")
        b1c = P[5]
        ebc = E.op("t", lambda t: t.matmul(
            out=b1c[:, 0:2 * TA], lhsT=onesr_t[0:1, :],
            rhs=rpc_t[0:1, 0:2 * TA],
            start=True, stop=True), wait=[e7, prev.get("b5_free")], inc=["ts"],
            rd=["row", "onesr"], wr=["P5"], tag=f"{tg}-lnbc")
        ev_xq = None
        for c in range(8):
            em = E.op("v", lambda v, c=c: v.tensor_mul(
                out=spc_t[:, (c % 2) * TA:(c % 2) * TA + TA],
                in0=xpc_t[:, c * TA:(c + 1) * TA], in1=b1c[:, 0:TA]),
                wait=[ebc, mm2ev.get(c % 2)],
                rd=["x", "P5"], wr=[f"sp{c % 2}"], tag=f"{tg}-xnm{c}")
            ey0 = E.op("v", lambda v, c=c: v.tensor_tensor(
                out=ypc_t[:], in0=spc_t[:, (c % 2) * TA:(c % 2) * TA + TA],
                in1=b1c[:, TA:2 * TA], op=ALU.add),
                wait=[em, prev.get("y_used")],
                rd=[f"sp{c % 2}", "P5"], wr=["yq"], tag=f"{tg}-xna{c}")
            mm2ev[c % 2] = ey0
            ey1 = E.op("a", lambda a: a.activation(
                out=ypc_t[:], in_=ypc_t[:], func=AF.Identity,
                scale=sc_t[:, 1:2]), wait=[ey0, ev_sc1],
                rd=["yq", "sc"], wr=["yq"], tag=f"{tg}-qy{c}")
            ei = E.op("v", lambda v: v.tensor_copy(
                out=ipc_t[:], in_=ypc_t[:]), wait=[ey1, prev.get("i_used")],
                rd=["yq"], wr=["iq"], tag=f"{tg}-qi{c}")
            ef = E.op("v", lambda v: v.tensor_copy(
                out=fpc_t[:], in_=ipc_t[:]), wait=[ei, prev.get("f_used")],
                rd=["iq"], wr=["fq"], tag=f"{tg}-qf{c}")
            ed = E.op("v", lambda v: v.tensor_sub(
                out=ypc_t[:], in0=ypc_t[:], in1=fpc_t[:]), wait=[ef],
                rd=["yq", "fq"], wr=["yq"], tag=f"{tg}-qd{c}")
            ef2 = E.op("v", lambda v: v.scalar_tensor_tensor(
                out=fpc_t[:], in0=ypc_t[:], scalar=0.5, in1=fpc_t[:],
                op0=ALU.is_ge, op1=ALU.add), wait=[ed],
                rd=["yq", "fq"], wr=["fq"], tag=f"{tg}-qf2{c}")
            eh = E.op("v", lambda v: v.scalar_tensor_tensor(
                out=ypc_t[:], in0=ypc_t[:], scalar=-0.5, in1=fpc_t[:],
                op0=ALU.is_le, op1=ALU.subtract), wait=[ef2],
                rd=["yq", "fq"], wr=["yq"], tag=f"{tg}-qh{c}")
            prev["i_used"] = ef
            prev["f_used"] = eh
            ev_xq = E.op("a", lambda a, c=c: a.activation(
                out=xqb_t[:, c * TA:(c + 1) * TA], in_=ypc_t[:],
                func=AF.Identity, scale=sc_t[:, 2:3]),
                wait=[eh, prev.get("xq_used")],
                rd=["yq", "sc"], wr=["xq"], tag=f"{tg}-qs{c}")
            prev["y_used"] = ev_xq
        prev["sp_free"] = ev_xq
        prev["b5_free"] = ev_xq
        prev["x_used"] = ev_xq

        ev_red = None
        for oc in range(24):
            for (wt, wky, boff, mcol, stash, dk, qn) in (
                    (wupT_t, "wupT", 0, 0, up_s, "u", "sdu"),
                    (wgT_t, "wgT", 2, 1, gate_s, "g", "sdg2")):
                bnk = P[boff + oc % 2]
                bky = f"P{boff + oc % 2}"
                ev_mm = None
                for kc in range(8):
                    w0 = []
                    if kc == 0:
                        w0 = [ev_xq, ev_wu, prev.get(("pf", bky))]
                    ev_mm = E.op("t", lambda t, oc=oc, kc=kc, bnk=bnk,
                                 wt=wt: t.matmul(
                        out=bnk[:, 0:TA],
                        lhsT=wt[:, kc * FFI + oc * 128:kc * FFI + oc * 128 + 128],
                        rhs=xqb_t[:, kc * TA:(kc + 1) * TA],
                        start=(kc == 0), stop=(kc == 7)),
                        wait=w0, inc=["ts"] if kc == 7 else [],
                        rd=[wky, "xq"], wr=[bky], tag=f"{tg}-{dk}{oc}.{kc}")
                slot = boff + oc % 2
                ev_ev = E.op("a", lambda a, bnk=bnk, slot=slot: a.activation(
                    out=upe_t[:, slot * TA:(slot + 1) * TA], in_=bnk[:, 0:TA],
                    func=AF.Identity),
                    wait=[ev_mm, prev.get(("ue", slot)),
                          prev.get(("uer", slot))],
                    rd=[bky], wr=[f"ue{slot}"], tag=f"{tg}-{dk}e{oc}")
                ev_st = E.dma("p",
                              bass.AP(stash.tensor,
                                      (it * FFI + oc * 128) * TA,
                                      [[TA, 128], [1, TA]]),
                              upe_t[:, slot * TA:(slot + 1) * TA],
                              wait=[ev_ev], rd=[f"ue{slot}"],
                              wr=[f"d:{dk}s:{it}.{oc}"],
                              tag=f"{tg}-{dk}st{oc}", q=f"{qn}{oc % 2}")
                ev_rd = E.op("v", lambda v, slot=slot: v.tensor_reduce(
                    out=mx2_t[:, 2:3], in_=upe_t[:, slot * TA:(slot + 1) * TA],
                    axis=AXX, op=ALU.max, apply_absolute_value=True),
                    wait=[ev_ev, ev_red if ev_red is not None else ev_m2],
                    rd=[f"ue{slot}"], wr=["mx2t"], tag=f"{tg}-{dk}r{oc}")
                ev_red = E.op("v", lambda v, mcol=mcol: v.tensor_tensor(
                    out=mx2_t[:, mcol:mcol + 1], in0=mx2_t[:, mcol:mcol + 1],
                    in1=mx2_t[:, 2:3], op=ALU.max), wait=[ev_rd, ev_m2],
                    rd=["mx2t", "mx2"], wr=["mx2"], tag=f"{tg}-{dk}m{oc}")
                prev[("pf", bky)] = ev_ev
                prev[("ue", slot)] = ev_st
                prev[("uer", slot)] = ev_rd
        prev["xq_used"] = E.ev("ts")
        prev["mx_tail"] = ev_red

    ev_d1 = E.dma("s", bass.AP(cc_in[1].tensor, 0, [[1, 128]]), mx2_t[:, 0:1],
                  wait=[prev.get("mx_tail")], rd=["mx2"], wr=["d:cc1i"],
                  tag="cc1-st0", q="sdcc")
    ev_d2 = E.dma("s", bass.AP(cc_in[1].tensor, 128, [[1, 128]]), mx2_t[:, 1:2],
                  rd=["mx2"], wr=["d:cc1i"], tag="cc1-st1", q="sdcc")
    ev_cc1 = E.op("p", lambda p: p.collective_compute(
        "AllReduce", ALU.max, replica_groups=[list(range(NCORES))],
        ins=[cc_in[1][0:256]], outs=[cc_out[1][0:256]]),
        wait=[ev_d2], inc=["ps"], rd=["d:cc1i"], wr=["d:cc1o"],
        tag="cc1")

    def scale_tail(cco, cc_done, n, dstc, tagp, bank):
        ev_l = E.dma("s", rowc_t[0:1, 0:n * 128],
                     bass.AP(cco.tensor, 0, [[n * 128, 1], [1, n * 128]]),
                     wait=[cc_done, E.ev("vs"), prev.get("rowc_used")],
                     rd=[f"d:{tagp}o"], wr=["rowc"], tag=f"{tagp}-ld", q="sdcc")
        base = n * 128
        evs = []
        for i in range(n):
            er = E.op("v", lambda v, i=i, base=base: v.tensor_reduce(
                out=rowc_t[0:1, base + 4 * i:base + 4 * i + 1],
                in_=rowc_t[0:1, i * 128:(i + 1) * 128], axis=AXX, op=ALU.max),
                wait=[ev_l], rd=["rowc"], wr=["rowc"], tag=f"{tagp}-red{i}")
            es_ = E.op("v", lambda v, i=i, base=base: v.tensor_scalar(
                out=rowc_t[0:1, base + 4 * i + 1:base + 4 * i + 2],
                in0=rowc_t[0:1, base + 4 * i:base + 4 * i + 1],
                scalar1=1.0 / 32767.0, scalar2=1e-12, op0=ALU.mult,
                op1=ALU.max), wait=[er], rd=["rowc"], wr=["rowc"],
                tag=f"{tagp}-s{i}")
            err_ = E.op("v", lambda v, i=i, base=base: v.reciprocal(
                out=rowc_t[0:1, base + 4 * i + 2:base + 4 * i + 3],
                in_=rowc_t[0:1, base + 4 * i + 1:base + 4 * i + 2]),
                wait=[es_], rd=["rowc"], wr=["rowc"], tag=f"{tagp}-r{i}")
            en_ = E.op("v", lambda v, i=i, base=base: v.tensor_scalar(
                out=rowc_t[0:1, base + 4 * i + 3:base + 4 * i + 4],
                in0=rowc_t[0:1, base + 4 * i + 1:base + 4 * i + 2],
                scalar1=-1.0, scalar2=None, op0=ALU.mult), wait=[err_],
                rd=["rowc"], wr=["rowc"], tag=f"{tagp}-n{i}")
            evs.append(en_)
        eb = E.op("t", lambda t, n=n, base=base, bank=bank: t.matmul(
            out=bank[:, 0:2 * n],
            lhsT=onesr_t[0:1, :],
            rhs=bass.AP(rowc_t[:].tensor, rowc_t[:].offset + base + 2,
                        [rowc_t[:].ap[0], [4, n], [1, 2]]),
            start=True, stop=True), wait=evs + [prev.get("scb_free")],
            inc=["ts"], rd=["rowc", "onesr"], wr=["Psc"], tag=f"{tagp}-bc")
        ev = E.op("v", lambda v, n=n, dstc=dstc, bank=bank: v.tensor_copy(
            out=sc_t[:, dstc:dstc + 2 * n], in_=bank[:, 0:2 * n]),
            wait=[eb], rd=["Psc"], wr=["sc"], tag=f"{tagp}-ev")
        prev["scb_free"] = ev
        prev["rowc_used"] = ev
        return ev

    ev_sc2 = scale_tail(cc_out[1], ev_cc1, 2, 3, "cc1", P[6])

    pc.close()
    E.barrier()

    pda = ExitStack()
    g2_t = sb("g2_t", [128, 2 * 1024], F32, stk=pda)
    ya_t = sb("ya_t", [128, 2 * 1024], F32, stk=pda)
    ia_t = sb("ia_t", [128, 2 * 1024], I32, stk=pda)
    fa_t = sb("fa_t", [128, 2 * 1024], F32, stk=pda)
    gq2_t = sb("gq2_t", [128, 2 * 1024], F32, stk=pda)
    gs2_t = sb("gs2_t", [128, 2 * 1024], F32, stk=pda)
    mx3_t = sb("mx3_t", [128, 4], F32, stk=pda)
    ev_m3 = E.op("v", lambda v: v.memset(mx3_t[:], 0.0), wr=["mx3"],
                 tag="mx3-init")

    def qdq_chain(src_ev, src_ap, skey, rcol, nscol, dst, dkey, tg, slot,
                  ya_q, ia_q, fa_q, sc_ev, ya_wait=None, W=512):
        sl0 = slot * W
        e1 = E.op("a", lambda a, ya_q=ya_q, sl0=sl0, src_ap=src_ap,
                  rcol=rcol: a.activation(
            out=ya_q[:, sl0:sl0 + W], in_=src_ap, func=AF.Identity,
            scale=sc_t[:, rcol:rcol + 1]),
            wait=[src_ev, sc_ev, ya_wait, prev.get(("yafr", slot))],
            rd=[skey, "sc"], wr=[f"ya{slot}"], tag=f"{tg}-y")
        e2 = E.op("v", lambda v, ya_q=ya_q, ia_q=ia_q, sl0=sl0: v.tensor_copy(
            out=ia_q[:, sl0:sl0 + W], in_=ya_q[:, sl0:sl0 + W]),
            wait=[e1, prev.get(("ia_used", slot))],
            rd=[f"ya{slot}"], wr=[f"ia{slot}"], tag=f"{tg}-i")
        e3 = E.op("v", lambda v, ia_q=ia_q, fa_q=fa_q, sl0=sl0: v.tensor_copy(
            out=fa_q[:, sl0:sl0 + W], in_=ia_q[:, sl0:sl0 + W]),
            wait=[e2, prev.get(("fa_used", slot))],
            rd=[f"ia{slot}"], wr=[f"fa{slot}"], tag=f"{tg}-f")
        e4 = E.op("v", lambda v, ya_q=ya_q, fa_q=fa_q, sl0=sl0: v.tensor_sub(
            out=ya_q[:, sl0:sl0 + W], in0=ya_q[:, sl0:sl0 + W],
            in1=fa_q[:, sl0:sl0 + W]), wait=[e3],
            rd=[f"ya{slot}", f"fa{slot}"], wr=[f"ya{slot}"], tag=f"{tg}-d")
        e5 = E.op("v", lambda v, ya_q=ya_q, fa_q=fa_q,
                  sl0=sl0: v.scalar_tensor_tensor(
            out=fa_q[:, sl0:sl0 + W], in0=ya_q[:, sl0:sl0 + W],
            scalar=0.5, in1=fa_q[:, sl0:sl0 + W],
            op0=ALU.is_ge, op1=ALU.add), wait=[e4],
            rd=[f"ya{slot}", f"fa{slot}"], wr=[f"fa{slot}"], tag=f"{tg}-f2")
        e6 = E.op("v", lambda v, ya_q=ya_q, fa_q=fa_q,
                  sl0=sl0: v.scalar_tensor_tensor(
            out=ya_q[:, sl0:sl0 + W], in0=ya_q[:, sl0:sl0 + W],
            scalar=-0.5, in1=fa_q[:, sl0:sl0 + W],
            op0=ALU.is_le, op1=ALU.subtract), wait=[e5],
            rd=[f"ya{slot}", f"fa{slot}"], wr=[f"ya{slot}"], tag=f"{tg}-h")
        prev[("ia_used", slot)] = e3
        prev[("fa_used", slot)] = e6
        e7 = E.op("a", lambda a, dst=dst, ya_q=ya_q, sl0=sl0,
                  nscol=nscol: a.activation(
            out=dst, in_=ya_q[:, sl0:sl0 + W], func=AF.Identity,
            scale=sc_t[:, nscol:nscol + 1]),
            wait=[e6, prev.get(dkey + "_used")],
            rd=[f"ya{slot}", "sc"], wr=[dkey], tag=f"{tg}-o")
        return e7

    def ffn_tile(dr, it, ob):
        return bass.AP(dr.tensor, (it * FFI + ob * 512) * TA,
                       [[TA, 128], [128 * TA, 4], [1, TA]])

    ev_g2tots = [E.ev("sdg20"), E.ev("sdg21")]
    ev_red = None
    for it in range(NTA):
        for ob in range(6):
            tg = f"da{it}.{ob}"
            slot = ob % 2
            sl0 = slot * 1024
            ld = E.dma("s", g2_t[:, sl0:sl0 + 1024], ffn_tile(gate_s, it, ob),
                       wait=[prev.get(("g2", slot))] + ev_g2tots,
                       rd=[f"d:gs:{it}.{oc}" for oc in range(ob * 4, ob * 4 + 4)],
                       wr=[f"g2{slot}"],
                       tag=f"{tg}-ld", q=f"sdda{slot}")
            e_gq = qdq_chain(ld, g2_t[:, sl0:sl0 + 1024], f"g2{slot}", 5, 6,
                             gq2_t[:, sl0:sl0 + 1024], f"gq2{slot}", tg, slot,
                             ya_t, ia_t, fa_t, ev_sc2, W=1024)
            prev[("g2", slot)] = e_gq
            e_sg = E.op("a", lambda a, sl0=sl0: a.activation(
                out=ya_t[:, sl0:sl0 + 1024], in_=gq2_t[:, sl0:sl0 + 1024],
                func=AF.Sigmoid),
                wait=[e_gq], rd=[f"gq2{slot}"], wr=[f"ya{slot}"],
                tag=f"{tg}-sg")
            e_ym = E.op("a", lambda a, sl0=sl0: a.activation(
                out=ya_t[:, sl0:sl0 + 1024], in_=ya_t[:, sl0:sl0 + 1024],
                func=AF.Identity, scale=c64k_t[:, 0:1]), wait=[e_sg],
                rd=[f"ya{slot}", "c64k"], wr=[f"ya{slot}"], tag=f"{tg}-ym")
            e2 = E.op("v", lambda v, sl0=sl0: v.tensor_copy(
                out=ia_t[:, sl0:sl0 + 1024], in_=ya_t[:, sl0:sl0 + 1024]),
                wait=[e_ym, prev.get(("ia_used", slot))],
                rd=[f"ya{slot}"], wr=[f"ia{slot}"], tag=f"{tg}-i2")
            e3 = E.op("v", lambda v, sl0=sl0: v.tensor_copy(
                out=fa_t[:, sl0:sl0 + 1024], in_=ia_t[:, sl0:sl0 + 1024]),
                wait=[e2, prev.get(("fa_used", slot))],
                rd=[f"ia{slot}"], wr=[f"fa{slot}"], tag=f"{tg}-f3")
            e4 = E.op("v", lambda v, sl0=sl0: v.tensor_sub(
                out=ya_t[:, sl0:sl0 + 1024], in0=ya_t[:, sl0:sl0 + 1024],
                in1=fa_t[:, sl0:sl0 + 1024]), wait=[e3],
                rd=[f"ya{slot}", f"fa{slot}"], wr=[f"ya{slot}"],
                tag=f"{tg}-d2")
            e5 = E.op("v", lambda v, sl0=sl0: v.scalar_tensor_tensor(
                out=fa_t[:, sl0:sl0 + 1024], in0=ya_t[:, sl0:sl0 + 1024],
                scalar=0.5, in1=fa_t[:, sl0:sl0 + 1024],
                op0=ALU.is_ge, op1=ALU.add), wait=[e4],
                rd=[f"ya{slot}", f"fa{slot}"], wr=[f"fa{slot}"],
                tag=f"{tg}-f4")
            e6 = E.op("v", lambda v, sl0=sl0: v.scalar_tensor_tensor(
                out=ya_t[:, sl0:sl0 + 1024], in0=ya_t[:, sl0:sl0 + 1024],
                scalar=-0.5, in1=fa_t[:, sl0:sl0 + 1024],
                op0=ALU.is_le, op1=ALU.subtract), wait=[e5],
                rd=[f"ya{slot}", f"fa{slot}"], wr=[f"ya{slot}"],
                tag=f"{tg}-h2")
            prev[("ia_used", slot)] = e3
            prev[("fa_used", slot)] = e6
            e7 = E.op("v", lambda v, sl0=sl0: v.tensor_scalar(
                out=ya_t[:, sl0:sl0 + 1024], in0=ya_t[:, sl0:sl0 + 1024],
                scalar1=-65535.0, scalar2=None, op0=ALU.max), wait=[e6],
                rd=[f"ya{slot}"], wr=[f"ya{slot}"], tag=f"{tg}-cl")
            e8 = E.op("v", lambda v, sl0=sl0: v.scalar_tensor_tensor(
                out=gs2_t[:, sl0:sl0 + 1024], in0=ya_t[:, sl0:sl0 + 1024],
                scalar=-1.0 / 65536.0, in1=gq2_t[:, sl0:sl0 + 1024],
                op0=ALU.mult, op1=ALU.mult),
                wait=[e7, prev.get(("gs2_used", slot))],
                rd=[f"ya{slot}", f"gq2{slot}"], wr=[f"gs2{slot}"],
                tag=f"{tg}-gs")
            prev[("yafr", slot)] = e8
            prev[f"gq2{slot}_used"] = e8
            ev_st = E.dma("p", ffn_tile(gs_s, it, ob),
                          gs2_t[:, sl0:sl0 + 1024],
                          wait=[e8], rd=[f"gs2{slot}"],
                          wr=[f"d:gss:{it}.{ob}"],
                          tag=f"{tg}-st", q=f"sddb{slot}")
            ev_rd = E.op("v", lambda v, sl0=sl0: v.tensor_reduce(
                out=mx3_t[:, 2:3], in_=gs2_t[:, sl0:sl0 + 1024], axis=AXX,
                op=ALU.max, apply_absolute_value=True),
                wait=[e8, ev_red if ev_red is not None else ev_m3],
                rd=[f"gs2{slot}"], wr=["mx3t"], tag=f"{tg}-mr")
            ev_red = E.op("v", lambda v: v.tensor_tensor(
                out=mx3_t[:, 0:1], in0=mx3_t[:, 0:1], in1=mx3_t[:, 2:3],
                op=ALU.max), wait=[ev_rd, ev_m3],
                rd=["mx3t", "mx3"], wr=["mx3"], tag=f"{tg}-mm")
            prev[("gs2_used", slot)] = ev_st
            prev[("gs2r", slot)] = ev_rd

    ev_d3 = E.dma("s", bass.AP(cc_in[2].tensor, 0, [[1, 128]]), mx3_t[:, 0:1],
                  wait=[ev_red], rd=["mx3"], wr=["d:cc2i"],
                  tag="cc2-st", q="sdcc")
    ev_cc2 = E.op("p", lambda p: p.collective_compute(
        "AllReduce", ALU.max, replica_groups=[list(range(NCORES))],
        ins=[cc_in[2][0:128]], outs=[cc_out[2][0:128]]),
        wait=[ev_d3], inc=["ps"], rd=["d:cc2i"], wr=["d:cc2o"], tag="cc2")
    ev_sc3 = scale_tail(cc_out[2], ev_cc2, 1, 7, "cc2", P[6])

    pda.close()
    E.barrier()

    pdb = ExitStack()
    g2b_t = sb("g2b_t", [128, 2 * 1024], F32, stk=pdb)
    u2b_t = sb("u2b_t", [128, 2 * 1024], F32, stk=pdb)
    yab_t = sb("yab_t", [128, 2 * 1024], F32, stk=pdb)
    iab_t = sb("iab_t", [128, 2 * 1024], I32, stk=pdb)
    fab_t = sb("fab_t", [128, 2 * 1024], F32, stk=pdb)
    gq2b_t = sb("gq2b_t", [128, 2 * 1024], F32, stk=pdb)
    gs2b_t = sb("gs2b_t", [128, 2 * 1024], F32, stk=pdb)
    mx3b_t = sb("mx3b_t", [128, 4], F32, stk=pdb)
    ev_m3b = E.op("v", lambda v: v.memset(mx3b_t[:], 0.0), wr=["mx3"],
                  tag="mx3b-init")
    for s_ in range(2):
        prev[("ia_used", s_)] = None
        prev[("fa_used", s_)] = None
        prev[("yafr", s_)] = None
        prev[("g2", s_)] = None
        prev[("u2", s_)] = None
        prev[("gs2_used", s_)] = None
        prev[f"gq2{s_}_used"] = None
        prev[f"uq2{s_}_used"] = None
        prev[f"gsq2{s_}_used"] = None

    ev_utots = [E.ev("sdu0"), E.ev("sdu1")]
    ev_gstots = [E.ev("sddb0"), E.ev("sddb1")]
    ev_red = None
    for it in range(NTA):
        for ob in range(6):
            tg = f"db{it}.{ob}"
            slot = ob % 2
            sl0 = slot * 1024
            ldg = E.dma("s", g2b_t[:, sl0:sl0 + 1024], ffn_tile(gs_s, it, ob),
                        wait=[prev.get(("g2", slot))] + ev_gstots,
                        rd=[f"d:gss:{it}.{ob}"], wr=[f"g2{slot}"],
                        tag=f"{tg}-lg", q=f"sddc{slot}")
            ldu = E.dma("s", u2b_t[:, sl0:sl0 + 1024], ffn_tile(up_s, it, ob),
                        wait=[prev.get(("u2", slot))] + ev_utots,
                        rd=[f"d:us:{it}.{oc}" for oc in range(ob * 4, ob * 4 + 4)],
                        wr=[f"u2{slot}"],
                        tag=f"{tg}-lu", q=f"sddd{slot}")
            e_uq = qdq_chain(ldu, u2b_t[:, sl0:sl0 + 1024], f"u2{slot}", 3, 4,
                             gq2b_t[:, sl0:sl0 + 1024], f"uq2{slot}", tg + "u",
                             slot, yab_t, iab_t, fab_t, ev_sc3, W=1024)
            prev[("u2", slot)] = e_uq
            e_gq = qdq_chain(ldg, g2b_t[:, sl0:sl0 + 1024], f"g2{slot}", 7, 8,
                             gs2b_t[:, sl0:sl0 + 1024], f"gsq2{slot}",
                             tg + "g", slot, yab_t, iab_t, fab_t, ev_sc3,
                             ya_wait=e_uq, W=1024)
            prev[("g2", slot)] = e_gq
            e_o = E.op("v", lambda v, sl0=sl0: v.tensor_mul(
                out=gs2b_t[:, sl0:sl0 + 1024], in0=gs2b_t[:, sl0:sl0 + 1024],
                in1=gq2b_t[:, sl0:sl0 + 1024]),
                wait=[e_gq], rd=[f"gsq2{slot}", f"uq2{slot}"],
                wr=[f"gsq2{slot}"], tag=f"{tg}-o")
            prev[("yafr", slot)] = e_gq
            prev[f"uq2{slot}_used"] = e_o
            ev_st = E.dma("p", ffn_tile(op_s, it, ob),
                          gs2b_t[:, sl0:sl0 + 1024],
                          wait=[e_o], rd=[f"gsq2{slot}"],
                          wr=[f"d:ops:{it}.{ob}"],
                          tag=f"{tg}-st", q=f"sdde{slot}")
            ev_rd = E.op("v", lambda v, sl0=sl0: v.tensor_reduce(
                out=mx3b_t[:, 2:3], in_=gs2b_t[:, sl0:sl0 + 1024], axis=AXX,
                op=ALU.max, apply_absolute_value=True),
                wait=[e_o, ev_red if ev_red is not None else ev_m3b],
                rd=[f"gsq2{slot}"], wr=["mx3t"], tag=f"{tg}-mr")
            ev_red = E.op("v", lambda v: v.tensor_tensor(
                out=mx3b_t[:, 0:1], in0=mx3b_t[:, 0:1], in1=mx3b_t[:, 2:3],
                op=ALU.max), wait=[ev_rd, ev_m3b],
                rd=["mx3t", "mx3"], wr=["mx3"], tag=f"{tg}-mm")
            prev[f"gsq2{slot}_used"] = ev_st

    ev_d4 = E.dma("s", bass.AP(cc_in[3].tensor, 0, [[1, 128]]), mx3b_t[:, 0:1],
                  wait=[ev_red], rd=["mx3"], wr=["d:cc3i"],
                  tag="cc3-st", q="sdcc")
    ev_cc3 = E.op("p", lambda p: p.collective_compute(
        "AllReduce", ALU.max, replica_groups=[list(range(NCORES))],
        ins=[cc_in[3][0:128]], outs=[cc_out[3][0:128]]),
        wait=[ev_d4], inc=["ps"], rd=["d:cc3i"], wr=["d:cc3o"], tag="cc3")
    ev_sc4 = scale_tail(cc_out[3], ev_cc3, 1, 9, "cc3", P[6])

    pdb.close()
    E.barrier()

    pdc = ExitStack()
    wdnT_t = sb("wdnT_t", [128, 24 * DM], BF16, stk=pdc)
    o2c_t = sb("o2c_t", [128, 2 * 1024], F32, stk=pdc)
    yac_t = sb("yac_t", [128, 2 * 1024], F32, stk=pdc)
    iac_t = sb("iac_t", [128, 2 * 1024], I32, stk=pdc)
    fac_t = sb("fac_t", [128, 2 * 1024], F32, stk=pdc)
    oqb_t = sb("oqb_t", [128, 24 * TA], BF16, stk=pdc)
    xf_t = sb("xf_t", [128, 8 * TA], F32, stk=pdc)
    af_t = sb("af_t", [128, 8 * TA], F32, stk=pdc)
    yo_t = sb("yo_t", [128, 8 * TA], F32, stk=pdc)
    ofe_t = sb("ofe_t", [128, 2 * TA], F32, stk=pdc)
    for s_ in range(2):
        prev[("ia_used", s_)] = None
        prev[("fa_used", s_)] = None
        prev[("yafr", s_)] = None
        prev[("o2", s_)] = None
    prev["oqb_used"] = None

    ev_wd = None
    for kc in range(24):
        ev_wd = E.dma("p", wdnT_t[:, kc * DM:(kc + 1) * DM],
                      wdownT_dq[kc * 128:(kc + 1) * 128, :],
                      wait=[ev_sdq_tot, ev_sdq_tot1] if kc == 0 else [],
                      rd=[f"d:wdn:{c}.{kc // 4}" for c in range(8)],
                      wr=["wdnT"], tag=f"ld-wdn{kc}", q="pdq")

    ev_optots = [E.ev("sdde0"), E.ev("sdde1")]
    for it in range(NTA):
        tg = f"dc{it}"
        ev_x = E.dma("s", xf_t[:], fmaj(x1_s[it], 8, TA),
                     wait=[prev.get("xf_used")],
                     rd=[f"d:y:{it}"], wr=["xf"], tag=f"{tg}-ldx", q="sdx")
        ev_a = E.dma("s", af_t[:], fmaj(attn1_s[it], 8, TA),
                     wait=[prev.get("af_used"), E.ev("sda1")],
                     rd=[f"d:a1:{it}"], wr=["af"], tag=f"{tg}-lda", q="sdl1")
        ev_oq = None
        for ob in range(6):
            slot = ob % 2
            sl0 = slot * 1024
            ld = E.dma("s", o2c_t[:, sl0:sl0 + 1024], ffn_tile(op_s, it, ob),
                       wait=[prev.get(("o2", slot))] + ev_optots,
                       rd=[f"d:ops:{it}.{ob}"], wr=[f"o2{slot}"],
                       tag=f"{tg}-lo{ob}", q=f"sddf{slot}")
            ev_oq = qdq_chain(
                ld, o2c_t[:, sl0:sl0 + 1024], f"o2{slot}", 9, 10,
                oqb_t[:, ob * 1024:(ob + 1) * 1024], "oqb",
                f"{tg}-q{ob}", slot, yac_t, iac_t, fac_t, ev_sc4,
                ya_wait=prev.get("oqb_used"), W=1024)
            prev[("o2", slot)] = ev_oq
            prev[("yafr", slot)] = ev_oq
        ev_pr = None
        e_a1 = e_a2 = None
        for oc in range(8):
            bnk = P[oc % 2]
            bky = f"P{oc % 2}"
            for kc in range(24):
                w0 = []
                if kc == 0:
                    w0 = [ev_oq, ev_wd, prev.get(("dpf", oc % 2))]
                ev_pr = E.op("t", lambda t, oc=oc, kc=kc, bnk=bnk: t.matmul(
                    out=bnk[:, 0:TA],
                    lhsT=wdnT_t[:, kc * DM + oc * 128:kc * DM + oc * 128 + 128],
                    rhs=oqb_t[:, kc * TA:(kc + 1) * TA],
                    start=(kc == 0), stop=(kc == 23)),
                    wait=w0, inc=["ts"] if kc == 23 else [],
                    rd=["wdnT", "oqb"], wr=[bky], tag=f"{tg}-dn{oc}.{kc}")
            ev_ev = E.op("a", lambda a, oc=oc, bnk=bnk: a.activation(
                out=ofe_t[:, (oc % 2) * TA:(oc % 2) * TA + TA],
                in_=bnk[:, 0:TA], func=AF.Identity),
                wait=[ev_pr, prev.get(("ofe", oc % 2))],
                rd=[bky], wr=[f"ofe{oc % 2}"], tag=f"{tg}-de{oc}")
            prev[("dpf", oc % 2)] = ev_ev
            e_a1 = E.op("v", lambda v, oc=oc: v.tensor_tensor(
                out=yo_t[:, oc * TA:(oc + 1) * TA],
                in0=ofe_t[:, (oc % 2) * TA:(oc % 2) * TA + TA],
                in1=xf_t[:, oc * TA:(oc + 1) * TA], op=ALU.add),
                wait=[ev_ev, ev_x, prev.get("yo_used")],
                rd=[f"ofe{oc % 2}", "xf"], wr=["yo"], tag=f"{tg}-a1.{oc}")
            e_a2 = E.op("v", lambda v, oc=oc: v.tensor_tensor(
                out=yo_t[:, oc * TA:(oc + 1) * TA],
                in0=yo_t[:, oc * TA:(oc + 1) * TA],
                in1=af_t[:, oc * TA:(oc + 1) * TA], op=ALU.add),
                wait=[e_a1, ev_a], rd=["yo", "af"], wr=["yo"],
                tag=f"{tg}-a2.{oc}")
            prev[("ofe", oc % 2)] = e_a1
        prev["xf_used"] = e_a1
        prev["af_used"] = e_a2
        prev["oqb_used"] = E.ev("ts")
        ev_yo = E.dma("s", fmaj(yout[it], 8, TA), yo_t[:],
                      wait=[e_a2], rd=["yo"], wr=[f"d:yout:{it}"],
                      tag=f"{tg}-sty", q="sdy")
        prev["yo_used"] = ev_yo

    pdc.close()
    _tail()
    E.verify()
    _finish(nc, E, es)
    return nc


def _finish(nc, E, es):
    with nc.Block() as block:
        @block.tensor
        def _(eng):
            for f in E.q["t"]:
                f(eng)

        @block.vector
        def _(eng):
            for f in E.q["v"]:
                f(eng)

        @block.scalar
        def _(eng):
            for f in E.q["a"]:
                f(eng)

        @block.gpsimd
        def _(eng):
            for f in E.q["p"]:
                f(eng)

        @block.sync
        def _(eng):
            for f in E.q["s"]:
                f(eng)

    es.close()


def _prep_inputs(inp, core):
    f32 = np.float32
    x = np.asarray(inp["L0_attn_x"][core], f32)
    x0 = np.ascontiguousarray(x.T).reshape(DM, NTA, TA).transpose(1, 0, 2)
    d = {}
    d["x0"] = np.ascontiguousarray(x0)
    d["w_in"] = np.ascontiguousarray(np.asarray(inp["L0_attn_W_in"], f32))
    d["w_x"] = np.ascontiguousarray(np.asarray(inp["L0_attn_W_x"], f32).reshape(16, 128, 33))
    d["w_out"] = np.ascontiguousarray(np.asarray(inp["L0_attn_W_out"], f32))
    cw = np.asarray(inp["L0_attn_conv_w"], f32)
    cd = np.zeros((64, 128, 128), f32)
    for ch in range(16):
        for k in range(4):
            cd[ch * 4 + k] = np.diag(cw[ch * 128:(ch + 1) * 128, k])
    d["l0cd"] = cd
    d["conv_b"] = np.ascontiguousarray(np.asarray(inp["L0_attn_conv_b"], f32).reshape(16, 128).T)
    d["dt_w"] = np.ascontiguousarray(np.asarray(inp["L0_attn_dt_w"], f32).reshape(16, 128).T)
    d["dt_b"] = np.ascontiguousarray(np.asarray(inp["L0_attn_dt_b"], f32).reshape(16, 128).T)
    d["a_log"] = np.ascontiguousarray(
        np.asarray(inp["L0_attn_A_log"], f32).reshape(16, 128, 16)
        .transpose(1, 0, 2).reshape(128, 256))
    d["d_res"] = np.ascontiguousarray(np.asarray(inp["L0_attn_D"], f32).reshape(16, 128).T)
    d["w1n"] = np.ascontiguousarray(np.asarray(inp["L0_ffn_w1"], f32))
    d["w1t"] = np.ascontiguousarray(np.asarray(inp["L0_ffn_w1"], f32).T)
    d["w2n"] = np.ascontiguousarray(np.asarray(inp["L0_ffn_w2"], f32))
    d["w2t"] = np.ascontiguousarray(np.asarray(inp["L0_ffn_w2"], f32).T)
    d["b1"] = np.ascontiguousarray(np.asarray(inp["L0_ffn_b1"], f32).reshape(32, 128).T)
    d["b2"] = np.ascontiguousarray(np.asarray(inp["L0_ffn_b2"], f32).reshape(8, 128).T)
    d["ones_f"] = np.ones((128, 2), f32)
    d["ones_r"] = np.ones((1, 128), f32)
    def kcmaj(a):
        R, C = a.shape
        n = R // 128
        return np.ascontiguousarray(
            a.reshape(n, 128, C).transpose(1, 0, 2).reshape(R, C))

    d["qwT"] = kcmaj(np.asarray(inp["L1_attn_q_proj_w"], f32).T)
    d["kwT"] = kcmaj(np.asarray(inp["L1_attn_k_proj_w"], f32).T)
    d["vwT"] = kcmaj(np.asarray(inp["L1_attn_v_proj_w"], f32).T)
    d["bwT"] = kcmaj(np.asarray(inp["L1_attn_b_proj_w"], f32).T)
    d["gm1T"] = kcmaj(np.asarray(inp["L1_attn_gmix_w1"], f32).T)
    d["gm2T"] = kcmaj(np.asarray(inp["L1_attn_gmix_w2"], f32).T)
    d["gmb1"] = np.ascontiguousarray(np.asarray(inp["L1_attn_gmix_b1"], f32).reshape(16, 128).T)
    d["gmb2"] = np.ascontiguousarray(np.asarray(inp["L1_attn_gmix_b2"], f32).reshape(16, 1))
    qkvcd = np.zeros((96, 128, 128), f32)
    for wi, wn in enumerate(["L1_attn_q_conv_w", "L1_attn_k_conv_w", "L1_attn_v_conv_w"]):
        cwv = np.asarray(inp[wn], f32)
        for ch in range(8):
            for k in range(4):
                qkvcd[(wi * 8 + ch) * 4 + k] = np.diag(cwv[ch * 128:(ch + 1) * 128, k])
    d["qkvcd"] = qkvcd
    fw_s = np.asarray(inp["L1_attn_fir_short_w"], f32).reshape(1024, 7)
    fw_l = np.asarray(inp["L1_attn_fir_long_w"], f32).reshape(1024, 31)
    fircd = np.zeros((304, 128, 128), f32)
    for ch in range(8):
        for k in range(7):
            fircd[ch * 38 + k] = np.diag(fw_s[ch * 128:(ch + 1) * 128, k])
        for k in range(31):
            fircd[ch * 38 + 7 + k] = np.diag(fw_l[ch * 128:(ch + 1) * 128, k])
    d["fircd"] = fircd
    d["oprojT"] = kcmaj(np.asarray(inp["L1_attn_o_proj_w"], f32).T)
    d["onw_rep"] = np.ascontiguousarray(
        np.broadcast_to(np.asarray(inp["L1_attn_o_norm_w"], f32)[None, :].repeat(128, 0)
                        .reshape(128, 256)[:, None, :], (128, 4, 256)).reshape(128, 1024))
    alr = np.zeros((16,), f32)
    alv = np.asarray(inp["L1_attn_alpha"], f32)[:, 0]
    for c4 in range(4):
        for h in range(4):
            alr[c4 * 4 + h] = alv[h]
    d["alpha_rep"] = np.broadcast_to(alr, (128, 16)).copy()
    tril_s = np.tril(np.ones((128, 128), f32), -1)
    d["maskstd"] = np.tile(-tril_s, (1, 4)).astype(f32)
    d["maskor"] = np.tile(-tril_s.T, (1, 4)).astype(f32)
    d["maskal"] = np.tile(np.triu(np.ones((128, 128), f32), 0), (1, 4)).astype(f32)
    ident = np.eye(128)
    d["idrep4"] = np.tile(ident, (1, 4)).astype(ml_dtypes.bfloat16)
    d["id_bf"] = ident.astype(ml_dtypes.bfloat16)
    bselv = np.zeros((4, 512), f32)
    for h in range(4):
        bselv[h, h * 128:(h + 1) * 128] = 1.0
    d["bsel"] = bselv
    d["id_f32"] = ident.astype(f32)
    d["wupN"] = np.ascontiguousarray(np.asarray(inp["L1_ffn_w_up"], f32))
    d["wgateN"] = np.ascontiguousarray(np.asarray(inp["L1_ffn_w_gate"], f32))
    d["wdownN"] = np.ascontiguousarray(np.asarray(inp["L1_ffn_w_down"], f32))
    return d


KMODE = "full"


def kernel(**inputs):
    nc = bass.Bass("TRN2", target_bir_lowering=False, debug=False,
                   detect_race_conditions=False)
    build(nc, mode=KMODE)
    in_maps = [_prep_inputs(inputs, c) for c in range(NCORES)]
    res = run_bass_kernel_spmd(nc, in_maps, core_ids=list(range(NCORES)))
    outs = []
    for c in range(NCORES):
        y = np.asarray(res.results[c]["yout"])
        y = y.transpose(1, 0, 2).reshape(DM, L).T
        outs.append(y)
    return np.stack(outs).astype(np.float32)
